# version 01
# baseline (speedup 1.0000x reference)
import math
import numpy as np
import ml_dtypes
import concourse.bass as bass
import concourse.mybir as mybir
from concourse.bass_utils import run_bass_kernel_spmd

F32 = mybir.dt.float32
BF16 = mybir.dt.bfloat16
AF = mybir.ActivationFunctionType
ALU = mybir.AluOpType
AXX = mybir.AxisListType.X

B = 4
T = 8192
D = 1024
NH = 8
NKV = 4
GR = NH // NKV
E = 4
W = 512
NTB = T // W
NCH = D // 128
LN_EPS = 1e-5
THETA = 10000.0
BF = ml_dtypes.bfloat16

DBG = {}
ROT = [1, 2, 3, 0] + list(range(4, 32))

def r3(dst2d, n_outer, src3d):
    return dst2d.rearrange("p (a b) -> p a b", a=n_outer), src3d



class St:

    def __init__(self, nc, block, sems):
        self.nc = nc
        self.block = block
        self.s = sems
        self.c = {k: 0 for k in sems}
        self.ap = {}
        self.sb = {}

    def inc(self, instr, name, n):
        instr.then_inc(self.s[name], n)
        self.c[name] += n

    def w(self, eng, *names):
        for n in names:
            if self.c[n] > 0:
                eng.wait_ge(self.s[n], self.c[n])

    def wall(self, eng):
        self.w(eng, "sl", "sg", "sp", "sa", "sv", "so", "sq")


def full_barrier(st):
    block = st.block
    for engname in ("sync", "gpsimd", "tensor", "scalar", "vector"):
        def mk(e, st=st):
            st.wall(e)
        getattr(block, engname)(mk)



def _rope_tables():
    j = np.arange(128) % 64
    inv_freq = 1.0 / THETA ** (2 * j / 128)
    ang = np.outer(inv_freq, np.arange(T))
    cosT = np.cos(ang).astype(BF)
    sign = np.where(np.arange(128) < 64, -1.0, 1.0)[:, None]
    sinTs = (np.sin(ang) * sign).astype(BF)
    return cosT, sinTs


def _perm_cols(w):
    d = w.shape[1]
    idx = np.arange(d)
    h = idx % 128
    rho = np.where(h < 64, idx + 64, idx - 64)
    return np.ascontiguousarray(w[:, rho])


def _masks():
    sl_ = np.arange(128)[:, None]
    ql = np.arange(W)[None, :]
    return np.stack([(sl_ <= ql + off).astype(BF) for off in (0, -128, -256, -384)])


def _cols(vec):
    return np.ascontiguousarray(np.asarray(vec, np.float32).reshape(NCH, 128).T)



def ln_block(st, buf, xf, xb, xh):
    nc, block = st.nc, st.block
    sq, rows, rows_bf, mrt, psA, psB, ps_bc = (
        buf["sq"], buf["rows"], buf["rows_bf"], buf["mrt"], buf["psA"], buf["psB"], buf["ps_bc"])

    @block.vector
    def _(v):
        st.w(v, "sg", "sl")
        v.tensor_tensor(sq[:], xb, xb, ALU.mult)
        st.inc(v.tensor_copy(sq[0:1, 0:1], sq[0:1, 0:1]), "sv", 1)

    @block.tensor
    def _(t):
        st.w(t, "sv")
        for c in range(NCH):
            t.matmul(psA[:], st.sb["ones_col"][:], xb[:, c * W:(c + 1) * W],
                     start=(c == 0), stop=(c == NCH - 1))
        for c in range(NCH):
            mm = t.matmul(psB[:], st.sb["ones_col"][:], sq[:, c * W:(c + 1) * W],
                          start=(c == 0), stop=(c == NCH - 1))
        st.inc(mm, "sp", 1)

    negm, var, tmp = rows[0:1, 0:W], rows[0:1, W:2 * W], rows[0:1, 2 * W:3 * W]
    negm_bf, rstd_bf = rows_bf[0:1, 0:W], rows_bf[0:1, W:2 * W]

    @block.vector
    def _(v):
        st.w(v, "sp")
        v.tensor_scalar(negm, psA[0:1, :], -1.0 / D, None, ALU.mult)
        v.tensor_tensor(tmp, negm, negm, ALU.mult)
        st.inc(v.scalar_tensor_tensor(var, psB[0:1, :], 1.0 / D, tmp, ALU.mult, ALU.subtract), "sv", 1)

    @block.scalar
    def _(a):
        st.w(a, "sv")
        a.activation(var, var, AF.Ln, bias=st.sb["eps"][0:1, 0:1])
        st.inc(a.activation(var, var, AF.Exp, scale=-0.5), "sa", 1)

    @block.vector
    def _(v):
        st.w(v, "sa")
        v.tensor_copy(negm_bf, negm)
        st.inc(v.tensor_copy(rstd_bf, var), "sv", 1)

    Mt = mrt[:, 0:W]

    @block.tensor
    def _(t):
        st.w(t, "sv")
        st.inc(t.matmul(ps_bc[:], st.sb["ones_row"][0:1, :], negm_bf, start=True, stop=True), "sp", 1)

    @block.vector
    def _(v):
        st.w(v, "sp")
        st.inc(v.tensor_copy(Mt, ps_bc[:]), "sv", 1)

    @block.tensor
    def _(t):
        st.w(t, "sv")
        st.inc(t.matmul(ps_bc[:], st.sb["ones_row"][0:1, :], rstd_bf, start=True, stop=True), "sp", 1)

    @block.vector
    def _(v):
        st.w(v, "sp")
        Rt = mrt[:, W:2 * W]
        tmp2 = mrt[:, 2 * W:3 * W]
        v.tensor_copy(Rt, ps_bc[:])
        for c in range(NCH):
            v.tensor_tensor(tmp2, xf[:, c * W:(c + 1) * W], Mt, ALU.add)
            v.tensor_tensor(xh[:, c * W:(c + 1) * W], tmp2, Rt, ALU.mult)
        st.inc(v.tensor_copy(xh[0:1, 0:1], xh[0:1, 0:1]), "sv", 1)


def mk_ln_buffers(st, nc, prefix, stack):
    bufs = {
        "sq": stack.enter_context(nc.sbuf_tensor(prefix + "_sq", [128, NCH * W], BF16)),
        "rows": stack.enter_context(nc.sbuf_tensor(prefix + "_rows", [1, 3 * W], F32)),
        "rows_bf": stack.enter_context(nc.sbuf_tensor(prefix + "_rowsbf", [1, 2 * W], BF16)),
        "mrt": stack.enter_context(nc.sbuf_tensor(prefix + "_mrt", [128, 3 * W], F32)),
        "psA": stack.enter_context(nc.psum_tensor(prefix + "_psA", [1, W], F32)),
        "psB": stack.enter_context(nc.psum_tensor(prefix + "_psB", [1, W], F32)),
        "ps_bc": stack.enter_context(nc.psum_tensor(prefix + "_bc", [128, W], F32)),
    }
    sqh = bufs["sq"]
    bufs["sq"] = sqh[:]
    for k in ("rows", "rows_bf", "mrt"):
        bufs[k] = bufs[k][:]
    for k in ("psA", "psB", "ps_bc"):
        bufs[k] = bufs[k][:]
    return bufs



def phase_preamble(st):
    nc, ap, block = st.nc, st.ap, st.block

    for name, shape, dt in (
        ("ones_col", [128, 1], BF16), ("ones_row", [1, 128], BF16),
        ("b1c", [128, NCH * E], F32), ("b2c", [128, NCH * E], F32),
        ("be1c", [128, NCH * E], F32),
        ("ew", [128, NCH], F32), ("eu", [128, NCH], F32),
        ("g1r", [1, E * D], BF16), ("g2r", [1, E * D], BF16),
        ("be2r", [1, E * D], BF16),
        ("mixk", [128, NCH], F32), ("mixv", [128, NCH], F32), ("mixr", [128, NCH], F32),
        ("gb", [E, 1], F32), ("eps", [1, 1], F32),
    ):
        st.sb[name] = nc.alloc_sbuf_tensor("pers_" + name, shape, dt)

    @block.sync
    def _(s):
        for name, src in (("ones_col", "ones_col"), ("ones_row", "ones_row"),
                          ("be1c", "be1_cc"), ("mixk", "mixk_c"), ("mixv", "mixv_c"),
                          ("mixr", "mixr_c"), ("gb", "gb_col"), ("eps", "eps11")):
            st.inc(s.dma_start(out=st.sb[name][:], in_=ap[src]), "sl", 16)

    @block.gpsimd
    def _(g):
        for name, src in (("g1r", "g1_r"), ("g2r", "g2_r"), ("be2r", "be2_r")):
            st.inc(g.dma_start(out=st.sb[name][:], in_=ap[src]), "sg", 16)

    from contextlib import ExitStack
    with ExitStack() as ctx:
        wbuf = ctx.enter_context(nc.sbuf_tensor("pre_w", [128, D], F32))
        obuf = ctx.enter_context(nc.sbuf_tensor("pre_o", [128, D], BF16))
        mcol = ctx.enter_context(nc.sbuf_tensor("pre_m", [128, 1], F32))
        mcol2 = ctx.enter_context(nc.sbuf_tensor("pre_m2", [128, 1], F32))
        ccol = ctx.enter_context(nc.sbuf_tensor("pre_cc", [128, NCH], F32))
        brow = ctx.enter_context(nc.sbuf_tensor("pre_br", [1, D], F32))
        bmv = ctx.enter_context(nc.sbuf_tensor("pre_bm", [1, 1], F32))
        bmb = ctx.enter_context(nc.sbuf_tensor("pre_bmb", [1, 1], BF16))
        pcol = ctx.enter_context(nc.psum_tensor("pre_pc", [128, 1], F32))

        @block.scalar
        def _(a):
            st.w(a, "sl")

        for vname, dest in (("sd_c", "ew"), ("sf_c", "eu")):
            @block.sync
            def _(s, vname=vname):
                st.w(s, "sa")
                st.inc(s.dma_start(out=ccol[:], in_=ap[vname]), "sl", 16)

            @block.scalar
            def _(a, dest=dest):
                st.w(a, "sl")
                st.inc(a.activation(st.sb[dest][:], ccol[:], AF.Exp, scale=1.0 / T), "sa", 1)

        for wi, (wname, sname, bname, cname, dest) in enumerate((
            ("w1", "s_w1c", "b1r", "b1_cc", "b1c"),
            ("w2", "s_w2c", "b2r", "b2_cc", "b2c"),
        )):
            for e in range(E):
                for ci in range(NCH):
                    @block.sync
                    def _(s, e=e, ci=ci, wname=wname):
                        st.w(s, "so")
                        st.inc(s.dma_start(out=wbuf[:], in_=ap[wname][e, ci * 128:(ci + 1) * 128, :]), "sl", 16)

                    @block.vector
                    def _(v):
                        st.w(v, "sl")
                        st.inc(v.reduce_sum(mcol[:], wbuf[:], axis=AXX), "sv", 1)

                    @block.scalar
                    def _(a):
                        st.w(a, "sv")
                        st.inc(a.activation(mcol2[:], mcol[:], AF.Identity, scale=1.0 / D), "sa", 1)

                    @block.vector
                    def _(v):
                        st.w(v, "sa")
                        st.inc(v.tensor_scalar(obuf[:], wbuf[:], mcol2[:, 0:1], None, ALU.subtract), "sv", 1)

                    @block.gpsimd
                    def _(g, e=e, ci=ci, sname=sname):
                        st.w(g, "sv")
                        st.inc(g.dma_start(out=ap[sname][e, ci], in_=obuf[:]), "so", 16)

                @block.sync
                def _(s, e=e, bname=bname, cname=cname):
                    st.w(s, "sv")
                    st.inc(s.dma_start(out=brow[:], in_=ap[bname][e]), "sl", 16)
                    st.inc(s.dma_start(out=ccol[:], in_=ap[cname][e]), "sl", 16)

                @block.vector
                def _(v):
                    st.w(v, "sl")
                    v.reduce_sum(bmv[:], brow[:], axis=AXX)
                    v.tensor_scalar(bmv[:], bmv[:], 1.0 / D, None, ALU.mult)
                    st.inc(v.tensor_copy(bmb[:], bmv[:]), "sv", 1)

                @block.tensor
                def _(t):
                    st.w(t, "sv")
                    st.inc(t.matmul(pcol[:], st.sb["ones_row"][0:1, :], bmb[:], start=True, stop=True), "sp", 1)

                @block.vector
                def _(v, e=e, dest=dest):
                    st.w(v, "sp")
                    st.inc(v.tensor_tensor(st.sb[dest][:, e * NCH:(e + 1) * NCH], ccol[:],
                                           pcol[:, 0:1].broadcast_to([128, NCH]), ALU.subtract), "sv", 1)

    full_barrier(st)


def phase_1a(st):
    nc, ap, block = st.nc, st.ap, st.block
    from contextlib import ExitStack
    with ExitStack() as ctx:
        wq = ctx.enter_context(nc.sbuf_tensor("a_wq", [128, NCH * D], BF16))
        wqp = ctx.enter_context(nc.sbuf_tensor("a_wqp", [128, NCH * D], BF16))
        wk = ctx.enter_context(nc.sbuf_tensor("a_wk", [128, NCH * 512], BF16))
        wkp = ctx.enter_context(nc.sbuf_tensor("a_wkp", [128, NCH * 512], BF16))
        wv = ctx.enter_context(nc.sbuf_tensor("a_wv", [128, NCH * 512], BF16))
        cosb = ctx.enter_context(nc.sbuf_tensor("a_cos", [128, T], BF16))
        sinb = ctx.enter_context(nc.sbuf_tensor("a_sin", [128, T], BF16))
        xf = ctx.enter_context(nc.sbuf_tensor("a_xf", [128, NCH * W], F32))
        xb = ctx.enter_context(nc.sbuf_tensor("a_xb", [128, NCH * W], BF16))
        xh = ctx.enter_context(nc.sbuf_tensor("a_xh", [128, NCH * W], BF16))
        q1 = ctx.enter_context(nc.sbuf_tensor("a_q1", [128, W], F32))
        q2 = ctx.enter_context(nc.sbuf_tensor("a_q2", [128, W], F32))
        qo = ctx.enter_context(nc.sbuf_tensor("a_qo", [128, W], BF16))
        vo = ctx.enter_context(nc.sbuf_tensor("a_vo", [128, 512], BF16))
        lnb = mk_ln_buffers(st, nc, "a", ctx)
        ps1 = ctx.enter_context(nc.psum_tensor("a_p1", [128, W], F32))
        ps2 = ctx.enter_context(nc.psum_tensor("a_p2", [128, W], F32))
        psv = ctx.enter_context(nc.psum_tensor("a_pv", [128, 512], F32))

        @block.gpsimd
        def _(g):
            st.inc(g.dma_start(out=wq[:].rearrange("p (c d) -> p c d", c=NCH), in_=ap["wq"].rearrange("(c p) d -> p c d", p=128)), "sg", 16)
            st.inc(g.dma_start(out=wqp[:].rearrange("p (c d) -> p c d", c=NCH), in_=ap["wqp"].rearrange("(c p) d -> p c d", p=128)), "sg", 16)
            st.inc(g.dma_start(out=wk[:].rearrange("p (c d) -> p c d", c=NCH), in_=ap["wk"].rearrange("(c p) d -> p c d", p=128)), "sg", 16)
            st.inc(g.dma_start(out=wkp[:].rearrange("p (c d) -> p c d", c=NCH), in_=ap["wkp"].rearrange("(c p) d -> p c d", p=128)), "sg", 16)
            st.inc(g.dma_start(out=wv[:].rearrange("p (c d) -> p c d", c=NCH), in_=ap["wv"].rearrange("(c p) d -> p c d", p=128)), "sg", 16)

        @block.sync
        def _(s):
            st.inc(s.dma_start(out=cosb[:], in_=ap["cosT"]), "sl", 16)
            st.inc(s.dma_start(out=sinb[:], in_=ap["sinTs"]), "sl", 16)

        for tb in range(NTB):
            @block.sync
            def _(s, tb=tb):
                st.w(s, "sv", "so")
                st.inc(s.dma_start(out=xf[:].rearrange("p (c t) -> p c t", c=NCH), in_=ap["x0"][:, :, tb * W:(tb + 1) * W]
                                   .rearrange("c p t -> p c t")), "sl", 16)

            @block.gpsimd
            def _(g, tb=tb):
                st.w(g, "sv")
                st.inc(g.dma_start(out=xb[:].rearrange("p (c t) -> p c t", c=NCH), in_=ap["x0"][:, :, tb * W:(tb + 1) * W]
                                   .rearrange("c p t -> p c t")), "sg", 16)

            ln_block(st, lnb, xf[:], xb[:], xh[:])

            for kind, nchunks, w_sb, wp_sb, outname, wstride in (
                ("q", NCH, wq, wqp, "s_qT", D),
                ("k", 4, wk, wkp, "s_kT", 512),
            ):
                for dj in range(nchunks):
                    @block.tensor
                    def _(t, dj=dj, w_sb=w_sb, wp_sb=wp_sb, wstride=wstride):
                        st.w(t, "sv")
                        for c in range(NCH):
                            t.matmul(ps1[:], w_sb[:, c * wstride + dj * 128: c * wstride + dj * 128 + 128],
                                     xh[:, c * W:(c + 1) * W], start=(c == 0), stop=(c == NCH - 1))
                        for c in range(NCH):
                            mm = t.matmul(ps2[:], wp_sb[:, c * wstride + dj * 128: c * wstride + dj * 128 + 128],
                                          xh[:, c * W:(c + 1) * W], start=(c == 0), stop=(c == NCH - 1))
                        st.inc(mm, "sp", 1)

                    @block.vector
                    def _(v, dj=dj, tb=tb):
                        st.w(v, "sp", "so")
                        csl = cosb[:, tb * W:(tb + 1) * W]
                        ssl = sinb[:, tb * W:(tb + 1) * W]
                        v.tensor_tensor(q1[:], ps1[:], csl, ALU.mult)
                        v.tensor_tensor(q2[:], ps2[:], ssl, ALU.mult)
                        st.inc(v.tensor_tensor(qo[:], q1[:], q2[:], ALU.add), "sv", 1)

                    @block.gpsimd
                    def _(g, dj=dj, tb=tb, outname=outname):
                        st.w(g, "sv")
                        st.inc(g.dma_start(out=ap[outname][dj, :, tb * W:(tb + 1) * W], in_=qo[:]), "so", 16)

            for tt in range(4):
                @block.tensor
                def _(t, tt=tt):
                    st.w(t, "sv")
                    for c in range(NCH):
                        mm = t.matmul(psv[:], xh[:, c * W + tt * 128: c * W + tt * 128 + 128],
                                      wv[:, c * 512:(c + 1) * 512], start=(c == 0), stop=(c == NCH - 1))
                    st.inc(mm, "sp", 1)

                @block.vector
                def _(v, tt=tt):
                    st.w(v, "sp", "so")
                    st.inc(v.tensor_copy(vo[:], psv[:]), "sv", 1)

                @block.gpsimd
                def _(g, tt=tt, tb=tb):
                    st.w(g, "sv")
                    st.inc(g.dma_start(
                        out=ap["s_v"].rearrange("g t d -> t g d")[tb * W + tt * 128: tb * W + tt * 128 + 128, :, :],
                        in_=vo[:].rearrange("p (g d) -> p g d", g=NKV)), "so", 16)
    full_barrier(st)


def phase_1b(st):
    nc, ap, block = st.nc, st.ap, st.block
    from contextlib import ExitStack
    with ExitStack() as ctx:
        kT = ctx.enter_context(nc.sbuf_tensor("b_kT", [128, T], BF16))
        vg = ctx.enter_context(nc.sbuf_tensor("b_v", [128, T], BF16))
        qsb = ctx.enter_context(nc.sbuf_tensor("b_q", [128, W], BF16))
        pT = ctx.enter_context(nc.sbuf_tensor("b_p", [128, 2 * 1024], BF16))
        rs = ctx.enter_context(nc.sbuf_tensor("b_rs", [1, T], F32))
        rcp = ctx.enter_context(nc.sbuf_tensor("b_rcp", [1, T], F32))
        msk = ctx.enter_context(nc.sbuf_tensor("b_m", [128, 4 * W], BF16))
        oev = ctx.enter_context(nc.sbuf_tensor("b_oe", [128, W], BF16))
        s0 = ctx.enter_context(nc.psum_tensor("b_s0", [128, W], F32))
        s1 = ctx.enter_context(nc.psum_tensor("b_s1", [128, W], F32))
        s2 = ctx.enter_context(nc.psum_tensor("b_s2", [128, W], F32))
        s3 = ctx.enter_context(nc.psum_tensor("b_s3", [128, W], F32))
        ob = ctx.enter_context(nc.psum_tensor("b_o", [128, W], F32))
        rb = ctx.enter_context(nc.psum_tensor("b_r", [1, W], F32))
        pair_banks = [(s0, s1), (s2, s3)]

        @block.sync
        def _(s):
            st.inc(s.dma_start(out=msk[:].rearrange("p (m q) -> p m q", m=4), in_=ap["masks"].rearrange("m p q -> p m q")), "sl", 16)

        for g in range(NKV):
            @block.sync
            def _(s, g=g):
                st.w(s, "sp", "sa", "sv")
                st.inc(s.dma_start(out=kT[:], in_=ap["s_kT"][g]), "sl", 16)
                st.inc(s.dma_start(out=vg[:].rearrange("p (j d) -> p j d", j=T // 128), in_=ap["s_v"][g].rearrange("(j p) d -> p j d", p=128)), "sl", 16)

            for hh in range(GR):
                h = g * GR + hh
                for qb in range(NTB):
                    nch_s = 4 * qb + 4

                    @block.sync
                    def _(s, h=h, qb=qb):
                        st.w(s, "sp")
                        st.inc(s.dma_start(out=qsb[:], in_=ap["s_qT"][h, :, qb * W:(qb + 1) * W]), "sl", 16)

                    npair = (nch_s + 1) // 2
                    for pi in range(npair):
                        c0 = 2 * pi
                        c1 = min(2 * pi + 1, nch_s - 1)
                        sA, sB = pair_banks[pi % 2]
                        pslice = pT[:, (pi % 2) * 1024:(pi % 2) * 1024 + 1024]

                        @block.tensor
                        def _(t, pi=pi, c0=c0, c1=c1, sA=sA, sB=sB):
                            st.w(t, "sl", "sa", "sv")
                            t.matmul(sA[:], kT[:, c0 * 128:(c0 + 1) * 128], qsb[:], start=True, stop=True)
                            mm = t.matmul(sB[:], kT[:, c1 * 128:(c1 + 1) * 128], qsb[:], start=True, stop=True)
                            st.inc(mm, "sp", 1)

                        @block.scalar
                        def _(a, sA=sA, sB=sB, pslice=pslice, c0=c0, c1=c1):
                            st.w(a, "sp")
                            ssc = 1.0 / math.sqrt(128.0)
                            if c1 > c0:
                                a.activation(pslice[:, 0:W], sA[:], AF.Exp, scale=ssc)
                                st.inc(a.activation(pslice[:, W:2 * W], sB[:], AF.Exp, scale=ssc), "sa", 1)
                            else:
                                st.inc(a.activation(pslice[:, 0:W], sA[:], AF.Exp, scale=ssc), "sa", 1)

                        masked = sorted(set(c for c in (c0, c1) if c >= 4 * qb))
                        if masked:
                            @block.vector
                            def _(v, c0=c0, qb=qb, pslice=pslice, masked=tuple(masked)):
                                st.w(v, "sa")
                                last = None
                                for c in masked:
                                    oi = c - 4 * qb
                                    psl = pslice[:, (c - c0) * W:(c - c0) * W + W]
                                    last = v.tensor_tensor(psl, psl, msk[:, oi * W:(oi + 1) * W], ALU.mult)
                                st.inc(last, "sv", 1)

                        @block.tensor
                        def _(t, c0=c0, c1=c1, pslice=pslice, nch_s=nch_s):
                            st.w(t, "sa", "sv")
                            cs = [c0] if c1 == c0 else [c0, c1]
                            for c in cs:
                                psl = pslice[:, (c - c0) * W:(c - c0) * W + W]
                                t.matmul(rb[:], st.sb["ones_col"][:], psl,
                                         start=(c == 0), stop=(c == nch_s - 1))
                                mm = t.matmul(ob[:], vg[:, c * 128:(c + 1) * 128], psl,
                                              start=(c == 0), stop=(c == nch_s - 1))
                            st.inc(mm, "sp", 1)

                    @block.vector
                    def _(v, h=h, qb=qb):
                        st.w(v, "sp", "so")
                        v.tensor_copy(rs[0:1, qb * W:(qb + 1) * W], rb[:])
                        st.inc(v.tensor_copy(oev[:], ob[:]), "sv", 1)

                    @block.gpsimd
                    def _(gq, h=h, qb=qb):
                        st.w(gq, "sv")
                        st.inc(gq.dma_start(out=ap["s_oT"][h, :, qb * W:(qb + 1) * W], in_=oev[:]), "so", 16)

                @block.scalar
                def _(a, h=h):
                    st.w(a, "sv", "so")
                    a.activation(rcp[:], rs[:], AF.Ln)
                    st.inc(a.activation(rcp[:], rcp[:], AF.Exp, scale=-1.0), "sa", 1)

                @block.gpsimd
                def _(gq, h=h):
                    st.w(gq, "sa")
                    st.inc(gq.dma_start(out=ap[f"s_rcp{h}"], in_=rcp[:]), "so", 16)
    full_barrier(st)


def phase_1c(st):
    nc, ap, block = st.nc, st.ap, st.block
    from contextlib import ExitStack
    with ExitStack() as ctx:
        wo = ctx.enter_context(nc.sbuf_tensor("c_wo", [128, NCH * D], BF16))
        oT = ctx.enter_context(nc.sbuf_tensor("c_oT", [128, NCH * W], BF16))
        on = ctx.enter_context(nc.sbuf_tensor("c_on", [128, NCH * W], BF16))
        x0b = ctx.enter_context(nc.sbuf_tensor("c_x0", [128, NCH * W], F32))
        x1b = ctx.enter_context(nc.sbuf_tensor("c_x1", [128, NCH * W], F32))
        rcr = ctx.enter_context(nc.sbuf_tensor("c_rc", [1, NH * W], F32))
        rcb = ctx.enter_context(nc.sbuf_tensor("c_rcb", [1, NH * W], BF16))
        bc = ctx.enter_context(nc.psum_tensor("c_bc", [128, W], F32))
        pa = ctx.enter_context(nc.psum_tensor("c_pa", [128, W], F32))

        @block.gpsimd
        def _(g):
            st.inc(g.dma_start(out=wo[:].rearrange("p (c d) -> p c d", c=NCH), in_=ap["woa"].rearrange("(c p) d -> p c d", p=128)), "sg", 16)

        for tb in range(NTB):
            @block.sync
            def _(s, tb=tb):
                st.w(s, "sv", "so")
                st.inc(s.dma_start(out=oT[:].rearrange("p (h t) -> p h t", h=NH),
                                   in_=ap["s_oT"][:, :, tb * W:(tb + 1) * W]
                                   .rearrange("h p t -> p h t")), "sl", 16)
                st.inc(s.dma_start(out=x0b[:].rearrange("p (c t) -> p c t", c=NCH), in_=ap["x0"][:, :, tb * W:(tb + 1) * W]
                                   .rearrange("c p t -> p c t")), "sl", 16)
                for h in range(NH):
                    st.inc(s.dma_start(out=rcr[0:1, h * W:(h + 1) * W],
                                       in_=ap[f"s_rcp{h}"][0:1, tb * W:(tb + 1) * W]), "sl", 16)

            @block.vector
            def _(v, tb=tb):
                st.w(v, "sl", "sg", "sq")
                st.inc(v.tensor_copy(rcb[:], rcr[:]), "sv", 1)

            for h in range(NH):
                @block.tensor
                def _(t, h=h):
                    st.w(t, "sv")
                    st.inc(t.matmul(bc[:], st.sb["ones_row"][0:1, :], rcb[0:1, h * W:(h + 1) * W],
                                    start=True, stop=True), "sp", 1)

                @block.vector
                def _(v, h=h):
                    st.w(v, "sp")
                    st.inc(v.tensor_tensor(on[:, h * W:(h + 1) * W], oT[:, h * W:(h + 1) * W],
                                           bc[:], ALU.mult), "sv", 1)

            for c in range(NCH):
                @block.tensor
                def _(t, c=c):
                    st.w(t, "sv")
                    for hc in range(NCH):
                        mm = t.matmul(pa[:], wo[:, hc * D + c * 128: hc * D + c * 128 + 128],
                                      on[:, hc * W:(hc + 1) * W], start=(hc == 0), stop=(hc == NCH - 1))
                    st.inc(mm, "sp", 1)

                @block.vector
                def _(v, c=c):
                    st.w(v, "sp")
                    st.inc(v.tensor_tensor(x1b[:, c * W:(c + 1) * W], pa[:],
                                           x0b[:, c * W:(c + 1) * W], ALU.add), "sv", 1)

            @block.gpsimd
            def _(g2, tb=tb):
                st.w(g2, "sv")
                st.inc(g2.dma_start(out=ap["s_x1"][:, :, tb * W:(tb + 1) * W].rearrange("c p t -> p c t"), in_=x1b[:].rearrange("p (c t) -> p c t", c=NCH)), "so", 16)
    full_barrier(st)


def phase_ffn(st, src_name, dst_name):
    nc, ap, block = st.nc, st.ap, st.block
    pfx = "f2_" if dst_name == "y" else "f_"
    if dst_name == "s_x2" and DBG.get("ffn") and "dbgr_t" not in st.sb:
        st.sb["dbgr_t"] = nc.alloc_sbuf_tensor("dbgr_t", [32, 9 * W], BF16)
    from contextlib import ExitStack
    with ExitStack() as ctx:
        gwsb = ctx.enter_context(nc.sbuf_tensor(pfx + "gw", [128, NCH * E], BF16))
        w1e = ctx.enter_context(nc.sbuf_tensor(pfx + "w1e", [128, NCH * D], BF16))
        w2e = ctx.enter_context(nc.sbuf_tensor(pfx + "w2e", [128, NCH * D], BF16))
        xf = ctx.enter_context(nc.sbuf_tensor(pfx + "xf", [128, NCH * W], F32))
        xb = ctx.enter_context(nc.sbuf_tensor(pfx + "xb", [128, NCH * W], BF16))
        xh = ctx.enter_context(nc.sbuf_tensor(pfx + "xh", [128, NCH * W], BF16))
        hsb = ctx.enter_context(nc.sbuf_tensor(pfx + "h", [128, NCH * W], BF16))
        hn = ctx.enter_context(nc.sbuf_tensor(pfx + "hn", [128, NCH * W], BF16))
        zsb = ctx.enter_context(nc.sbuf_tensor(pfx + "z", [128, NCH * W], BF16))
        gs = ctx.enter_context(nc.sbuf_tensor(pfx + "gs", [32, 8 * W], F32))
        wr = ctx.enter_context(nc.sbuf_tensor(pfx + "wr", [32, 4 * W], F32))
        wrb = ctx.enter_context(nc.sbuf_tensor(pfx + "wrb", [1, 4 * W], BF16))
        er = ctx.enter_context(nc.sbuf_tensor(pfx + "er", [1, 6 * W], F32))
        erb = ctx.enter_context(nc.sbuf_tensor(pfx + "erb", [1, 4 * W], BF16))
        acc = ctx.enter_context(nc.sbuf_tensor(pfx + "acc", [128, W], F32))
        xout = ctx.enter_context(nc.sbuf_tensor(pfx + "xo", [128, NCH * W], F32))
        lnb = mk_ln_buffers(st, nc, pfx, ctx)
        ps1 = ctx.enter_context(nc.psum_tensor(pfx + "p1", [128, W], F32))
        ps2 = ctx.enter_context(nc.psum_tensor(pfx + "p2", [128, W], F32))
        psg = ctx.enter_context(nc.psum_tensor(pfx + "pg", [4, W], F32))
        psbe = ctx.enter_context(nc.psum_tensor(pfx + "pbe", [128, W], F32))

        @block.gpsimd
        def _(g):
            st.inc(g.dma_start(out=gwsb[:].rearrange("p (c e) -> p c e", c=NCH), in_=ap["gw"].rearrange("(c p) e -> p c e", p=128)), "sg", 16)

        @block.vector
        def _(v):
            v.memset(gs[:], 0.0)
            st.inc(v.memset(wr[:], 0.0), "sv", 1)

        for tb in range(NTB):
            @block.sync
            def _(s, tb=tb):
                st.w(s, "sv", "so", "sa")
                st.inc(s.dma_start(out=xf[:].rearrange("p (c t) -> p c t", c=NCH), in_=ap[src_name][:, :, tb * W:(tb + 1) * W]
                                   .rearrange("c p t -> p c t")), "sl", 16)

            @block.gpsimd
            def _(g, tb=tb):
                st.w(g, "sv")
                st.inc(g.dma_start(out=xb[:].rearrange("p (c t) -> p c t", c=NCH), in_=ap[src_name][:, :, tb * W:(tb + 1) * W]
                                   .rearrange("c p t -> p c t")), "sg", 16)

            ln_block(st, lnb, xf[:], xb[:], xh[:])
            @block.tensor
            def _(t):
                st.w(t, "sv")
                for c in range(NCH):
                    mm = t.matmul(psg[:], gwsb[:, c * E:(c + 1) * E], xh[:, c * W:(c + 1) * W],
                                  start=(c == 0), stop=(c == NCH - 1))
                st.inc(mm, "sp", 1)

            sgr = gs[:, 0:W]
            t1 = gs[:, W:2 * W]
            t2 = gs[:, 2 * W:3 * W]
            t3 = gs[:, 3 * W:4 * W]
            m1 = gs[:, 4 * W:5 * W]
            m2 = gs[:, 5 * W:6 * W]
            ind = gs[:, 6 * W:7 * W]
            wun = gs[:, 7 * W:8 * W]

            @block.scalar
            def _(a):
                st.w(a, "sp")
                st.inc(a.activation(sgr[0:4, :] if hasattr(sgr, "shape") else sgr, psg[:], AF.Identity, bias=st.sb["gb"][:, 0:1]), "sa", 1)

            @block.vector
            def _(v):
                st.w(v, "sa")
                v.stream_shuffle(t1, sgr, ROT)
                v.stream_shuffle(t2, t1, ROT)
                v.stream_shuffle(t3, t2, ROT)
                v.tensor_tensor(m1, sgr, t1, ALU.max)
                v.tensor_tensor(m2, t2, t3, ALU.max)
                v.tensor_tensor(wun, sgr, t1, ALU.min)
                v.tensor_tensor(ind, t2, t3, ALU.min)
                v.tensor_tensor(t1, m1, m2, ALU.min)
                v.tensor_tensor(m1, m1, m2, ALU.max)
                v.tensor_tensor(t2, wun, ind, ALU.max)
                v.tensor_tensor(m2, t1, t2, ALU.max)
                v.tensor_tensor(t1, sgr, m1, ALU.subtract)
                v.tensor_tensor(t2, sgr, m2, ALU.subtract)
                v.tensor_scalar(t2, t2, 1e12, 1.0, ALU.mult, ALU.add)
                st.inc(v.tensor_scalar(ind, t2, 1.0, 0.0, ALU.min, ALU.max), "sv", 1)

            @block.scalar
            def _(a):
                st.w(a, "sv")
                st.inc(a.activation(t1[0:4, :] if True else t1, t1[0:4, :] if True else t1, AF.Exp), "sa", 1)

            @block.vector
            def _(v):
                st.w(v, "sa")
                v.tensor_tensor(wun, t1, ind, ALU.mult)
                v.stream_shuffle(t2, wun, ROT)
                v.tensor_tensor(t2, wun, t2, ALU.add)
                v.stream_shuffle(t3, t2, ROT)
                v.stream_shuffle(t3, t3, ROT)
                st.inc(v.tensor_tensor(t2, t2, t3, ALU.add), "sv", 1)

            @block.scalar
            def _(a):
                st.w(a, "sv")
                a.activation(t3[0:4, :], t2[0:4, :], AF.Ln)
                st.inc(a.activation(t3[0:4, :], t3[0:4, :], AF.Exp, scale=-1.0), "sa", 1)

            @block.vector
            def _(v):
                st.w(v, "sa")
                v.tensor_tensor(sgr, wun, t3, ALU.mult)
                v.stream_shuffle(wr[:, W:2 * W], sgr, ROT)
                v.stream_shuffle(wr[:, 2 * W:3 * W], wr[:, W:2 * W], ROT)
                v.stream_shuffle(wr[:, 3 * W:4 * W], wr[:, 2 * W:3 * W], ROT)
                v.tensor_copy(wrb[0:1, 0:W], sgr[0:1, :])
                v.tensor_copy(wrb[0:1, W:2 * W], wr[0:1, W:2 * W])
                v.tensor_copy(wrb[0:1, 2 * W:3 * W], wr[0:1, 2 * W:3 * W])
                st.inc(v.tensor_copy(wrb[0:1, 3 * W:4 * W], wr[0:1, 3 * W:4 * W]), "sv", 1)

            for c in range(NCH):
                @block.tensor
                def _(t, c=c):
                    st.w(t, "sv")
                    for e in range(E):
                        mm = t.matmul(psbe[:], st.sb["be2r"][0:1, e * D + c * 128: e * D + c * 128 + 128],
                                      wrb[0:1, e * W:(e + 1) * W], start=(e == 0), stop=(e == E - 1))
                    st.inc(mm, "sp", 1)

                @block.vector
                def _(v, c=c):
                    st.w(v, "sp")
                    v.tensor_tensor(acc[:], psbe[:], xh[:, c * W:(c + 1) * W], ALU.add)
                    st.inc(v.tensor_tensor(xout[:, c * W:(c + 1) * W], acc[:],
                                           xf[:, c * W:(c + 1) * W], ALU.add), "sv", 1)

            if dst_name == "s_x2" and tb == 0 and DBG.get("ffn"):
                @block.vector
                def _(v):
                    st.w(v, "sv", "sa")
                    dbg = st.sb["dbgr_t"][:]
                    v.tensor_copy(dbg[:, 0 * W:1 * W], sgr)
                    v.tensor_copy(dbg[:, 1 * W:2 * W], m1)
                    v.tensor_copy(dbg[:, 2 * W:3 * W], m2)
                    v.tensor_copy(dbg[:, 3 * W:4 * W], ind)
                    v.tensor_copy(dbg[:, 4 * W:5 * W], wun)
                    st.inc(v.tensor_copy(dbg[0:1, 0:1], dbg[0:1, 0:1]), "sv", 1)

            for e in range(E):
                @block.sync
                def _(s, e=e):
                    st.w(s, "sp", "sa")
                    st.inc(s.dma_start(out=w1e[:].rearrange("p (c d) -> p c d", c=NCH), in_=ap["s_w1c"][e].rearrange("c p d -> p c d")), "sl", 16)
                    st.inc(s.dma_start(out=w2e[:].rearrange("p (c d) -> p c d", c=NCH), in_=ap["s_w2c"][e].rearrange("c p d -> p c d")), "sl", 16)

                for dj in range(NCH):
                    pse = ps1 if dj % 2 == 0 else ps2

                    @block.tensor
                    def _(t, e=e, dj=dj, pse=pse):
                        st.w(t, "sl", "sa")
                        for c in range(NCH):
                            mm = t.matmul(pse[:], w1e[:, c * D + dj * 128: c * D + dj * 128 + 128],
                                          xh[:, c * W:(c + 1) * W], start=(c == 0), stop=(c == NCH - 1))
                        st.inc(mm, "sp", 1)

                    @block.scalar
                    def _(a, e=e, dj=dj, pse=pse):
                        st.w(a, "sp")
                        st.inc(a.activation(hsb[:, dj * W:(dj + 1) * W], pse[:], AF.Identity,
                                            bias=st.sb["b1c"][:, e * NCH + dj: e * NCH + dj + 1]), "sa", 1)

                @block.vector
                def _(v, e=e):
                    st.w(v, "sa")
                    st.inc(v.tensor_tensor(lnb["sq"], hsb[:], hsb[:], ALU.mult), "sv", 1)

                @block.tensor
                def _(t, e=e):
                    st.w(t, "sv")
                    for c in range(NCH):
                        mm = t.matmul(lnb["psA"], st.sb["ones_col"][:], lnb["sq"][:, c * W:(c + 1) * W],
                                      start=(c == 0), stop=(c == NCH - 1))
                    st.inc(mm, "sp", 1)

                rst1 = er[0:1, 0:W]

                @block.vector
                def _(v):
                    st.w(v, "sp")
                    st.inc(v.tensor_scalar(rst1, lnb["psA"][0:1, :], 1.0 / D, None, ALU.mult), "sv", 1)

                @block.scalar
                def _(a):
                    st.w(a, "sv")
                    a.activation(rst1, rst1, AF.Ln, bias=st.sb["eps"][0:1, 0:1])
                    st.inc(a.activation(rst1, rst1, AF.Exp, scale=-0.5), "sa", 1)

                @block.vector
                def _(v):
                    st.w(v, "sa")
                    st.inc(v.tensor_copy(erb[0:1, 0:W], rst1), "sv", 1)

                for dj in range(NCH):
                    @block.tensor
                    def _(t, e=e, dj=dj):
                        st.w(t, "sv")
                        st.inc(t.matmul(lnb["ps_bc"], st.sb["g1r"][0:1, e * D + dj * 128: e * D + dj * 128 + 128],
                                        erb[0:1, 0:W], start=True, stop=True), "sp", 1)

                    @block.vector
                    def _(v, e=e, dj=dj):
                        st.w(v, "sp")
                        v.tensor_tensor(hn[:, dj * W:(dj + 1) * W], hsb[:, dj * W:(dj + 1) * W],
                                        lnb["ps_bc"], ALU.mult)
                        st.inc(v.tensor_scalar(hn[:, dj * W:(dj + 1) * W], hn[:, dj * W:(dj + 1) * W],
                                               st.sb["be1c"][:, e * NCH + dj: e * NCH + dj + 1], 0.0,
                                               ALU.add, ALU.max), "sv", 1)

                for dj in range(NCH):
                    pse = ps1 if dj % 2 == 0 else ps2

                    @block.tensor
                    def _(t, e=e, dj=dj, pse=pse):
                        st.w(t, "sv", "sa")
                        for c in range(NCH):
                            mm = t.matmul(pse[:], w2e[:, c * D + dj * 128: c * D + dj * 128 + 128],
                                          hn[:, c * W:(c + 1) * W], start=(c == 0), stop=(c == NCH - 1))
                        st.inc(mm, "sp", 1)

                    @block.scalar
                    def _(a, e=e, dj=dj, pse=pse):
                        st.w(a, "sp")
                        st.inc(a.activation(zsb[:, dj * W:(dj + 1) * W],
                                            pse[:], AF.Identity,
                                            bias=st.sb["b2c"][:, e * NCH + dj: e * NCH + dj + 1]), "sa", 1)

                @block.vector
                def _(v, e=e):
                    st.w(v, "sa")
                    st.inc(v.tensor_tensor(lnb["sq"], zsb[:],
                                           zsb[:], ALU.mult), "sv", 1)

                @block.tensor
                def _(t, e=e):
                    st.w(t, "sv")
                    for c in range(NCH):
                        mm = t.matmul(lnb["psB"], st.sb["ones_col"][:], lnb["sq"][:, c * W:(c + 1) * W],
                                      start=(c == 0), stop=(c == NCH - 1))
                    st.inc(mm, "sp", 1)

                rst2 = er[0:1, (1 + e) * W:(2 + e) * W]

                @block.vector
                def _(v, rst2=rst2, e=e, tb=tb):
                    st.w(v, "sp")
                    if dst_name == "s_x2" and tb == 0 and e == 0 and DBG.get("ffn"):
                        v.tensor_copy(st.sb["dbgr_t"][:][0:1, 4 * W:5 * W].bitcast(F32)[0:1, 0:W // 2] if False else st.sb["dbgr_t"][:][0:1, 4 * W:5 * W], lnb["psB"][0:1, :])
                    st.inc(v.tensor_scalar(rst2, lnb["psB"][0:1, :], 1.0 / D, None, ALU.mult), "sv", 1)

                @block.scalar
                def _(a, rst2=rst2):
                    st.w(a, "sv")
                    a.activation(rst2, rst2, AF.Ln, bias=st.sb["eps"][0:1, 0:1])
                    st.inc(a.activation(rst2, rst2, AF.Exp, scale=-0.5), "sa", 1)

                @block.vector
                def _(v, e=e, rst2=rst2):
                    st.w(v, "sa")
                    wrow = sgr[0:1, :] if e == 0 else wr[0:1, e * W:(e + 1) * W]
                    v.tensor_tensor(er[0:1, 5 * W:6 * W], wrow, rst2, ALU.mult)
                    st.inc(v.tensor_copy(erb[0:1, 0:W], er[0:1, 5 * W:6 * W]), "sv", 1)

                for c in range(NCH):
                    @block.tensor
                    def _(t, e=e, c=c):
                        st.w(t, "sv")
                        st.inc(t.matmul(lnb["ps_bc"], st.sb["g2r"][0:1, e * D + c * 128: e * D + c * 128 + 128],
                                        erb[0:1, 0:W], start=True, stop=True), "sp", 1)

                    @block.vector
                    def _(v, c=c):
                        st.w(v, "sp")
                        v.tensor_tensor(acc[:], zsb[:, c * W:(c + 1) * W], lnb["ps_bc"], ALU.mult)
                        st.inc(v.tensor_tensor(xout[:, c * W:(c + 1) * W], xout[:, c * W:(c + 1) * W],
                                               acc[:], ALU.add), "sv", 1)


            @block.gpsimd
            def _(g2, tb=tb):
                st.w(g2, "sv")
                st.inc(g2.dma_start(out=ap[dst_name][:, :, tb * W:(tb + 1) * W].rearrange("c p t -> p c t"), in_=xout[:].rearrange("p (c t) -> p c t", c=NCH)), "so", 16)

            if dst_name == "s_x2" and tb == 0 and DBG.get("ffn"):
                @block.vector
                def _(v):
                    st.w(v, "sv")
                    dbg = st.sb["dbgr_t"][:]
                    v.tensor_copy(dbg[0:1, 5 * W:6 * W], er[0:1, 1 * W:2 * W])
                    v.tensor_copy(dbg[0:1, 6 * W:7 * W], er[0:1, 2 * W:3 * W])
                    v.tensor_copy(dbg[0:1, 7 * W:8 * W], er[0:1, 3 * W:4 * W])
                    st.inc(v.tensor_copy(dbg[0:1, 8 * W:9 * W], er[0:1, 4 * W:5 * W]), "sv", 1)

                @block.gpsimd
                def _(g3):
                    st.w(g3, "sv")
                    st.inc(g3.dma_start(out=ap["s_dbgR"][:, 0:9 * W], in_=st.sb["dbgr_t"][:]), "so", 16)
                    dw = DBG.get("dump_what", "xout")
                    buf = {"xout": xout, "z0": zsb, "h0": hsb, "hn0": hn, "xh0": xh, "sq0": lnb and None}.get(dw)
                    if dw == "z0":
                        st.inc(g3.dma_start(out=ap["s_dbgA"][:, 0:NCH * W], in_=zsb[:]), "so", 16)
                    elif dw == "h0":
                        st.inc(g3.dma_start(out=ap["s_dbgA"][:, 0:NCH * W], in_=hsb[:]), "so", 16)
                    elif dw == "hn0":
                        st.inc(g3.dma_start(out=ap["s_dbgA"][:, 0:NCH * W], in_=hn[:]), "so", 16)
                    elif dw == "xh0":
                        st.inc(g3.dma_start(out=ap["s_dbgA"][:, 0:NCH * W], in_=xh[:]), "so", 16)
                    else:
                        st.inc(g3.dma_start(out=ap["s_dbgA"][:, 0:NCH * W], in_=xout[:]), "so", 16)
    full_barrier(st)


def phase_3(st):
    nc, ap, block = st.nc, st.ap, st.block
    from contextlib import ExitStack

    with ExitStack() as ctx:
        xf = ctx.enter_context(nc.sbuf_tensor("d_xf", [128, NCH * W], F32))
        xb = ctx.enter_context(nc.sbuf_tensor("d_xb", [128, NCH * W], BF16))
        xh = ctx.enter_context(nc.sbuf_tensor("d_xh", [128, NCH * W], BF16))
        lnb = mk_ln_buffers(st, nc, "d", ctx)
        for tb in range(NTB):
            @block.sync
            def _(s, tb=tb):
                st.w(s, "sv", "so")
                st.inc(s.dma_start(out=xf[:].rearrange("p (c t) -> p c t", c=NCH), in_=ap["s_x2"][:, :, tb * W:(tb + 1) * W]
                                   .rearrange("c p t -> p c t")), "sl", 16)

            @block.gpsimd
            def _(g, tb=tb):
                st.w(g, "sv")
                st.inc(g.dma_start(out=xb[:].rearrange("p (c t) -> p c t", c=NCH), in_=ap["s_x2"][:, :, tb * W:(tb + 1) * W]
                                   .rearrange("c p t -> p c t")), "sg", 16)

            ln_block(st, lnb, xf[:], xb[:], xh[:])

            @block.gpsimd
            def _(g, tb=tb):
                st.w(g, "sv")
                st.inc(g.dma_start(out=ap["s_xh3"][:, :, tb * W:(tb + 1) * W].rearrange("c p t -> p c t"), in_=xh[:].rearrange("p (c t) -> p c t", c=NCH)), "so", 16)
    full_barrier(st)

    with ExitStack() as ctx:
        wsb = ctx.enter_context(nc.sbuf_tensor("e_w", [128, NCH * D], BF16))
        xhs = ctx.enter_context(nc.sbuf_tensor("e_xh", [128, NCH * (W + 2)], BF16))
        mix = ctx.enter_context(nc.sbuf_tensor("e_mx", [128, 4 * W], BF16))
        evac = ctx.enter_context(nc.sbuf_tensor("e_ev", [128, W], BF16))
        ps1 = ctx.enter_context(nc.psum_tensor("e_p1", [128, W], F32))
        ps2 = ctx.enter_context(nc.psum_tensor("e_p2", [128, W], F32))

        def mix_pass(w_name, mixc_name, out_name, act):
            @block.gpsimd
            def _(g):
                st.w(g, "sp", "sa", "so")
                st.inc(g.dma_start(out=wsb[:].rearrange("p (c d) -> p c d", c=NCH), in_=ap[w_name].rearrange("(c p) d -> p c d", p=128)), "sg", 16)

            for tb in range(NTB):
                lo = max(0, tb * W - 1)
                hi = min(T, (tb + 1) * W + 1)
                npad_l = 1 if tb == 0 else 0
                xv = xhs[:].rearrange("p (c w) -> p c w", c=NCH)

                @block.sync
                def _(s, tb=tb, lo=lo, hi=hi, npad_l=npad_l, xv=xv):
                    st.w(s, "sp", "sa")
                    st.inc(s.dma_start(
                        out=xv[:, :, npad_l:npad_l + (hi - lo)],
                        in_=ap["s_xh3"][:, :, lo:hi].rearrange("c p t -> p c t")), "sl", 16)

                if tb == 0 or tb == NTB - 1:
                    @block.vector
                    def _(v, tb=tb, xv=xv):
                        st.w(v, "sl", "sp")
                        col = 0 if tb == 0 else (W + 1)
                        st.inc(v.memset(xv[:, :, col:col + 1], 0.0), "sv", 1)

                @block.vector
                def _(v, tb=tb, xv=xv, mixc_name=mixc_name):
                    st.w(v, "sl", "sp")
                    for c in range(4):
                        cur = xv[:, c, 1:1 + W]
                        sh = xv[:, c, 0:W] if c < 2 else xv[:, c, 2:2 + W]
                        dst = mix[:, c * W:(c + 1) * W]
                        v.tensor_tensor(dst, cur, sh, ALU.subtract)
                        v.tensor_scalar(dst, dst, st.sb[mixc_name][:, c:c + 1], None, ALU.mult)
                        v.tensor_tensor(dst, dst, sh, ALU.add)
                    st.inc(v.tensor_copy(mix[0:1, 0:1], mix[0:1, 0:1]), "sv", 1)

                for dj in range(NCH):
                    pse = ps1 if dj % 2 == 0 else ps2

                    @block.tensor
                    def _(t, dj=dj, pse=pse, xv=xv):
                        st.w(t, "sv", "sa", "sg")
                        for c in range(NCH):
                            rhs = mix[:, c * W:(c + 1) * W] if c < 4 else xv[:, c, 1:1 + W]
                            mm = t.matmul(pse[:], wsb[:, c * D + dj * 128: c * D + dj * 128 + 128],
                                          rhs, start=(c == 0), stop=(c == NCH - 1))
                        st.inc(mm, "sp", 1)

                    @block.scalar
                    def _(a, dj=dj, pse=pse, act=act):
                        st.w(a, "sp", "so")
                        st.inc(a.activation(evac[:], pse[:], act), "sa", 1)

                    @block.gpsimd
                    def _(g, tb=tb, dj=dj, out_name=out_name):
                        st.w(g, "sa")
                        st.inc(g.dma_start(out=ap[out_name][dj, :, tb * W:(tb + 1) * W], in_=evac[:]), "so", 16)

        mix_pass("wbk", "mixk", "s_ek", AF.Exp)
        mix_pass("wbv", "mixv", "s_vb", AF.Identity)
        mix_pass("wbr", "mixr", "s_sr", AF.Sigmoid)
    full_barrier(st)

    with ExitStack() as ctx:
        ek = ctx.enter_context(nc.sbuf_tensor("s3_ek", [128, T], BF16))
        vb = ctx.enter_context(nc.sbuf_tensor("s3_vb", [128, T], BF16))
        sr = ctx.enter_context(nc.sbuf_tensor("s3_sr", [128, T], BF16))
        eku = ctx.enter_context(nc.sbuf_tensor("s3_eku", [128, T], BF16))
        ekv = ctx.enter_context(nc.sbuf_tensor("s3_ekv", [128, T], BF16))
        asc = ctx.enter_context(nc.sbuf_tensor("s3_a", [128, T + 1], F32))
        bsc = ctx.enter_context(nc.sbuf_tensor("s3_b", [128, T + 1], F32))
        wkv = ctx.enter_context(nc.sbuf_tensor("s3_w", [128, T], BF16))
        for cj in range(NCH):
            @block.sync
            def _(s, cj=cj):
                st.w(s, "so", "sv")
                st.inc(s.dma_start(out=ek[:], in_=ap["s_ek"][cj]), "sl", 16)
                st.inc(s.dma_start(out=vb[:], in_=ap["s_vb"][cj]), "sl", 16)
                st.inc(s.dma_start(out=sr[:], in_=ap["s_sr"][cj]), "sl", 16)

            @block.vector
            def _(v, cj=cj):
                st.w(v, "sl")
                v.tensor_scalar(eku[:], ek[:], st.sb["eu"][:, cj:cj + 1], None, ALU.mult)
                v.tensor_tensor(ekv[:], ek[:], vb[:], ALU.mult)
                v.memset(asc[:, 0:1], 0.0)
                v.memset(bsc[:, 0:1], 0.0)
                v.tensor_tensor_scan(asc[:, 1:T + 1], st.sb["ew"][:, cj:cj + 1].broadcast_to([128, T]),
                                     ekv[:], 0.0, ALU.mult, ALU.add)
                v.tensor_tensor_scan(bsc[:, 1:T + 1], st.sb["ew"][:, cj:cj + 1].broadcast_to([128, T]),
                                     ek[:], 0.0, ALU.mult, ALU.add)
                v.tensor_tensor(ekv[:], eku[:], vb[:], ALU.mult)
                v.tensor_tensor(asc[:, 0:T], asc[:, 0:T], ekv[:], ALU.add)
                st.inc(v.tensor_tensor(bsc[:, 0:T], bsc[:, 0:T], eku[:], ALU.add), "sv", 1)

            @block.scalar
            def _(a, cj=cj):
                st.w(a, "sa", "sv")
                a.activation(bsc[:, 0:T], bsc[:, 0:T], AF.Ln)
                st.inc(a.activation(bsc[:, 0:T], bsc[:, 0:T], AF.Exp, scale=-1.0), "sa", 1)

            @block.vector
            def _(v, cj=cj):
                st.w(v, "sa")
                v.tensor_tensor(asc[:, 0:T], asc[:, 0:T], bsc[:, 0:T], ALU.mult)
                st.inc(v.tensor_tensor(wkv[:], asc[:, 0:T], sr[:], ALU.mult), "sv", 1)

            @block.gpsimd
            def _(g, cj=cj):
                st.w(g, "sv")
                st.inc(g.dma_start(out=ap["s_srw"][cj], in_=wkv[:]), "so", 16)
    full_barrier(st)

    with ExitStack() as ctx:
        wo = ctx.enter_context(nc.sbuf_tensor("g3_wo", [128, NCH * D], BF16))
        sw = ctx.enter_context(nc.sbuf_tensor("g3_sw", [128, NCH * W], BF16))
        x2b = ctx.enter_context(nc.sbuf_tensor("g3_x2", [128, NCH * W], F32))
        x3b = ctx.enter_context(nc.sbuf_tensor("g3_x3", [128, NCH * W], F32))
        pa = ctx.enter_context(nc.psum_tensor("g3_pa", [128, W], F32))

        @block.gpsimd
        def _(g):
            st.w(g, "so")
            st.inc(g.dma_start(out=wo[:].rearrange("p (c d) -> p c d", c=NCH), in_=ap["wbo"].rearrange("(c p) d -> p c d", p=128)), "sg", 16)

        for tb in range(NTB):
            @block.sync
            def _(s, tb=tb):
                st.w(s, "sv", "so")
                st.inc(s.dma_start(out=sw[:].rearrange("p (c t) -> p c t", c=NCH), in_=ap["s_srw"][:, :, tb * W:(tb + 1) * W]
                                   .rearrange("c p t -> p c t")), "sl", 16)
                st.inc(s.dma_start(out=x2b[:].rearrange("p (c t) -> p c t", c=NCH), in_=ap["s_x2"][:, :, tb * W:(tb + 1) * W]
                                   .rearrange("c p t -> p c t")), "sl", 16)

            for c in range(NCH):
                @block.tensor
                def _(t, c=c):
                    st.w(t, "sl", "sv", "sg")
                    for ic in range(NCH):
                        mm = t.matmul(pa[:], wo[:, ic * D + c * 128: ic * D + c * 128 + 128],
                                      sw[:, ic * W:(ic + 1) * W], start=(ic == 0), stop=(ic == NCH - 1))
                    st.inc(mm, "sp", 1)

                @block.vector
                def _(v, c=c):
                    st.w(v, "sp")
                    st.inc(v.tensor_tensor(x3b[:, c * W:(c + 1) * W], pa[:],
                                           x2b[:, c * W:(c + 1) * W], ALU.add), "sv", 1)

            @block.gpsimd
            def _(g2, tb=tb):
                st.w(g2, "sv")
                st.inc(g2.dma_start(out=ap["s_x3"][:, :, tb * W:(tb + 1) * W].rearrange("c p t -> p c t"), in_=x3b[:].rearrange("p (c t) -> p c t", c=NCH)), "so", 16)
    full_barrier(st)


def phase_final(st, debug_taps):
    block = st.block

    @block.sync
    def _(s):
        st.wall(s)
        for tap in debug_taps:
            st.inc(s.dma_start(out=st.ap["dbg_" + tap], in_=st.ap[tap]), "sl", 16)
        s.wait_ge(st.s["sl"], st.c["sl"])

    @block.gpsimd
    def _(g):
        st.wall(g)



TAP_SPECS = {
    "s_x1": ([NCH, 128, T], F32), "s_x2": ([NCH, 128, T], F32), "s_x3": ([NCH, 128, T], F32),
    "s_qT": ([NH, 128, T], BF16), "s_kT": ([NKV, 128, T], BF16), "s_v": ([NKV, T, 128], BF16),
    "s_oT": ([NH, 128, T], BF16), "s_xh3": ([NCH, 128, T], BF16),
    "s_ek": ([NCH, 128, T], BF16), "s_vb": ([NCH, 128, T], BF16),
    "s_sr": ([NCH, 128, T], BF16), "s_srw": ([NCH, 128, T], BF16),
    "s_w1c": ([E, NCH, 128, D], BF16), "s_w2c": ([E, NCH, 128, D], BF16),
    "s_dbgR": ([32, 12 * W], BF16), "s_dbgA": ([128, 8 * W], F32),
}


def build_nc(debug_taps=(), phases=("pre", "1a", "1b", "1c", "2", "3", "4")):
    nc = bass.Bass("TRN2", target_bir_lowering=False, debug=False)

    with (
        nc.semaphore("sl") as sl, nc.semaphore("sg") as sg, nc.semaphore("sp") as sp,
        nc.semaphore("sa") as sa, nc.semaphore("sv") as sv, nc.semaphore("so") as so,
        nc.semaphore("sq") as sq, nc.Block() as block,
    ):
        st = St(nc, block, dict(sl=sl, sg=sg, sp=sp, sa=sa, sv=sv, so=so, sq=sq))

        def inp(name, shape, dt=F32):
            st.ap[name] = nc.dram_tensor(name, shape, dt, kind="ExternalInput").ap()

        def scr(name, shape, dt):
            st.ap[name] = nc.dram_tensor(name, shape, dt, kind="Internal").ap()

        inp("x0", [NCH, 128, T])
        inp("wq", [D, D]); inp("wqp", [D, D])
        inp("wk", [D, 512]); inp("wkp", [D, 512]); inp("wv", [D, 512])
        inp("woa", [D, D])
        inp("wbk", [D, D]); inp("wbv", [D, D]); inp("wbr", [D, D]); inp("wbo", [D, D])
        inp("mixk_c", [128, NCH]); inp("mixv_c", [128, NCH]); inp("mixr_c", [128, NCH])
        inp("sd_c", [128, NCH]); inp("sf_c", [128, NCH])
        inp("w1", [E, D, D]); inp("w2", [E, D, D])
        inp("b1r", [E, 1, D]); inp("b2r", [E, 1, D])
        inp("b1_cc", [E, 128, NCH]); inp("b2_cc", [E, 128, NCH])
        inp("be1_cc", [128, NCH * E])
        inp("g1_r", [1, E * D]); inp("g2_r", [1, E * D]); inp("be2_r", [1, E * D])
        inp("gw", [D, E]); inp("gb_col", [E, 1]); inp("eps11", [1, 1])
        inp("cosT", [128, T], BF16); inp("sinTs", [128, T], BF16)
        inp("masks", [4, 128, W], BF16)
        inp("ones_col", [128, 1], BF16); inp("ones_row", [1, 128], BF16)

        st.ap["y"] = nc.dram_tensor("y", [NCH, 128, T], F32, kind="ExternalOutput").ap()

        scr("s_qT", [NH, 128, T], BF16); scr("s_kT", [NKV, 128, T], BF16)
        scr("s_v", [NKV, T, 128], BF16); scr("s_oT", [NH, 128, T], BF16)
        for h in range(NH):
            scr(f"s_rcp{h}", [1, T], F32)
        scr("s_x1", [NCH, 128, T], F32); scr("s_x2", [NCH, 128, T], F32)
        scr("s_x3", [NCH, 128, T], F32); scr("s_xh3", [NCH, 128, T], BF16)
        scr("s_w1c", [E, NCH, 128, D], BF16); scr("s_w2c", [E, NCH, 128, D], BF16)
        scr("s_dbgR", [32, 12 * W], BF16); scr("s_dbgA", [128, 8 * W], F32)
        scr("s_ek", [NCH, 128, T], BF16); scr("s_vb", [NCH, 128, T], BF16)
        scr("s_sr", [NCH, 128, T], BF16); scr("s_srw", [NCH, 128, T], BF16)

        for tap in debug_taps:
            shp, dt = TAP_SPECS[tap]
            st.ap["dbg_" + tap] = nc.dram_tensor("dbg_" + tap, shp, dt, kind="ExternalOutput").ap()

        if "pre" in phases:
            phase_preamble(st)
        if "1a" in phases:
            phase_1a(st)
        if "1b" in phases:
            phase_1b(st)
        if "1c" in phases:
            phase_1c(st)
        if "2" in phases:
            phase_ffn(st, "s_x1", "s_x2")
        if "3" in phases:
            phase_3(st)
        if "4" in phases:
            phase_ffn(st, "s_x3", "y")
        phase_final(st, debug_taps)
    return nc


def stage_inputs(inputs):
    x = np.asarray(inputs["L0_attn_a_hidden_states"], np.float32)
    wq = np.asarray(inputs["L0_attn_a_Wq"], np.float32)
    wk = np.asarray(inputs["L0_attn_a_Wk"], np.float32)
    wv = np.asarray(inputs["L0_attn_a_Wv"], np.float32)
    woa = np.asarray(inputs["L0_attn_a_Wo"], np.float32)
    bk = np.asarray(inputs["L0_attn_b_Wk"], np.float32)
    bv = np.asarray(inputs["L0_attn_b_Wv"], np.float32)
    br = np.asarray(inputs["L0_attn_b_Wr"], np.float32)
    bo = np.asarray(inputs["L0_attn_b_Wo"], np.float32)
    sd = np.asarray(inputs["L0_attn_b_spatial_decay"], np.float32)
    sf = np.asarray(inputs["L0_attn_b_spatial_first"], np.float32)
    mk = np.asarray(inputs["L0_attn_b_mix_k"], np.float32)
    mv = np.asarray(inputs["L0_attn_b_mix_v"], np.float32)
    mr = np.asarray(inputs["L0_attn_b_mix_r"], np.float32)
    gw = np.asarray(inputs["L0_ffn_gate_W"], np.float32)
    gb = np.asarray(inputs["L0_ffn_gate_b"], np.float32)
    w1 = np.asarray(inputs["L0_ffn_W1"], np.float32)
    b1 = np.asarray(inputs["L0_ffn_b1"], np.float32)
    g1 = np.asarray(inputs["L0_ffn_g1"], np.float32)
    be1 = np.asarray(inputs["L0_ffn_be1"], np.float32)
    w2 = np.asarray(inputs["L0_ffn_W2"], np.float32)
    b2 = np.asarray(inputs["L0_ffn_b2"], np.float32)
    g2 = np.asarray(inputs["L0_ffn_g2"], np.float32)
    be2 = np.asarray(inputs["L0_ffn_be2"], np.float32)

    cosT, sinTs = _rope_tables()
    common = {
        "wq": np.ascontiguousarray(wq), "wqp": _perm_cols(wq),
        "wk": np.ascontiguousarray(wk), "wkp": _perm_cols(wk),
        "wv": np.ascontiguousarray(wv), "woa": np.ascontiguousarray(woa),
        "wbk": np.ascontiguousarray(bk.T), "wbv": np.ascontiguousarray(bv.T),
        "wbr": np.ascontiguousarray(br.T), "wbo": np.ascontiguousarray(bo.T),
        "mixk_c": _cols(mk), "mixv_c": _cols(mv), "mixr_c": _cols(mr),
        "sd_c": _cols(sd), "sf_c": _cols(sf),
        "w1": np.ascontiguousarray(w1), "w2": np.ascontiguousarray(w2),
        "b1r": np.ascontiguousarray(b1.reshape(E, 1, D)),
        "b2r": np.ascontiguousarray(b2.reshape(E, 1, D)),
        "b1_cc": np.stack([_cols(b1[e]) for e in range(E)]),
        "b2_cc": np.stack([_cols(b2[e]) for e in range(E)]),
        "be1_cc": np.concatenate([_cols(be1[e]) for e in range(E)], axis=1),
        "g1_r": np.ascontiguousarray(g1.reshape(1, E * D)),
        "g2_r": np.ascontiguousarray(g2.reshape(1, E * D)),
        "be2_r": np.ascontiguousarray(be2.reshape(1, E * D)),
        "gw": np.ascontiguousarray(gw), "gb_col": np.ascontiguousarray(gb.reshape(E, 1)),
        "eps11": np.full((1, 1), LN_EPS, np.float32),
        "cosT": cosT, "sinTs": sinTs, "masks": _masks(),
        "ones_col": np.ones((128, 1), BF), "ones_row": np.ones((1, 128), BF),
    }
    in_maps = []
    for c in range(8):
        m = dict(common)
        m["x0"] = np.ascontiguousarray(x[c // 2].T.reshape(NCH, 128, T))
        in_maps.append(m)
    return in_maps


def kernel(**inputs):
    in_maps = stage_inputs(inputs)
    nc = build_nc()
    res = run_bass_kernel_spmd(nc, in_maps, core_ids=list(range(8)))
    out = np.empty((B, T, D), np.float32)
    for b in range(B):
        out[b] = res.results[2 * b]["y"].reshape(D, T).T
    return out


# version 02
# speedup vs baseline: 1.0049x; 1.0049x over previous
import math
import numpy as np
import ml_dtypes
import concourse.bass as bass
import concourse.mybir as mybir
from concourse.bass_utils import run_bass_kernel_spmd

F32 = mybir.dt.float32
BF16 = mybir.dt.bfloat16
AF = mybir.ActivationFunctionType
ALU = mybir.AluOpType
AXX = mybir.AxisListType.X

B = 4
T = 8192
D = 1024
NH = 8
NKV = 4
GR = NH // NKV
E = 4
W = 512
NTB = T // W
NCH = D // 128
LN_EPS = 1e-5
THETA = 10000.0
BF = ml_dtypes.bfloat16

DBG = {}
ROT = [1, 2, 3, 0] + list(range(4, 32))

def r3(dst2d, n_outer, src3d):
    return dst2d.rearrange("p (a b) -> p a b", a=n_outer), src3d



class St:

    def __init__(self, nc, block, sems):
        self.nc = nc
        self.block = block
        self.s = sems
        self.c = {k: 0 for k in sems}
        self.ap = {}
        self.sb = {}

    def inc(self, instr, name, n):
        instr.then_inc(self.s[name], n)
        self.c[name] += n

    def w(self, eng, *names):
        for n in names:
            if self.c[n] > 0:
                eng.wait_ge(self.s[n], self.c[n])

    def wall(self, eng):
        self.w(eng, "sl", "sg", "sp", "sa", "sv", "so", "sq")


def full_barrier(st):
    block = st.block
    for engname in ("sync", "gpsimd", "tensor", "scalar", "vector"):
        def mk(e, st=st):
            st.wall(e)
        getattr(block, engname)(mk)



def _rope_tables():
    j = np.arange(128) % 64
    inv_freq = 1.0 / THETA ** (2 * j / 128)
    ang = np.outer(inv_freq, np.arange(T))
    cosT = np.cos(ang).astype(BF)
    sign = np.where(np.arange(128) < 64, -1.0, 1.0)[:, None]
    sinTs = (np.sin(ang) * sign).astype(BF)
    return cosT, sinTs


def _perm_cols(w):
    d = w.shape[1]
    idx = np.arange(d)
    h = idx % 128
    rho = np.where(h < 64, idx + 64, idx - 64)
    return np.ascontiguousarray(w[:, rho])


def _masks():
    sl_ = np.arange(128)[:, None]
    ql = np.arange(W)[None, :]
    return np.stack([(sl_ <= ql + off).astype(BF) for off in (0, -128, -256, -384)])


def _cols(vec):
    return np.ascontiguousarray(np.asarray(vec, np.float32).reshape(NCH, 128).T)



def ln_block(st, buf, xf, xb, xh):
    nc, block = st.nc, st.block
    sq, rows, rows_bf, mrt, psA, psB, ps_bc = (
        buf["sq"], buf["rows"], buf["rows_bf"], buf["mrt"], buf["psA"], buf["psB"], buf["ps_bc"])

    @block.vector
    def _(v):
        st.w(v, "sg", "sl")
        v.tensor_tensor(sq[:], xb, xb, ALU.mult)
        st.inc(v.tensor_copy(sq[0:1, 0:1], sq[0:1, 0:1]), "sv", 1)

    @block.tensor
    def _(t):
        st.w(t, "sv")
        for c in range(NCH):
            t.matmul(psA[:], st.sb["ones_col"][:], xb[:, c * W:(c + 1) * W],
                     start=(c == 0), stop=(c == NCH - 1))
        for c in range(NCH):
            mm = t.matmul(psB[:], st.sb["ones_col"][:], sq[:, c * W:(c + 1) * W],
                          start=(c == 0), stop=(c == NCH - 1))
        st.inc(mm, "sp", 1)

    negm, var, tmp = rows[0:1, 0:W], rows[0:1, W:2 * W], rows[0:1, 2 * W:3 * W]
    negm_bf, rstd_bf = rows_bf[0:1, 0:W], rows_bf[0:1, W:2 * W]

    @block.vector
    def _(v):
        st.w(v, "sp")
        v.tensor_scalar(negm, psA[0:1, :], -1.0 / D, None, ALU.mult)
        v.tensor_tensor(tmp, negm, negm, ALU.mult)
        st.inc(v.scalar_tensor_tensor(var, psB[0:1, :], 1.0 / D, tmp, ALU.mult, ALU.subtract), "sv", 1)

    @block.scalar
    def _(a):
        st.w(a, "sv")
        a.activation(var, var, AF.Ln, bias=st.sb["eps"][0:1, 0:1])
        st.inc(a.activation(var, var, AF.Exp, scale=-0.5), "sa", 1)

    @block.vector
    def _(v):
        st.w(v, "sa")
        v.tensor_copy(negm_bf, negm)
        st.inc(v.tensor_copy(rstd_bf, var), "sv", 1)

    Mt = mrt[:, 0:W]

    @block.tensor
    def _(t):
        st.w(t, "sv")
        st.inc(t.matmul(ps_bc[:], st.sb["ones_row"][0:1, :], negm_bf, start=True, stop=True), "sp", 1)

    @block.vector
    def _(v):
        st.w(v, "sp")
        st.inc(v.tensor_copy(Mt, ps_bc[:]), "sv", 1)

    @block.tensor
    def _(t):
        st.w(t, "sv")
        st.inc(t.matmul(ps_bc[:], st.sb["ones_row"][0:1, :], rstd_bf, start=True, stop=True), "sp", 1)

    @block.vector
    def _(v):
        st.w(v, "sp")
        Rt = mrt[:, W:2 * W]
        tmp2 = mrt[:, 2 * W:3 * W]
        v.tensor_copy(Rt, ps_bc[:])
        for c in range(NCH):
            v.tensor_tensor(tmp2, xf[:, c * W:(c + 1) * W], Mt, ALU.add)
            v.tensor_tensor(xh[:, c * W:(c + 1) * W], tmp2, Rt, ALU.mult)
        st.inc(v.tensor_copy(xh[0:1, 0:1], xh[0:1, 0:1]), "sv", 1)


def mk_ln_buffers(st, nc, prefix, stack):
    bufs = {
        "sq": stack.enter_context(nc.sbuf_tensor(prefix + "_sq", [128, NCH * W], BF16)),
        "rows": stack.enter_context(nc.sbuf_tensor(prefix + "_rows", [1, 3 * W], F32)),
        "rows_bf": stack.enter_context(nc.sbuf_tensor(prefix + "_rowsbf", [1, 2 * W], BF16)),
        "mrt": stack.enter_context(nc.sbuf_tensor(prefix + "_mrt", [128, 3 * W], F32)),
        "psA": stack.enter_context(nc.psum_tensor(prefix + "_psA", [1, W], F32)),
        "psB": stack.enter_context(nc.psum_tensor(prefix + "_psB", [1, W], F32)),
        "ps_bc": stack.enter_context(nc.psum_tensor(prefix + "_bc", [128, W], F32)),
    }
    sqh = bufs["sq"]
    bufs["sq"] = sqh[:]
    for k in ("rows", "rows_bf", "mrt"):
        bufs[k] = bufs[k][:]
    for k in ("psA", "psB", "ps_bc"):
        bufs[k] = bufs[k][:]
    return bufs



def phase_preamble(st):
    nc, ap, block = st.nc, st.ap, st.block

    for name, shape, dt in (
        ("ones_col", [128, 1], BF16), ("ones_row", [1, 128], BF16),
        ("b1c", [128, NCH * E], F32), ("b2c", [128, NCH * E], F32),
        ("be1c", [128, NCH * E], F32),
        ("ew", [128, NCH], F32), ("eu", [128, NCH], F32),
        ("g1r", [1, E * D], BF16), ("g2r", [1, E * D], BF16),
        ("be2r", [1, E * D], BF16),
        ("mixk", [128, NCH], F32), ("mixv", [128, NCH], F32), ("mixr", [128, NCH], F32),
        ("gb", [E, 1], F32), ("eps", [1, 1], F32),
    ):
        st.sb[name] = nc.alloc_sbuf_tensor("pers_" + name, shape, dt)

    @block.sync
    def _(s):
        for name, src in (("ones_col", "ones_col"), ("ones_row", "ones_row"),
                          ("be1c", "be1_cc"), ("mixk", "mixk_c"), ("mixv", "mixv_c"),
                          ("mixr", "mixr_c"), ("gb", "gb_col"), ("eps", "eps11")):
            st.inc(s.dma_start(out=st.sb[name][:], in_=ap[src]), "sl", 16)

    @block.gpsimd
    def _(g):
        for name, src in (("g1r", "g1_r"), ("g2r", "g2_r"), ("be2r", "be2_r")):
            st.inc(g.dma_start(out=st.sb[name][:], in_=ap[src]), "sg", 16)

    from contextlib import ExitStack
    with ExitStack() as ctx:
        wbuf = ctx.enter_context(nc.sbuf_tensor("pre_w", [128, D], F32))
        obuf = ctx.enter_context(nc.sbuf_tensor("pre_o", [128, D], BF16))
        mcol = ctx.enter_context(nc.sbuf_tensor("pre_m", [128, 1], F32))
        mcol2 = ctx.enter_context(nc.sbuf_tensor("pre_m2", [128, 1], F32))
        ccol = ctx.enter_context(nc.sbuf_tensor("pre_cc", [128, NCH], F32))
        brow = ctx.enter_context(nc.sbuf_tensor("pre_br", [1, D], F32))
        bmv = ctx.enter_context(nc.sbuf_tensor("pre_bm", [1, 1], F32))
        bmb = ctx.enter_context(nc.sbuf_tensor("pre_bmb", [1, 1], BF16))
        pcol = ctx.enter_context(nc.psum_tensor("pre_pc", [128, 1], F32))

        @block.scalar
        def _(a):
            st.w(a, "sl")

        for vname, dest in (("sd_c", "ew"), ("sf_c", "eu")):
            @block.sync
            def _(s, vname=vname):
                st.w(s, "sa")
                st.inc(s.dma_start(out=ccol[:], in_=ap[vname]), "sl", 16)

            @block.scalar
            def _(a, dest=dest):
                st.w(a, "sl")
                st.inc(a.activation(st.sb[dest][:], ccol[:], AF.Exp, scale=1.0 / T), "sa", 1)

        for wi, (wname, sname, bname, cname, dest) in enumerate((
            ("w1", "s_w1c", "b1r", "b1_cc", "b1c"),
            ("w2", "s_w2c", "b2r", "b2_cc", "b2c"),
        )):
            for e in range(E):
                for ci in range(NCH):
                    @block.sync
                    def _(s, e=e, ci=ci, wname=wname):
                        st.w(s, "so")
                        st.inc(s.dma_start(out=wbuf[:], in_=ap[wname][e, ci * 128:(ci + 1) * 128, :]), "sl", 16)

                    @block.vector
                    def _(v):
                        st.w(v, "sl")
                        st.inc(v.reduce_sum(mcol[:], wbuf[:], axis=AXX), "sv", 1)

                    @block.scalar
                    def _(a):
                        st.w(a, "sv")
                        st.inc(a.activation(mcol2[:], mcol[:], AF.Identity, scale=1.0 / D), "sa", 1)

                    @block.vector
                    def _(v):
                        st.w(v, "sa")
                        st.inc(v.tensor_scalar(obuf[:], wbuf[:], mcol2[:, 0:1], None, ALU.subtract), "sv", 1)

                    @block.gpsimd
                    def _(g, e=e, ci=ci, sname=sname):
                        st.w(g, "sv")
                        st.inc(g.dma_start(out=ap[sname][e, ci], in_=obuf[:]), "so", 16)

                @block.sync
                def _(s, e=e, bname=bname, cname=cname):
                    st.w(s, "sv")
                    st.inc(s.dma_start(out=brow[:], in_=ap[bname][e]), "sl", 16)
                    st.inc(s.dma_start(out=ccol[:], in_=ap[cname][e]), "sl", 16)

                @block.vector
                def _(v):
                    st.w(v, "sl")
                    v.reduce_sum(bmv[:], brow[:], axis=AXX)
                    v.tensor_scalar(bmv[:], bmv[:], 1.0 / D, None, ALU.mult)
                    st.inc(v.tensor_copy(bmb[:], bmv[:]), "sv", 1)

                @block.tensor
                def _(t):
                    st.w(t, "sv")
                    st.inc(t.matmul(pcol[:], st.sb["ones_row"][0:1, :], bmb[:], start=True, stop=True), "sp", 1)

                @block.vector
                def _(v, e=e, dest=dest):
                    st.w(v, "sp")
                    st.inc(v.tensor_tensor(st.sb[dest][:, e * NCH:(e + 1) * NCH], ccol[:],
                                           pcol[:, 0:1].broadcast_to([128, NCH]), ALU.subtract), "sv", 1)

    full_barrier(st)


def phase_1a(st):
    nc, ap, block = st.nc, st.ap, st.block
    from contextlib import ExitStack
    with ExitStack() as ctx:
        wq = ctx.enter_context(nc.sbuf_tensor("a_wq", [128, NCH * D], BF16))
        wqp = ctx.enter_context(nc.sbuf_tensor("a_wqp", [128, NCH * D], BF16))
        wk = ctx.enter_context(nc.sbuf_tensor("a_wk", [128, NCH * 512], BF16))
        wkp = ctx.enter_context(nc.sbuf_tensor("a_wkp", [128, NCH * 512], BF16))
        wv = ctx.enter_context(nc.sbuf_tensor("a_wv", [128, NCH * 512], BF16))
        cosb = ctx.enter_context(nc.sbuf_tensor("a_cos", [128, T], BF16))
        sinb = ctx.enter_context(nc.sbuf_tensor("a_sin", [128, T], BF16))
        xf = ctx.enter_context(nc.sbuf_tensor("a_xf", [128, NCH * W], F32))
        xb = ctx.enter_context(nc.sbuf_tensor("a_xb", [128, NCH * W], BF16))
        xh = ctx.enter_context(nc.sbuf_tensor("a_xh", [128, NCH * W], BF16))
        q1 = ctx.enter_context(nc.sbuf_tensor("a_q1", [128, W], F32))
        q2 = ctx.enter_context(nc.sbuf_tensor("a_q2", [128, W], F32))
        qo = ctx.enter_context(nc.sbuf_tensor("a_qo", [128, W], BF16))
        vo = ctx.enter_context(nc.sbuf_tensor("a_vo", [128, 512], BF16))
        lnb = mk_ln_buffers(st, nc, "a", ctx)
        ps1 = ctx.enter_context(nc.psum_tensor("a_p1", [128, W], F32))
        ps2 = ctx.enter_context(nc.psum_tensor("a_p2", [128, W], F32))
        psv = ctx.enter_context(nc.psum_tensor("a_pv", [128, 512], F32))

        @block.gpsimd
        def _(g):
            st.inc(g.dma_start(out=wq[:].rearrange("p (c d) -> p c d", c=NCH), in_=ap["wq"].rearrange("(c p) d -> p c d", p=128)), "sg", 16)
            st.inc(g.dma_start(out=wqp[:].rearrange("p (c d) -> p c d", c=NCH), in_=ap["wqp"].rearrange("(c p) d -> p c d", p=128)), "sg", 16)
            st.inc(g.dma_start(out=wk[:].rearrange("p (c d) -> p c d", c=NCH), in_=ap["wk"].rearrange("(c p) d -> p c d", p=128)), "sg", 16)
            st.inc(g.dma_start(out=wkp[:].rearrange("p (c d) -> p c d", c=NCH), in_=ap["wkp"].rearrange("(c p) d -> p c d", p=128)), "sg", 16)
            st.inc(g.dma_start(out=wv[:].rearrange("p (c d) -> p c d", c=NCH), in_=ap["wv"].rearrange("(c p) d -> p c d", p=128)), "sg", 16)

        @block.sync
        def _(s):
            st.inc(s.dma_start(out=cosb[:], in_=ap["cosT"]), "sl", 16)
            st.inc(s.dma_start(out=sinb[:], in_=ap["sinTs"]), "sl", 16)

        for tb in range(NTB):
            @block.sync
            def _(s, tb=tb):
                st.w(s, "sv", "so")
                st.inc(s.dma_start(out=xf[:].rearrange("p (c t) -> p c t", c=NCH), in_=ap["x0"][:, :, tb * W:(tb + 1) * W]
                                   .rearrange("c p t -> p c t")), "sl", 16)

            @block.gpsimd
            def _(g, tb=tb):
                st.w(g, "sv")
                st.inc(g.dma_start(out=xb[:].rearrange("p (c t) -> p c t", c=NCH), in_=ap["x0"][:, :, tb * W:(tb + 1) * W]
                                   .rearrange("c p t -> p c t")), "sg", 16)

            ln_block(st, lnb, xf[:], xb[:], xh[:])

            for kind, nchunks, w_sb, wp_sb, outname, wstride in (
                ("q", NCH, wq, wqp, "s_qT", D),
                ("k", 4, wk, wkp, "s_kT", 512),
            ):
                for dj in range(nchunks):
                    @block.tensor
                    def _(t, dj=dj, w_sb=w_sb, wp_sb=wp_sb, wstride=wstride):
                        st.w(t, "sv")
                        for c in range(NCH):
                            t.matmul(ps1[:], w_sb[:, c * wstride + dj * 128: c * wstride + dj * 128 + 128],
                                     xh[:, c * W:(c + 1) * W], start=(c == 0), stop=(c == NCH - 1))
                        for c in range(NCH):
                            mm = t.matmul(ps2[:], wp_sb[:, c * wstride + dj * 128: c * wstride + dj * 128 + 128],
                                          xh[:, c * W:(c + 1) * W], start=(c == 0), stop=(c == NCH - 1))
                        st.inc(mm, "sp", 1)

                    @block.vector
                    def _(v, dj=dj, tb=tb):
                        st.w(v, "sp", "so")
                        csl = cosb[:, tb * W:(tb + 1) * W]
                        ssl = sinb[:, tb * W:(tb + 1) * W]
                        v.tensor_tensor(q1[:], ps1[:], csl, ALU.mult)
                        v.tensor_tensor(q2[:], ps2[:], ssl, ALU.mult)
                        st.inc(v.tensor_tensor(qo[:], q1[:], q2[:], ALU.add), "sv", 1)

                    @block.gpsimd
                    def _(g, dj=dj, tb=tb, outname=outname):
                        st.w(g, "sv")
                        st.inc(g.dma_start(out=ap[outname][dj, :, tb * W:(tb + 1) * W], in_=qo[:]), "so", 16)

            for tt in range(4):
                @block.tensor
                def _(t, tt=tt):
                    st.w(t, "sv")
                    for c in range(NCH):
                        mm = t.matmul(psv[:], xh[:, c * W + tt * 128: c * W + tt * 128 + 128],
                                      wv[:, c * 512:(c + 1) * 512], start=(c == 0), stop=(c == NCH - 1))
                    st.inc(mm, "sp", 1)

                @block.vector
                def _(v, tt=tt):
                    st.w(v, "sp", "so")
                    st.inc(v.tensor_copy(vo[:], psv[:]), "sv", 1)

                @block.gpsimd
                def _(g, tt=tt, tb=tb):
                    st.w(g, "sv")
                    st.inc(g.dma_start(
                        out=ap["s_v"].rearrange("g t d -> t g d")[tb * W + tt * 128: tb * W + tt * 128 + 128, :, :],
                        in_=vo[:].rearrange("p (g d) -> p g d", g=NKV)), "so", 16)
    full_barrier(st)


def phase_1b(st):
    nc, ap, block = st.nc, st.ap, st.block
    from contextlib import ExitStack
    with ExitStack() as ctx:
        kT = ctx.enter_context(nc.sbuf_tensor("b_kT", [128, T], BF16))
        vg = ctx.enter_context(nc.sbuf_tensor("b_v", [128, T], BF16))
        qsb = ctx.enter_context(nc.sbuf_tensor("b_q", [128, 2 * W], BF16))
        pT = ctx.enter_context(nc.sbuf_tensor("b_p", [128, 2 * 1024], BF16))
        rs = ctx.enter_context(nc.sbuf_tensor("b_rs", [1, T], F32))
        rcp = ctx.enter_context(nc.sbuf_tensor("b_rcp", [1, T], F32))
        msk = ctx.enter_context(nc.sbuf_tensor("b_m", [128, 4 * W], BF16))
        oev = ctx.enter_context(nc.sbuf_tensor("b_oe", [128, 2 * W], BF16))
        s0 = ctx.enter_context(nc.psum_tensor("b_s0", [128, W], F32))
        s1 = ctx.enter_context(nc.psum_tensor("b_s1", [128, W], F32))
        s2 = ctx.enter_context(nc.psum_tensor("b_s2", [128, W], F32))
        s3 = ctx.enter_context(nc.psum_tensor("b_s3", [128, W], F32))
        ob0 = ctx.enter_context(nc.psum_tensor("b_o0", [128, W], F32))
        ob1 = ctx.enter_context(nc.psum_tensor("b_o1", [128, W], F32))
        rb0 = ctx.enter_context(nc.psum_tensor("b_r0", [1, W], F32))
        rb1 = ctx.enter_context(nc.psum_tensor("b_r1", [1, W], F32))
        pair_banks = [(s0, s1), (s2, s3)]

        @block.sync
        def _(s):
            st.inc(s.dma_start(out=msk[:].rearrange("p (m q) -> p m q", m=4), in_=ap["masks"].rearrange("m p q -> p m q")), "sl", 16)

        exp_done = {}
        rsav_done = {}
        mask_done = {}
        scores_done = {}
        qload_done = {}
        evac_done = {}
        store_done = {}
        gpi = 0
        gqb = 0

        for g in range(NKV):
            @block.sync
            def _(s, g=g):
                st.w(s, "sp", "sa", "sv")
                st.inc(s.dma_start(out=kT[:], in_=ap["s_kT"][g]), "sl", 16)
                st.inc(s.dma_start(out=vg[:].rearrange("p (j d) -> p j d", j=T // 128), in_=ap["s_v"][g].rearrange("(j p) d -> p j d", p=128)), "sl", 16)

            for hh in range(GR):
                h = g * GR + hh
                for qb in range(NTB):
                    nch_s = 4 * qb + 4
                    qpar = gqb % 2
                    qsl = qsb[:, qpar * W:(qpar + 1) * W]

                    @block.sync
                    def _(s, h=h, qb=qb, gqb=gqb, qsl=qsl):
                        if gqb - 2 in scores_done:
                            s.wait_ge(st.s["sp"], scores_done[gqb - 2])
                        st.inc(s.dma_start(out=qsl, in_=ap["s_qT"][h, :, qb * W:(qb + 1) * W]), "sl", 16)
                        qload_done[gqb] = st.c["sl"]

                    npair = (nch_s + 1) // 2
                    obk = ob0 if gqb % 2 == 0 else ob1
                    rbk = rb0 if gqb % 2 == 0 else rb1
                    for pi in range(npair):
                        c0 = 2 * pi
                        c1 = min(2 * pi + 1, nch_s - 1)
                        sA, sB = pair_banks[pi % 2]
                        pslice = pT[:, (pi % 2) * 1024:(pi % 2) * 1024 + 1024]

                        @block.tensor
                        def _(t, c0=c0, c1=c1, sA=sA, sB=sB, gpi=gpi, gqb=gqb, qsl=qsl, pi=pi):
                            t.wait_ge(st.s["sl"], qload_done[gqb])
                            if gpi - 2 in exp_done:
                                t.wait_ge(st.s["sa"], exp_done[gpi - 2])
                            if pi == 0 and gqb - 1 in evac_done:
                                t.wait_ge(st.s["sv"], evac_done[gqb - 1])
                            t.matmul(sA[:], kT[:, c0 * 128:(c0 + 1) * 128], qsl, start=True, stop=True)
                            mm = t.matmul(sB[:], kT[:, c1 * 128:(c1 + 1) * 128], qsl, start=True, stop=True)
                            st.inc(mm, "sp", 1)
                            if pi == npair - 1:
                                scores_done[gqb] = st.c["sp"]

                        sp_after_scores = st.c["sp"]

                        @block.scalar
                        def _(a, sA=sA, sB=sB, pslice=pslice, c0=c0, c1=c1, gpi=gpi, n=sp_after_scores):
                            a.wait_ge(st.s["sp"], n)
                            if gpi - 2 in rsav_done:
                                a.wait_ge(st.s["sp"], rsav_done[gpi - 2])
                            ssc = 1.0 / math.sqrt(128.0)
                            if c1 > c0:
                                a.activation(pslice[:, 0:W], sA[:], AF.Exp, scale=ssc)
                                st.inc(a.activation(pslice[:, W:2 * W], sB[:], AF.Exp, scale=ssc), "sa", 1)
                            else:
                                st.inc(a.activation(pslice[:, 0:W], sA[:], AF.Exp, scale=ssc), "sa", 1)
                            exp_done[gpi] = st.c["sa"]

                        sa_after_exp = st.c["sa"]

                        masked = sorted(set(c for c in (c0, c1) if c >= 4 * qb))
                        if masked:
                            @block.vector
                            def _(v, c0=c0, qb=qb, pslice=pslice, masked=tuple(masked), n=sa_after_exp):
                                v.wait_ge(st.s["sa"], n)
                                last = None
                                for c in masked:
                                    oi = c - 4 * qb
                                    psl = pslice[:, (c - c0) * W:(c - c0) * W + W]
                                    last = v.tensor_tensor(psl, psl, msk[:, oi * W:(oi + 1) * W], ALU.mult)
                                st.inc(last, "sv", 1)
                                mask_done[gpi] = st.c["sv"]

                        sv_after_mask = st.c["sv"]

                        @block.tensor
                        def _(t, c0=c0, c1=c1, pslice=pslice, nch_s=nch_s, gpi=gpi,
                              na=sa_after_exp, nv=sv_after_mask, had_mask=bool(masked),
                              obk=obk, rbk=rbk):
                            t.wait_ge(st.s["sa"], na)
                            if had_mask:
                                t.wait_ge(st.s["sv"], nv)
                            cs = [c0] if c1 == c0 else [c0, c1]
                            for c in cs:
                                psl = pslice[:, (c - c0) * W:(c - c0) * W + W]
                                t.matmul(rbk[:], st.sb["ones_col"][:], psl,
                                         start=(c == 0), stop=(c == nch_s - 1))
                                mm = t.matmul(obk[:], vg[:, c * 128:(c + 1) * 128], psl,
                                              start=(c == 0), stop=(c == nch_s - 1))
                            st.inc(mm, "sp", 1)
                            rsav_done[gpi] = st.c["sp"]

                        gpi += 1

                    sp_after_last_rsav = st.c["sp"]

                    @block.vector
                    def _(v, h=h, qb=qb, gqb=gqb, n=sp_after_last_rsav, obk=obk, rbk=rbk, qpar=qpar):
                        v.wait_ge(st.s["sp"], n)
                        if qb == 0 and "rcp_sa" in st.sb and isinstance(st.sb.get("rcp_sa"), int):
                            v.wait_ge(st.s["sa"], st.sb["rcp_sa"])
                        if gqb - 2 in store_done:
                            v.wait_ge(st.s["so"], store_done[gqb - 2])
                        v.tensor_copy(rs[0:1, qb * W:(qb + 1) * W], rbk[:])
                        st.inc(v.tensor_copy(oev[:, qpar * W:(qpar + 1) * W], obk[:]), "sv", 1)
                        evac_done[gqb] = st.c["sv"]

                    sv_after_evac = st.c["sv"]

                    @block.gpsimd
                    def _(gq, h=h, qb=qb, gqb=gqb, n=sv_after_evac, qpar=qpar):
                        gq.wait_ge(st.s["sv"], n)
                        st.inc(gq.dma_start(out=ap["s_oT"][h, :, qb * W:(qb + 1) * W],
                                            in_=oev[:, qpar * W:(qpar + 1) * W]), "so", 16)
                        store_done[gqb] = st.c["so"]

                    gqb += 1

                @block.scalar
                def _(a, h=h):
                    st.w(a, "sv", "so")
                    a.activation(rcp[:], rs[:], AF.Ln)
                    st.inc(a.activation(rcp[:], rcp[:], AF.Exp, scale=-1.0), "sa", 1)
                st.sb["rcp_sa"] = st.c["sa"]

                @block.gpsimd
                def _(gq, h=h):
                    st.w(gq, "sa")
                    st.inc(gq.dma_start(out=ap[f"s_rcp{h}"], in_=rcp[:]), "so", 16)
    full_barrier(st)


def phase_1c(st):
    nc, ap, block = st.nc, st.ap, st.block
    from contextlib import ExitStack
    with ExitStack() as ctx:
        wo = ctx.enter_context(nc.sbuf_tensor("c_wo", [128, NCH * D], BF16))
        oT = ctx.enter_context(nc.sbuf_tensor("c_oT", [128, NCH * W], BF16))
        on = ctx.enter_context(nc.sbuf_tensor("c_on", [128, NCH * W], BF16))
        x0b = ctx.enter_context(nc.sbuf_tensor("c_x0", [128, NCH * W], F32))
        x1b = ctx.enter_context(nc.sbuf_tensor("c_x1", [128, NCH * W], F32))
        rcr = ctx.enter_context(nc.sbuf_tensor("c_rc", [1, NH * W], F32))
        rcb = ctx.enter_context(nc.sbuf_tensor("c_rcb", [1, NH * W], BF16))
        bc = ctx.enter_context(nc.psum_tensor("c_bc", [128, W], F32))
        pa = ctx.enter_context(nc.psum_tensor("c_pa", [128, W], F32))

        @block.gpsimd
        def _(g):
            st.inc(g.dma_start(out=wo[:].rearrange("p (c d) -> p c d", c=NCH), in_=ap["woa"].rearrange("(c p) d -> p c d", p=128)), "sg", 16)

        for tb in range(NTB):
            @block.sync
            def _(s, tb=tb):
                st.w(s, "sv", "so")
                st.inc(s.dma_start(out=oT[:].rearrange("p (h t) -> p h t", h=NH),
                                   in_=ap["s_oT"][:, :, tb * W:(tb + 1) * W]
                                   .rearrange("h p t -> p h t")), "sl", 16)
                st.inc(s.dma_start(out=x0b[:].rearrange("p (c t) -> p c t", c=NCH), in_=ap["x0"][:, :, tb * W:(tb + 1) * W]
                                   .rearrange("c p t -> p c t")), "sl", 16)
                for h in range(NH):
                    st.inc(s.dma_start(out=rcr[0:1, h * W:(h + 1) * W],
                                       in_=ap[f"s_rcp{h}"][0:1, tb * W:(tb + 1) * W]), "sl", 16)

            @block.vector
            def _(v, tb=tb):
                st.w(v, "sl", "sg", "sq")
                st.inc(v.tensor_copy(rcb[:], rcr[:]), "sv", 1)

            for h in range(NH):
                @block.tensor
                def _(t, h=h):
                    st.w(t, "sv")
                    st.inc(t.matmul(bc[:], st.sb["ones_row"][0:1, :], rcb[0:1, h * W:(h + 1) * W],
                                    start=True, stop=True), "sp", 1)

                @block.vector
                def _(v, h=h):
                    st.w(v, "sp")
                    st.inc(v.tensor_tensor(on[:, h * W:(h + 1) * W], oT[:, h * W:(h + 1) * W],
                                           bc[:], ALU.mult), "sv", 1)

            for c in range(NCH):
                @block.tensor
                def _(t, c=c):
                    st.w(t, "sv")
                    for hc in range(NCH):
                        mm = t.matmul(pa[:], wo[:, hc * D + c * 128: hc * D + c * 128 + 128],
                                      on[:, hc * W:(hc + 1) * W], start=(hc == 0), stop=(hc == NCH - 1))
                    st.inc(mm, "sp", 1)

                @block.vector
                def _(v, c=c):
                    st.w(v, "sp")
                    st.inc(v.tensor_tensor(x1b[:, c * W:(c + 1) * W], pa[:],
                                           x0b[:, c * W:(c + 1) * W], ALU.add), "sv", 1)

            @block.gpsimd
            def _(g2, tb=tb):
                st.w(g2, "sv")
                st.inc(g2.dma_start(out=ap["s_x1"][:, :, tb * W:(tb + 1) * W].rearrange("c p t -> p c t"), in_=x1b[:].rearrange("p (c t) -> p c t", c=NCH)), "so", 16)
    full_barrier(st)


def phase_ffn(st, src_name, dst_name):
    nc, ap, block = st.nc, st.ap, st.block
    pfx = "f2_" if dst_name == "y" else "f_"
    if dst_name == "s_x2" and DBG.get("ffn") and "dbgr_t" not in st.sb:
        st.sb["dbgr_t"] = nc.alloc_sbuf_tensor("dbgr_t", [32, 9 * W], BF16)
    from contextlib import ExitStack
    with ExitStack() as ctx:
        gwsb = ctx.enter_context(nc.sbuf_tensor(pfx + "gw", [128, NCH * E], BF16))
        w1e = ctx.enter_context(nc.sbuf_tensor(pfx + "w1e", [128, NCH * D], BF16))
        w2e = ctx.enter_context(nc.sbuf_tensor(pfx + "w2e", [128, NCH * D], BF16))
        xf = ctx.enter_context(nc.sbuf_tensor(pfx + "xf", [128, NCH * W], F32))
        xb = ctx.enter_context(nc.sbuf_tensor(pfx + "xb", [128, NCH * W], BF16))
        xh = ctx.enter_context(nc.sbuf_tensor(pfx + "xh", [128, NCH * W], BF16))
        hsb = ctx.enter_context(nc.sbuf_tensor(pfx + "h", [128, NCH * W], BF16))
        hn = ctx.enter_context(nc.sbuf_tensor(pfx + "hn", [128, NCH * W], BF16))
        zsb = ctx.enter_context(nc.sbuf_tensor(pfx + "z", [128, NCH * W], BF16))
        gs = ctx.enter_context(nc.sbuf_tensor(pfx + "gs", [32, 8 * W], F32))
        wr = ctx.enter_context(nc.sbuf_tensor(pfx + "wr", [32, 4 * W], F32))
        wrb = ctx.enter_context(nc.sbuf_tensor(pfx + "wrb", [1, 4 * W], BF16))
        er = ctx.enter_context(nc.sbuf_tensor(pfx + "er", [1, 6 * W], F32))
        erb = ctx.enter_context(nc.sbuf_tensor(pfx + "erb", [1, 4 * W], BF16))
        acc = ctx.enter_context(nc.sbuf_tensor(pfx + "acc", [128, W], F32))
        xout = ctx.enter_context(nc.sbuf_tensor(pfx + "xo", [128, NCH * W], F32))
        lnb = mk_ln_buffers(st, nc, pfx, ctx)
        ps1 = ctx.enter_context(nc.psum_tensor(pfx + "p1", [128, W], F32))
        ps2 = ctx.enter_context(nc.psum_tensor(pfx + "p2", [128, W], F32))
        psg = ctx.enter_context(nc.psum_tensor(pfx + "pg", [4, W], F32))
        psbe = ctx.enter_context(nc.psum_tensor(pfx + "pbe", [128, W], F32))

        @block.gpsimd
        def _(g):
            st.inc(g.dma_start(out=gwsb[:].rearrange("p (c e) -> p c e", c=NCH), in_=ap["gw"].rearrange("(c p) e -> p c e", p=128)), "sg", 16)

        @block.vector
        def _(v):
            v.memset(gs[:], 0.0)
            st.inc(v.memset(wr[:], 0.0), "sv", 1)

        for tb in range(NTB):
            @block.sync
            def _(s, tb=tb):
                st.w(s, "sv", "so", "sa")
                st.inc(s.dma_start(out=xf[:].rearrange("p (c t) -> p c t", c=NCH), in_=ap[src_name][:, :, tb * W:(tb + 1) * W]
                                   .rearrange("c p t -> p c t")), "sl", 16)

            @block.gpsimd
            def _(g, tb=tb):
                st.w(g, "sv")
                st.inc(g.dma_start(out=xb[:].rearrange("p (c t) -> p c t", c=NCH), in_=ap[src_name][:, :, tb * W:(tb + 1) * W]
                                   .rearrange("c p t -> p c t")), "sg", 16)

            ln_block(st, lnb, xf[:], xb[:], xh[:])
            @block.tensor
            def _(t):
                st.w(t, "sv")
                for c in range(NCH):
                    mm = t.matmul(psg[:], gwsb[:, c * E:(c + 1) * E], xh[:, c * W:(c + 1) * W],
                                  start=(c == 0), stop=(c == NCH - 1))
                st.inc(mm, "sp", 1)

            sgr = gs[:, 0:W]
            t1 = gs[:, W:2 * W]
            t2 = gs[:, 2 * W:3 * W]
            t3 = gs[:, 3 * W:4 * W]
            m1 = gs[:, 4 * W:5 * W]
            m2 = gs[:, 5 * W:6 * W]
            ind = gs[:, 6 * W:7 * W]
            wun = gs[:, 7 * W:8 * W]

            @block.scalar
            def _(a):
                st.w(a, "sp")
                st.inc(a.activation(sgr[0:4, :] if hasattr(sgr, "shape") else sgr, psg[:], AF.Identity, bias=st.sb["gb"][:, 0:1]), "sa", 1)

            @block.vector
            def _(v):
                st.w(v, "sa")
                v.stream_shuffle(t1, sgr, ROT)
                v.stream_shuffle(t2, t1, ROT)
                v.stream_shuffle(t3, t2, ROT)
                v.tensor_tensor(m1, sgr, t1, ALU.max)
                v.tensor_tensor(m2, t2, t3, ALU.max)
                v.tensor_tensor(wun, sgr, t1, ALU.min)
                v.tensor_tensor(ind, t2, t3, ALU.min)
                v.tensor_tensor(t1, m1, m2, ALU.min)
                v.tensor_tensor(m1, m1, m2, ALU.max)
                v.tensor_tensor(t2, wun, ind, ALU.max)
                v.tensor_tensor(m2, t1, t2, ALU.max)
                v.tensor_tensor(t1, sgr, m1, ALU.subtract)
                v.tensor_tensor(t2, sgr, m2, ALU.subtract)
                v.tensor_scalar(t2, t2, 1e12, 1.0, ALU.mult, ALU.add)
                st.inc(v.tensor_scalar(ind, t2, 1.0, 0.0, ALU.min, ALU.max), "sv", 1)

            @block.scalar
            def _(a):
                st.w(a, "sv")
                st.inc(a.activation(t1[0:4, :] if True else t1, t1[0:4, :] if True else t1, AF.Exp), "sa", 1)

            @block.vector
            def _(v):
                st.w(v, "sa")
                v.tensor_tensor(wun, t1, ind, ALU.mult)
                v.stream_shuffle(t2, wun, ROT)
                v.tensor_tensor(t2, wun, t2, ALU.add)
                v.stream_shuffle(t3, t2, ROT)
                v.stream_shuffle(t3, t3, ROT)
                st.inc(v.tensor_tensor(t2, t2, t3, ALU.add), "sv", 1)

            @block.scalar
            def _(a):
                st.w(a, "sv")
                a.activation(t3[0:4, :], t2[0:4, :], AF.Ln)
                st.inc(a.activation(t3[0:4, :], t3[0:4, :], AF.Exp, scale=-1.0), "sa", 1)

            @block.vector
            def _(v):
                st.w(v, "sa")
                v.tensor_tensor(sgr, wun, t3, ALU.mult)
                v.stream_shuffle(wr[:, W:2 * W], sgr, ROT)
                v.stream_shuffle(wr[:, 2 * W:3 * W], wr[:, W:2 * W], ROT)
                v.stream_shuffle(wr[:, 3 * W:4 * W], wr[:, 2 * W:3 * W], ROT)
                v.tensor_copy(wrb[0:1, 0:W], sgr[0:1, :])
                v.tensor_copy(wrb[0:1, W:2 * W], wr[0:1, W:2 * W])
                v.tensor_copy(wrb[0:1, 2 * W:3 * W], wr[0:1, 2 * W:3 * W])
                st.inc(v.tensor_copy(wrb[0:1, 3 * W:4 * W], wr[0:1, 3 * W:4 * W]), "sv", 1)

            for c in range(NCH):
                @block.tensor
                def _(t, c=c):
                    st.w(t, "sv")
                    for e in range(E):
                        mm = t.matmul(psbe[:], st.sb["be2r"][0:1, e * D + c * 128: e * D + c * 128 + 128],
                                      wrb[0:1, e * W:(e + 1) * W], start=(e == 0), stop=(e == E - 1))
                    st.inc(mm, "sp", 1)

                @block.vector
                def _(v, c=c):
                    st.w(v, "sp")
                    v.tensor_tensor(acc[:], psbe[:], xh[:, c * W:(c + 1) * W], ALU.add)
                    st.inc(v.tensor_tensor(xout[:, c * W:(c + 1) * W], acc[:],
                                           xf[:, c * W:(c + 1) * W], ALU.add), "sv", 1)

            if dst_name == "s_x2" and tb == 0 and DBG.get("ffn"):
                @block.vector
                def _(v):
                    st.w(v, "sv", "sa")
                    dbg = st.sb["dbgr_t"][:]
                    v.tensor_copy(dbg[:, 0 * W:1 * W], sgr)
                    v.tensor_copy(dbg[:, 1 * W:2 * W], m1)
                    v.tensor_copy(dbg[:, 2 * W:3 * W], m2)
                    v.tensor_copy(dbg[:, 3 * W:4 * W], ind)
                    v.tensor_copy(dbg[:, 4 * W:5 * W], wun)
                    st.inc(v.tensor_copy(dbg[0:1, 0:1], dbg[0:1, 0:1]), "sv", 1)

            for e in range(E):
                @block.sync
                def _(s, e=e):
                    st.w(s, "sp", "sa")
                    st.inc(s.dma_start(out=w1e[:].rearrange("p (c d) -> p c d", c=NCH), in_=ap["s_w1c"][e].rearrange("c p d -> p c d")), "sl", 16)
                    st.inc(s.dma_start(out=w2e[:].rearrange("p (c d) -> p c d", c=NCH), in_=ap["s_w2c"][e].rearrange("c p d -> p c d")), "sl", 16)

                for dj in range(NCH):
                    pse = ps1 if dj % 2 == 0 else ps2

                    @block.tensor
                    def _(t, e=e, dj=dj, pse=pse):
                        st.w(t, "sl", "sa")
                        for c in range(NCH):
                            mm = t.matmul(pse[:], w1e[:, c * D + dj * 128: c * D + dj * 128 + 128],
                                          xh[:, c * W:(c + 1) * W], start=(c == 0), stop=(c == NCH - 1))
                        st.inc(mm, "sp", 1)

                    @block.scalar
                    def _(a, e=e, dj=dj, pse=pse):
                        st.w(a, "sp")
                        st.inc(a.activation(hsb[:, dj * W:(dj + 1) * W], pse[:], AF.Identity,
                                            bias=st.sb["b1c"][:, e * NCH + dj: e * NCH + dj + 1]), "sa", 1)

                @block.vector
                def _(v, e=e):
                    st.w(v, "sa")
                    st.inc(v.tensor_tensor(lnb["sq"], hsb[:], hsb[:], ALU.mult), "sv", 1)

                @block.tensor
                def _(t, e=e):
                    st.w(t, "sv")
                    for c in range(NCH):
                        mm = t.matmul(lnb["psA"], st.sb["ones_col"][:], lnb["sq"][:, c * W:(c + 1) * W],
                                      start=(c == 0), stop=(c == NCH - 1))
                    st.inc(mm, "sp", 1)

                rst1 = er[0:1, 0:W]

                @block.vector
                def _(v):
                    st.w(v, "sp")
                    st.inc(v.tensor_scalar(rst1, lnb["psA"][0:1, :], 1.0 / D, None, ALU.mult), "sv", 1)

                @block.scalar
                def _(a):
                    st.w(a, "sv")
                    a.activation(rst1, rst1, AF.Ln, bias=st.sb["eps"][0:1, 0:1])
                    st.inc(a.activation(rst1, rst1, AF.Exp, scale=-0.5), "sa", 1)

                @block.vector
                def _(v):
                    st.w(v, "sa")
                    st.inc(v.tensor_copy(erb[0:1, 0:W], rst1), "sv", 1)

                for dj in range(NCH):
                    @block.tensor
                    def _(t, e=e, dj=dj):
                        st.w(t, "sv")
                        st.inc(t.matmul(lnb["ps_bc"], st.sb["g1r"][0:1, e * D + dj * 128: e * D + dj * 128 + 128],
                                        erb[0:1, 0:W], start=True, stop=True), "sp", 1)

                    @block.vector
                    def _(v, e=e, dj=dj):
                        st.w(v, "sp")
                        v.tensor_tensor(hn[:, dj * W:(dj + 1) * W], hsb[:, dj * W:(dj + 1) * W],
                                        lnb["ps_bc"], ALU.mult)
                        st.inc(v.tensor_scalar(hn[:, dj * W:(dj + 1) * W], hn[:, dj * W:(dj + 1) * W],
                                               st.sb["be1c"][:, e * NCH + dj: e * NCH + dj + 1], 0.0,
                                               ALU.add, ALU.max), "sv", 1)

                for dj in range(NCH):
                    pse = ps1 if dj % 2 == 0 else ps2

                    @block.tensor
                    def _(t, e=e, dj=dj, pse=pse):
                        st.w(t, "sv", "sa")
                        for c in range(NCH):
                            mm = t.matmul(pse[:], w2e[:, c * D + dj * 128: c * D + dj * 128 + 128],
                                          hn[:, c * W:(c + 1) * W], start=(c == 0), stop=(c == NCH - 1))
                        st.inc(mm, "sp", 1)

                    @block.scalar
                    def _(a, e=e, dj=dj, pse=pse):
                        st.w(a, "sp")
                        st.inc(a.activation(zsb[:, dj * W:(dj + 1) * W],
                                            pse[:], AF.Identity,
                                            bias=st.sb["b2c"][:, e * NCH + dj: e * NCH + dj + 1]), "sa", 1)

                @block.vector
                def _(v, e=e):
                    st.w(v, "sa")
                    st.inc(v.tensor_tensor(lnb["sq"], zsb[:],
                                           zsb[:], ALU.mult), "sv", 1)

                @block.tensor
                def _(t, e=e):
                    st.w(t, "sv")
                    for c in range(NCH):
                        mm = t.matmul(lnb["psB"], st.sb["ones_col"][:], lnb["sq"][:, c * W:(c + 1) * W],
                                      start=(c == 0), stop=(c == NCH - 1))
                    st.inc(mm, "sp", 1)

                rst2 = er[0:1, (1 + e) * W:(2 + e) * W]

                @block.vector
                def _(v, rst2=rst2, e=e, tb=tb):
                    st.w(v, "sp")
                    if dst_name == "s_x2" and tb == 0 and e == 0 and DBG.get("ffn"):
                        v.tensor_copy(st.sb["dbgr_t"][:][0:1, 4 * W:5 * W].bitcast(F32)[0:1, 0:W // 2] if False else st.sb["dbgr_t"][:][0:1, 4 * W:5 * W], lnb["psB"][0:1, :])
                    st.inc(v.tensor_scalar(rst2, lnb["psB"][0:1, :], 1.0 / D, None, ALU.mult), "sv", 1)

                @block.scalar
                def _(a, rst2=rst2):
                    st.w(a, "sv")
                    a.activation(rst2, rst2, AF.Ln, bias=st.sb["eps"][0:1, 0:1])
                    st.inc(a.activation(rst2, rst2, AF.Exp, scale=-0.5), "sa", 1)

                @block.vector
                def _(v, e=e, rst2=rst2):
                    st.w(v, "sa")
                    wrow = sgr[0:1, :] if e == 0 else wr[0:1, e * W:(e + 1) * W]
                    v.tensor_tensor(er[0:1, 5 * W:6 * W], wrow, rst2, ALU.mult)
                    st.inc(v.tensor_copy(erb[0:1, 0:W], er[0:1, 5 * W:6 * W]), "sv", 1)

                for c in range(NCH):
                    @block.tensor
                    def _(t, e=e, c=c):
                        st.w(t, "sv")
                        st.inc(t.matmul(lnb["ps_bc"], st.sb["g2r"][0:1, e * D + c * 128: e * D + c * 128 + 128],
                                        erb[0:1, 0:W], start=True, stop=True), "sp", 1)

                    @block.vector
                    def _(v, c=c):
                        st.w(v, "sp")
                        v.tensor_tensor(acc[:], zsb[:, c * W:(c + 1) * W], lnb["ps_bc"], ALU.mult)
                        st.inc(v.tensor_tensor(xout[:, c * W:(c + 1) * W], xout[:, c * W:(c + 1) * W],
                                               acc[:], ALU.add), "sv", 1)


            @block.gpsimd
            def _(g2, tb=tb):
                st.w(g2, "sv")
                st.inc(g2.dma_start(out=ap[dst_name][:, :, tb * W:(tb + 1) * W].rearrange("c p t -> p c t"), in_=xout[:].rearrange("p (c t) -> p c t", c=NCH)), "so", 16)

            if dst_name == "s_x2" and tb == 0 and DBG.get("ffn"):
                @block.vector
                def _(v):
                    st.w(v, "sv")
                    dbg = st.sb["dbgr_t"][:]
                    v.tensor_copy(dbg[0:1, 5 * W:6 * W], er[0:1, 1 * W:2 * W])
                    v.tensor_copy(dbg[0:1, 6 * W:7 * W], er[0:1, 2 * W:3 * W])
                    v.tensor_copy(dbg[0:1, 7 * W:8 * W], er[0:1, 3 * W:4 * W])
                    st.inc(v.tensor_copy(dbg[0:1, 8 * W:9 * W], er[0:1, 4 * W:5 * W]), "sv", 1)

                @block.gpsimd
                def _(g3):
                    st.w(g3, "sv")
                    st.inc(g3.dma_start(out=ap["s_dbgR"][:, 0:9 * W], in_=st.sb["dbgr_t"][:]), "so", 16)
                    dw = DBG.get("dump_what", "xout")
                    buf = {"xout": xout, "z0": zsb, "h0": hsb, "hn0": hn, "xh0": xh, "sq0": lnb and None}.get(dw)
                    if dw == "z0":
                        st.inc(g3.dma_start(out=ap["s_dbgA"][:, 0:NCH * W], in_=zsb[:]), "so", 16)
                    elif dw == "h0":
                        st.inc(g3.dma_start(out=ap["s_dbgA"][:, 0:NCH * W], in_=hsb[:]), "so", 16)
                    elif dw == "hn0":
                        st.inc(g3.dma_start(out=ap["s_dbgA"][:, 0:NCH * W], in_=hn[:]), "so", 16)
                    elif dw == "xh0":
                        st.inc(g3.dma_start(out=ap["s_dbgA"][:, 0:NCH * W], in_=xh[:]), "so", 16)
                    else:
                        st.inc(g3.dma_start(out=ap["s_dbgA"][:, 0:NCH * W], in_=xout[:]), "so", 16)
    full_barrier(st)


def phase_3(st):
    nc, ap, block = st.nc, st.ap, st.block
    from contextlib import ExitStack

    with ExitStack() as ctx:
        xf = ctx.enter_context(nc.sbuf_tensor("d_xf", [128, NCH * W], F32))
        xb = ctx.enter_context(nc.sbuf_tensor("d_xb", [128, NCH * W], BF16))
        xh = ctx.enter_context(nc.sbuf_tensor("d_xh", [128, NCH * W], BF16))
        lnb = mk_ln_buffers(st, nc, "d", ctx)
        for tb in range(NTB):
            @block.sync
            def _(s, tb=tb):
                st.w(s, "sv", "so")
                st.inc(s.dma_start(out=xf[:].rearrange("p (c t) -> p c t", c=NCH), in_=ap["s_x2"][:, :, tb * W:(tb + 1) * W]
                                   .rearrange("c p t -> p c t")), "sl", 16)

            @block.gpsimd
            def _(g, tb=tb):
                st.w(g, "sv")
                st.inc(g.dma_start(out=xb[:].rearrange("p (c t) -> p c t", c=NCH), in_=ap["s_x2"][:, :, tb * W:(tb + 1) * W]
                                   .rearrange("c p t -> p c t")), "sg", 16)

            ln_block(st, lnb, xf[:], xb[:], xh[:])

            @block.gpsimd
            def _(g, tb=tb):
                st.w(g, "sv")
                st.inc(g.dma_start(out=ap["s_xh3"][:, :, tb * W:(tb + 1) * W].rearrange("c p t -> p c t"), in_=xh[:].rearrange("p (c t) -> p c t", c=NCH)), "so", 16)
    full_barrier(st)

    with ExitStack() as ctx:
        wsb = ctx.enter_context(nc.sbuf_tensor("e_w", [128, NCH * D], BF16))
        xhs = ctx.enter_context(nc.sbuf_tensor("e_xh", [128, NCH * (W + 2)], BF16))
        mix = ctx.enter_context(nc.sbuf_tensor("e_mx", [128, 4 * W], BF16))
        evac = ctx.enter_context(nc.sbuf_tensor("e_ev", [128, W], BF16))
        ps1 = ctx.enter_context(nc.psum_tensor("e_p1", [128, W], F32))
        ps2 = ctx.enter_context(nc.psum_tensor("e_p2", [128, W], F32))

        def mix_pass(w_name, mixc_name, out_name, act):
            @block.gpsimd
            def _(g):
                st.w(g, "sp", "sa", "so")
                st.inc(g.dma_start(out=wsb[:].rearrange("p (c d) -> p c d", c=NCH), in_=ap[w_name].rearrange("(c p) d -> p c d", p=128)), "sg", 16)

            for tb in range(NTB):
                lo = max(0, tb * W - 1)
                hi = min(T, (tb + 1) * W + 1)
                npad_l = 1 if tb == 0 else 0
                xv = xhs[:].rearrange("p (c w) -> p c w", c=NCH)

                @block.sync
                def _(s, tb=tb, lo=lo, hi=hi, npad_l=npad_l, xv=xv):
                    st.w(s, "sp", "sa")
                    st.inc(s.dma_start(
                        out=xv[:, :, npad_l:npad_l + (hi - lo)],
                        in_=ap["s_xh3"][:, :, lo:hi].rearrange("c p t -> p c t")), "sl", 16)

                if tb == 0 or tb == NTB - 1:
                    @block.vector
                    def _(v, tb=tb, xv=xv):
                        st.w(v, "sl", "sp")
                        col = 0 if tb == 0 else (W + 1)
                        st.inc(v.memset(xv[:, :, col:col + 1], 0.0), "sv", 1)

                @block.vector
                def _(v, tb=tb, xv=xv, mixc_name=mixc_name):
                    st.w(v, "sl", "sp")
                    for c in range(4):
                        cur = xv[:, c, 1:1 + W]
                        sh = xv[:, c, 0:W] if c < 2 else xv[:, c, 2:2 + W]
                        dst = mix[:, c * W:(c + 1) * W]
                        v.tensor_tensor(dst, cur, sh, ALU.subtract)
                        v.tensor_scalar(dst, dst, st.sb[mixc_name][:, c:c + 1], None, ALU.mult)
                        v.tensor_tensor(dst, dst, sh, ALU.add)
                    st.inc(v.tensor_copy(mix[0:1, 0:1], mix[0:1, 0:1]), "sv", 1)

                for dj in range(NCH):
                    pse = ps1 if dj % 2 == 0 else ps2

                    @block.tensor
                    def _(t, dj=dj, pse=pse, xv=xv):
                        st.w(t, "sv", "sa", "sg")
                        for c in range(NCH):
                            rhs = mix[:, c * W:(c + 1) * W] if c < 4 else xv[:, c, 1:1 + W]
                            mm = t.matmul(pse[:], wsb[:, c * D + dj * 128: c * D + dj * 128 + 128],
                                          rhs, start=(c == 0), stop=(c == NCH - 1))
                        st.inc(mm, "sp", 1)

                    @block.scalar
                    def _(a, dj=dj, pse=pse, act=act):
                        st.w(a, "sp", "so")
                        st.inc(a.activation(evac[:], pse[:], act), "sa", 1)

                    @block.gpsimd
                    def _(g, tb=tb, dj=dj, out_name=out_name):
                        st.w(g, "sa")
                        st.inc(g.dma_start(out=ap[out_name][dj, :, tb * W:(tb + 1) * W], in_=evac[:]), "so", 16)

        mix_pass("wbk", "mixk", "s_ek", AF.Exp)
        mix_pass("wbv", "mixv", "s_vb", AF.Identity)
        mix_pass("wbr", "mixr", "s_sr", AF.Sigmoid)
    full_barrier(st)

    with ExitStack() as ctx:
        ek = ctx.enter_context(nc.sbuf_tensor("s3_ek", [128, T], BF16))
        vb = ctx.enter_context(nc.sbuf_tensor("s3_vb", [128, T], BF16))
        sr = ctx.enter_context(nc.sbuf_tensor("s3_sr", [128, T], BF16))
        eku = ctx.enter_context(nc.sbuf_tensor("s3_eku", [128, T], BF16))
        ekv = ctx.enter_context(nc.sbuf_tensor("s3_ekv", [128, T], BF16))
        asc = ctx.enter_context(nc.sbuf_tensor("s3_a", [128, T + 1], F32))
        bsc = ctx.enter_context(nc.sbuf_tensor("s3_b", [128, T + 1], F32))
        wkv = ctx.enter_context(nc.sbuf_tensor("s3_w", [128, T], BF16))
        for cj in range(NCH):
            @block.sync
            def _(s, cj=cj):
                st.w(s, "so", "sv")
                st.inc(s.dma_start(out=ek[:], in_=ap["s_ek"][cj]), "sl", 16)
                st.inc(s.dma_start(out=vb[:], in_=ap["s_vb"][cj]), "sl", 16)
                st.inc(s.dma_start(out=sr[:], in_=ap["s_sr"][cj]), "sl", 16)

            @block.vector
            def _(v, cj=cj):
                st.w(v, "sl")
                v.tensor_scalar(eku[:], ek[:], st.sb["eu"][:, cj:cj + 1], None, ALU.mult)
                v.tensor_tensor(ekv[:], ek[:], vb[:], ALU.mult)
                v.memset(asc[:, 0:1], 0.0)
                v.memset(bsc[:, 0:1], 0.0)
                v.tensor_tensor_scan(asc[:, 1:T + 1], st.sb["ew"][:, cj:cj + 1].broadcast_to([128, T]),
                                     ekv[:], 0.0, ALU.mult, ALU.add)
                v.tensor_tensor_scan(bsc[:, 1:T + 1], st.sb["ew"][:, cj:cj + 1].broadcast_to([128, T]),
                                     ek[:], 0.0, ALU.mult, ALU.add)
                v.tensor_tensor(ekv[:], eku[:], vb[:], ALU.mult)
                v.tensor_tensor(asc[:, 0:T], asc[:, 0:T], ekv[:], ALU.add)
                st.inc(v.tensor_tensor(bsc[:, 0:T], bsc[:, 0:T], eku[:], ALU.add), "sv", 1)

            @block.scalar
            def _(a, cj=cj):
                st.w(a, "sa", "sv")
                a.activation(bsc[:, 0:T], bsc[:, 0:T], AF.Ln)
                st.inc(a.activation(bsc[:, 0:T], bsc[:, 0:T], AF.Exp, scale=-1.0), "sa", 1)

            @block.vector
            def _(v, cj=cj):
                st.w(v, "sa")
                v.tensor_tensor(asc[:, 0:T], asc[:, 0:T], bsc[:, 0:T], ALU.mult)
                st.inc(v.tensor_tensor(wkv[:], asc[:, 0:T], sr[:], ALU.mult), "sv", 1)

            @block.gpsimd
            def _(g, cj=cj):
                st.w(g, "sv")
                st.inc(g.dma_start(out=ap["s_srw"][cj], in_=wkv[:]), "so", 16)
    full_barrier(st)

    with ExitStack() as ctx:
        wo = ctx.enter_context(nc.sbuf_tensor("g3_wo", [128, NCH * D], BF16))
        sw = ctx.enter_context(nc.sbuf_tensor("g3_sw", [128, NCH * W], BF16))
        x2b = ctx.enter_context(nc.sbuf_tensor("g3_x2", [128, NCH * W], F32))
        x3b = ctx.enter_context(nc.sbuf_tensor("g3_x3", [128, NCH * W], F32))
        pa = ctx.enter_context(nc.psum_tensor("g3_pa", [128, W], F32))

        @block.gpsimd
        def _(g):
            st.w(g, "so")
            st.inc(g.dma_start(out=wo[:].rearrange("p (c d) -> p c d", c=NCH), in_=ap["wbo"].rearrange("(c p) d -> p c d", p=128)), "sg", 16)

        for tb in range(NTB):
            @block.sync
            def _(s, tb=tb):
                st.w(s, "sv", "so")
                st.inc(s.dma_start(out=sw[:].rearrange("p (c t) -> p c t", c=NCH), in_=ap["s_srw"][:, :, tb * W:(tb + 1) * W]
                                   .rearrange("c p t -> p c t")), "sl", 16)
                st.inc(s.dma_start(out=x2b[:].rearrange("p (c t) -> p c t", c=NCH), in_=ap["s_x2"][:, :, tb * W:(tb + 1) * W]
                                   .rearrange("c p t -> p c t")), "sl", 16)

            for c in range(NCH):
                @block.tensor
                def _(t, c=c):
                    st.w(t, "sl", "sv", "sg")
                    for ic in range(NCH):
                        mm = t.matmul(pa[:], wo[:, ic * D + c * 128: ic * D + c * 128 + 128],
                                      sw[:, ic * W:(ic + 1) * W], start=(ic == 0), stop=(ic == NCH - 1))
                    st.inc(mm, "sp", 1)

                @block.vector
                def _(v, c=c):
                    st.w(v, "sp")
                    st.inc(v.tensor_tensor(x3b[:, c * W:(c + 1) * W], pa[:],
                                           x2b[:, c * W:(c + 1) * W], ALU.add), "sv", 1)

            @block.gpsimd
            def _(g2, tb=tb):
                st.w(g2, "sv")
                st.inc(g2.dma_start(out=ap["s_x3"][:, :, tb * W:(tb + 1) * W].rearrange("c p t -> p c t"), in_=x3b[:].rearrange("p (c t) -> p c t", c=NCH)), "so", 16)
    full_barrier(st)


def phase_final(st, debug_taps):
    block = st.block

    @block.sync
    def _(s):
        st.wall(s)
        for tap in debug_taps:
            st.inc(s.dma_start(out=st.ap["dbg_" + tap], in_=st.ap[tap]), "sl", 16)
        s.wait_ge(st.s["sl"], st.c["sl"])

    @block.gpsimd
    def _(g):
        st.wall(g)



TAP_SPECS = {
    "s_x1": ([NCH, 128, T], F32), "s_x2": ([NCH, 128, T], F32), "s_x3": ([NCH, 128, T], F32),
    "s_qT": ([NH, 128, T], BF16), "s_kT": ([NKV, 128, T], BF16), "s_v": ([NKV, T, 128], BF16),
    "s_oT": ([NH, 128, T], BF16), "s_xh3": ([NCH, 128, T], BF16),
    "s_ek": ([NCH, 128, T], BF16), "s_vb": ([NCH, 128, T], BF16),
    "s_sr": ([NCH, 128, T], BF16), "s_srw": ([NCH, 128, T], BF16),
    "s_w1c": ([E, NCH, 128, D], BF16), "s_w2c": ([E, NCH, 128, D], BF16),
    "s_dbgR": ([32, 12 * W], BF16), "s_dbgA": ([128, 8 * W], F32),
}


def build_nc(debug_taps=(), phases=("pre", "1a", "1b", "1c", "2", "3", "4")):
    nc = bass.Bass("TRN2", target_bir_lowering=False, debug=False)

    with (
        nc.semaphore("sl") as sl, nc.semaphore("sg") as sg, nc.semaphore("sp") as sp,
        nc.semaphore("sa") as sa, nc.semaphore("sv") as sv, nc.semaphore("so") as so,
        nc.semaphore("sq") as sq, nc.Block() as block,
    ):
        st = St(nc, block, dict(sl=sl, sg=sg, sp=sp, sa=sa, sv=sv, so=so, sq=sq))

        def inp(name, shape, dt=F32):
            st.ap[name] = nc.dram_tensor(name, shape, dt, kind="ExternalInput").ap()

        def scr(name, shape, dt):
            st.ap[name] = nc.dram_tensor(name, shape, dt, kind="Internal").ap()

        inp("x0", [NCH, 128, T])
        inp("wq", [D, D]); inp("wqp", [D, D])
        inp("wk", [D, 512]); inp("wkp", [D, 512]); inp("wv", [D, 512])
        inp("woa", [D, D])
        inp("wbk", [D, D]); inp("wbv", [D, D]); inp("wbr", [D, D]); inp("wbo", [D, D])
        inp("mixk_c", [128, NCH]); inp("mixv_c", [128, NCH]); inp("mixr_c", [128, NCH])
        inp("sd_c", [128, NCH]); inp("sf_c", [128, NCH])
        inp("w1", [E, D, D]); inp("w2", [E, D, D])
        inp("b1r", [E, 1, D]); inp("b2r", [E, 1, D])
        inp("b1_cc", [E, 128, NCH]); inp("b2_cc", [E, 128, NCH])
        inp("be1_cc", [128, NCH * E])
        inp("g1_r", [1, E * D]); inp("g2_r", [1, E * D]); inp("be2_r", [1, E * D])
        inp("gw", [D, E]); inp("gb_col", [E, 1]); inp("eps11", [1, 1])
        inp("cosT", [128, T], BF16); inp("sinTs", [128, T], BF16)
        inp("masks", [4, 128, W], BF16)
        inp("ones_col", [128, 1], BF16); inp("ones_row", [1, 128], BF16)

        st.ap["y"] = nc.dram_tensor("y", [NCH, 128, T], F32, kind="ExternalOutput").ap()

        scr("s_qT", [NH, 128, T], BF16); scr("s_kT", [NKV, 128, T], BF16)
        scr("s_v", [NKV, T, 128], BF16); scr("s_oT", [NH, 128, T], BF16)
        for h in range(NH):
            scr(f"s_rcp{h}", [1, T], F32)
        scr("s_x1", [NCH, 128, T], F32); scr("s_x2", [NCH, 128, T], F32)
        scr("s_x3", [NCH, 128, T], F32); scr("s_xh3", [NCH, 128, T], BF16)
        scr("s_w1c", [E, NCH, 128, D], BF16); scr("s_w2c", [E, NCH, 128, D], BF16)
        scr("s_dbgR", [32, 12 * W], BF16); scr("s_dbgA", [128, 8 * W], F32)
        scr("s_ek", [NCH, 128, T], BF16); scr("s_vb", [NCH, 128, T], BF16)
        scr("s_sr", [NCH, 128, T], BF16); scr("s_srw", [NCH, 128, T], BF16)

        for tap in debug_taps:
            shp, dt = TAP_SPECS[tap]
            st.ap["dbg_" + tap] = nc.dram_tensor("dbg_" + tap, shp, dt, kind="ExternalOutput").ap()

        if "pre" in phases:
            phase_preamble(st)
        if "1a" in phases:
            phase_1a(st)
        if "1b" in phases:
            phase_1b(st)
        if "1c" in phases:
            phase_1c(st)
        if "2" in phases:
            phase_ffn(st, "s_x1", "s_x2")
        if "3" in phases:
            phase_3(st)
        if "4" in phases:
            phase_ffn(st, "s_x3", "y")
        phase_final(st, debug_taps)
    return nc


def stage_inputs(inputs):
    x = np.asarray(inputs["L0_attn_a_hidden_states"], np.float32)
    wq = np.asarray(inputs["L0_attn_a_Wq"], np.float32)
    wk = np.asarray(inputs["L0_attn_a_Wk"], np.float32)
    wv = np.asarray(inputs["L0_attn_a_Wv"], np.float32)
    woa = np.asarray(inputs["L0_attn_a_Wo"], np.float32)
    bk = np.asarray(inputs["L0_attn_b_Wk"], np.float32)
    bv = np.asarray(inputs["L0_attn_b_Wv"], np.float32)
    br = np.asarray(inputs["L0_attn_b_Wr"], np.float32)
    bo = np.asarray(inputs["L0_attn_b_Wo"], np.float32)
    sd = np.asarray(inputs["L0_attn_b_spatial_decay"], np.float32)
    sf = np.asarray(inputs["L0_attn_b_spatial_first"], np.float32)
    mk = np.asarray(inputs["L0_attn_b_mix_k"], np.float32)
    mv = np.asarray(inputs["L0_attn_b_mix_v"], np.float32)
    mr = np.asarray(inputs["L0_attn_b_mix_r"], np.float32)
    gw = np.asarray(inputs["L0_ffn_gate_W"], np.float32)
    gb = np.asarray(inputs["L0_ffn_gate_b"], np.float32)
    w1 = np.asarray(inputs["L0_ffn_W1"], np.float32)
    b1 = np.asarray(inputs["L0_ffn_b1"], np.float32)
    g1 = np.asarray(inputs["L0_ffn_g1"], np.float32)
    be1 = np.asarray(inputs["L0_ffn_be1"], np.float32)
    w2 = np.asarray(inputs["L0_ffn_W2"], np.float32)
    b2 = np.asarray(inputs["L0_ffn_b2"], np.float32)
    g2 = np.asarray(inputs["L0_ffn_g2"], np.float32)
    be2 = np.asarray(inputs["L0_ffn_be2"], np.float32)

    cosT, sinTs = _rope_tables()
    common = {
        "wq": np.ascontiguousarray(wq), "wqp": _perm_cols(wq),
        "wk": np.ascontiguousarray(wk), "wkp": _perm_cols(wk),
        "wv": np.ascontiguousarray(wv), "woa": np.ascontiguousarray(woa),
        "wbk": np.ascontiguousarray(bk.T), "wbv": np.ascontiguousarray(bv.T),
        "wbr": np.ascontiguousarray(br.T), "wbo": np.ascontiguousarray(bo.T),
        "mixk_c": _cols(mk), "mixv_c": _cols(mv), "mixr_c": _cols(mr),
        "sd_c": _cols(sd), "sf_c": _cols(sf),
        "w1": np.ascontiguousarray(w1), "w2": np.ascontiguousarray(w2),
        "b1r": np.ascontiguousarray(b1.reshape(E, 1, D)),
        "b2r": np.ascontiguousarray(b2.reshape(E, 1, D)),
        "b1_cc": np.stack([_cols(b1[e]) for e in range(E)]),
        "b2_cc": np.stack([_cols(b2[e]) for e in range(E)]),
        "be1_cc": np.concatenate([_cols(be1[e]) for e in range(E)], axis=1),
        "g1_r": np.ascontiguousarray(g1.reshape(1, E * D)),
        "g2_r": np.ascontiguousarray(g2.reshape(1, E * D)),
        "be2_r": np.ascontiguousarray(be2.reshape(1, E * D)),
        "gw": np.ascontiguousarray(gw), "gb_col": np.ascontiguousarray(gb.reshape(E, 1)),
        "eps11": np.full((1, 1), LN_EPS, np.float32),
        "cosT": cosT, "sinTs": sinTs, "masks": _masks(),
        "ones_col": np.ones((128, 1), BF), "ones_row": np.ones((1, 128), BF),
    }
    in_maps = []
    for c in range(8):
        m = dict(common)
        m["x0"] = np.ascontiguousarray(x[c // 2].T.reshape(NCH, 128, T))
        in_maps.append(m)
    return in_maps


def kernel(**inputs):
    in_maps = stage_inputs(inputs)
    nc = build_nc()
    res = run_bass_kernel_spmd(nc, in_maps, core_ids=list(range(8)))
    out = np.empty((B, T, D), np.float32)
    for b in range(B):
        out[b] = res.results[2 * b]["y"].reshape(D, T).T
    return out


# version 03 (final)
# speedup vs baseline: 1.3067x; 1.3004x over previous
import math
import numpy as np
import ml_dtypes
import concourse.bass as bass
import concourse.mybir as mybir
from concourse.bass_utils import run_bass_kernel_spmd

F32 = mybir.dt.float32
BF16 = mybir.dt.bfloat16
AF = mybir.ActivationFunctionType
ALU = mybir.AluOpType
AXX = mybir.AxisListType.X

B = 4
T = 8192
D = 1024
NH = 8
NKV = 4
GR = NH // NKV
E = 4
W = 512
NTB = T // W
NCH = D // 128
LN_EPS = 1e-5
THETA = 10000.0
BF = ml_dtypes.bfloat16

DBG = {}
ROT = [1, 2, 3, 0] + list(range(4, 32))

def r3(dst2d, n_outer, src3d):
    return dst2d.rearrange("p (a b) -> p a b", a=n_outer), src3d



class DryInstr:
    def then_inc(self, sem, n):
        return self


_DRY_INSTR = DryInstr()


class DryEngine:

    def __getattr__(self, name):
        def f(*a, **k):
            return _DRY_INSTR
        return f


_DRY = DryEngine()

ENGS = ("sync", "gpsimd", "tensor", "scalar", "vector")


class Blk:

    def __init__(self, st):
        self.st = st
        self.real_name = None
        self.real_eng = None

    def _route(self, name, f):
        if self.real_name is None:
            getattr(self.st.block, name)(f)
        elif name == self.real_name:
            f(self.real_eng)
        else:
            f(_DRY)

    def sync(self, f):
        self._route("sync", f)

    def gpsimd(self, f):
        self._route("gpsimd", f)

    def tensor(self, f):
        self._route("tensor", f)

    def scalar(self, f):
        self._route("scalar", f)

    def vector(self, f):
        self._route("vector", f)


class St:

    def __init__(self, nc, block, sems):
        self.nc = nc
        self.block = block
        self.blk = Blk(self)
        self.s = sems
        self.c = {k: 0 for k in sems}
        self.ap = {}
        self.sb = {}

    def inc(self, instr, name, n):
        instr.then_inc(self.s[name], n)
        self.c[name] += n

    def w(self, eng, *names):
        for n in names:
            if self.c[n] > 0:
                eng.wait_ge(self.s[n], self.c[n])

    def wall(self, eng):
        self.w(eng, "sl", "sg", "sp", "sa", "sv", "so", "sq")

    def run_phase(self, emit_fn):
        snap = dict(self.c)
        for engname in ENGS:
            self.c = dict(snap)
            self.blk.real_name = engname

            def body(e, emit_fn=emit_fn):
                self.blk.real_eng = e
                emit_fn()
            getattr(self.block, engname)(body)
        self.blk.real_name = None


def full_barrier(st):
    def emit():
        for engname in ENGS:
            def mk(e, st=st):
                st.wall(e)
            getattr(st.blk, engname)(mk)
    st.run_phase(emit)



def _rope_tables():
    j = np.arange(128) % 64
    inv_freq = 1.0 / THETA ** (2 * j / 128)
    ang = np.outer(inv_freq, np.arange(T))
    cosT = np.cos(ang).astype(BF)
    sign = np.where(np.arange(128) < 64, -1.0, 1.0)[:, None]
    sinTs = (np.sin(ang) * sign).astype(BF)
    return cosT, sinTs


def _perm_cols(w):
    d = w.shape[1]
    idx = np.arange(d)
    h = idx % 128
    rho = np.where(h < 64, idx + 64, idx - 64)
    return np.ascontiguousarray(w[:, rho])


def _masks():
    sl_ = np.arange(128)[:, None]
    ql = np.arange(W)[None, :]
    return np.stack([(sl_ <= ql + off).astype(BF) for off in (0, -128, -256, -384)])


def _cols(vec):
    return np.ascontiguousarray(np.asarray(vec, np.float32).reshape(NCH, 128).T)



def ln_block(st, buf, xf, xb, xh):
    nc, block = st.nc, st.blk
    sq, rows, rows_bf, mrt, psA, psB, ps_bc = (
        buf["sq"], buf["rows"], buf["rows_bf"], buf["mrt"], buf["psA"], buf["psB"], buf["ps_bc"])

    @block.vector
    def _(v):
        st.w(v, "sg", "sl")
        v.tensor_tensor(sq[:], xb, xb, ALU.mult)
        st.inc(v.tensor_copy(sq[0:1, 0:1], sq[0:1, 0:1]), "sv", 1)

    @block.tensor
    def _(t):
        st.w(t, "sv")
        for c in range(NCH):
            t.matmul(psA[:], st.sb["ones_col"][:], xb[:, c * W:(c + 1) * W],
                     start=(c == 0), stop=(c == NCH - 1))
        for c in range(NCH):
            mm = t.matmul(psB[:], st.sb["ones_col"][:], sq[:, c * W:(c + 1) * W],
                          start=(c == 0), stop=(c == NCH - 1))
        st.inc(mm, "sp", 1)

    negm, var, tmp = rows[0:1, 0:W], rows[0:1, W:2 * W], rows[0:1, 2 * W:3 * W]
    negm_bf, rstd_bf = rows_bf[0:1, 0:W], rows_bf[0:1, W:2 * W]

    @block.vector
    def _(v):
        st.w(v, "sp")
        v.tensor_scalar(negm, psA[0:1, :], -1.0 / D, None, ALU.mult)
        v.tensor_tensor(tmp, negm, negm, ALU.mult)
        st.inc(v.scalar_tensor_tensor(var, psB[0:1, :], 1.0 / D, tmp, ALU.mult, ALU.subtract), "sv", 1)

    @block.scalar
    def _(a):
        st.w(a, "sv")
        a.activation(var, var, AF.Ln, bias=st.sb["eps"][0:1, 0:1])
        st.inc(a.activation(var, var, AF.Exp, scale=-0.5), "sa", 1)

    @block.vector
    def _(v):
        st.w(v, "sa")
        v.tensor_copy(negm_bf, negm)
        st.inc(v.tensor_copy(rstd_bf, var), "sv", 1)

    Mt = mrt[:, 0:W]

    @block.tensor
    def _(t):
        st.w(t, "sv")
        st.inc(t.matmul(ps_bc[:], st.sb["ones_row"][0:1, :], negm_bf, start=True, stop=True), "sp", 1)

    @block.vector
    def _(v):
        st.w(v, "sp")
        st.inc(v.tensor_copy(Mt, ps_bc[:]), "sv", 1)

    @block.tensor
    def _(t):
        st.w(t, "sv")
        st.inc(t.matmul(ps_bc[:], st.sb["ones_row"][0:1, :], rstd_bf, start=True, stop=True), "sp", 1)

    @block.vector
    def _(v):
        st.w(v, "sp")
        Rt = mrt[:, W:2 * W]
        tmp2 = mrt[:, 2 * W:3 * W]
        v.tensor_copy(Rt, ps_bc[:])
        for c in range(NCH):
            v.tensor_tensor(tmp2, xf[:, c * W:(c + 1) * W], Mt, ALU.add)
            v.tensor_tensor(xh[:, c * W:(c + 1) * W], tmp2, Rt, ALU.mult)
        st.inc(v.tensor_copy(xh[0:1, 0:1], xh[0:1, 0:1]), "sv", 1)


def mk_ln_buffers(st, nc, prefix, stack):
    bufs = {
        "sq": stack.enter_context(nc.sbuf_tensor(prefix + "_sq", [128, NCH * W], BF16)),
        "rows": stack.enter_context(nc.sbuf_tensor(prefix + "_rows", [1, 3 * W], F32)),
        "rows_bf": stack.enter_context(nc.sbuf_tensor(prefix + "_rowsbf", [1, 2 * W], BF16)),
        "mrt": stack.enter_context(nc.sbuf_tensor(prefix + "_mrt", [128, 3 * W], F32)),
        "psA": stack.enter_context(nc.psum_tensor(prefix + "_psA", [1, W], F32)),
        "psB": stack.enter_context(nc.psum_tensor(prefix + "_psB", [1, W], F32)),
        "ps_bc": stack.enter_context(nc.psum_tensor(prefix + "_bc", [128, W], F32)),
    }
    sqh = bufs["sq"]
    bufs["sq"] = sqh[:]
    for k in ("rows", "rows_bf", "mrt"):
        bufs[k] = bufs[k][:]
    for k in ("psA", "psB", "ps_bc"):
        bufs[k] = bufs[k][:]
    return bufs



def phase_preamble(st):
    nc, ap, block = st.nc, st.ap, st.blk

    for name, shape, dt in (
        ("ones_col", [128, 1], BF16), ("ones_row", [1, 128], BF16),
        ("b1c", [128, NCH * E], F32), ("b2c", [128, NCH * E], F32),
        ("be1c", [128, NCH * E], F32),
        ("ew", [128, NCH], F32), ("eu", [128, NCH], F32),
        ("g1r", [1, E * D], BF16), ("g2r", [1, E * D], BF16),
        ("be2r", [1, E * D], BF16),
        ("mixk", [128, NCH], F32), ("mixv", [128, NCH], F32), ("mixr", [128, NCH], F32),
        ("gb", [E, 1], F32), ("eps", [1, 1], F32),
    ):
        st.sb[name] = nc.alloc_sbuf_tensor("pers_" + name, shape, dt)

    @block.sync
    def _(s):
        for name, src in (("ones_col", "ones_col"), ("ones_row", "ones_row"),
                          ("be1c", "be1_cc"), ("mixk", "mixk_c"), ("mixv", "mixv_c"),
                          ("mixr", "mixr_c"), ("gb", "gb_col"), ("eps", "eps11")):
            st.inc(s.dma_start(out=st.sb[name][:], in_=ap[src]), "sl", 16)

    @block.gpsimd
    def _(g):
        for name, src in (("g1r", "g1_r"), ("g2r", "g2_r"), ("be2r", "be2_r")):
            st.inc(g.dma_start(out=st.sb[name][:], in_=ap[src]), "sg", 16)

    from contextlib import ExitStack
    with ExitStack() as ctx:
        wbuf = ctx.enter_context(nc.sbuf_tensor("pre_w", [128, D], F32))
        obuf = ctx.enter_context(nc.sbuf_tensor("pre_o", [128, D], BF16))
        mcol = ctx.enter_context(nc.sbuf_tensor("pre_m", [128, 1], F32))
        mcol2 = ctx.enter_context(nc.sbuf_tensor("pre_m2", [128, 1], F32))
        ccol = ctx.enter_context(nc.sbuf_tensor("pre_cc", [128, NCH], F32))
        brow = ctx.enter_context(nc.sbuf_tensor("pre_br", [1, D], F32))
        bmv = ctx.enter_context(nc.sbuf_tensor("pre_bm", [1, 1], F32))
        bmb = ctx.enter_context(nc.sbuf_tensor("pre_bmb", [1, 1], BF16))
        pcol = ctx.enter_context(nc.psum_tensor("pre_pc", [128, 1], F32))

        def _emit():
            @block.scalar
            def _(a):
                st.w(a, "sl")

            for vname, dest in (("sd_c", "ew"), ("sf_c", "eu")):
                @block.sync
                def _(s, vname=vname):
                    st.w(s, "sa")
                    st.inc(s.dma_start(out=ccol[:], in_=ap[vname]), "sl", 16)

                @block.scalar
                def _(a, dest=dest):
                    st.w(a, "sl")
                    st.inc(a.activation(st.sb[dest][:], ccol[:], AF.Exp, scale=1.0 / T), "sa", 1)

            for wi, (wname, sname, bname, cname, dest) in enumerate((
                ("w1", "s_w1c", "b1r", "b1_cc", "b1c"),
                ("w2", "s_w2c", "b2r", "b2_cc", "b2c"),
            )):
                for e in range(E):
                    for ci in range(NCH):
                        @block.sync
                        def _(s, e=e, ci=ci, wname=wname):
                            st.w(s, "so")
                            st.inc(s.dma_start(out=wbuf[:], in_=ap[wname][e, ci * 128:(ci + 1) * 128, :]), "sl", 16)

                        @block.vector
                        def _(v):
                            st.w(v, "sl")
                            st.inc(v.reduce_sum(mcol[:], wbuf[:], axis=AXX), "sv", 1)

                        @block.scalar
                        def _(a):
                            st.w(a, "sv")
                            st.inc(a.activation(mcol2[:], mcol[:], AF.Identity, scale=1.0 / D), "sa", 1)

                        @block.vector
                        def _(v):
                            st.w(v, "sa")
                            st.inc(v.tensor_scalar(obuf[:], wbuf[:], mcol2[:, 0:1], None, ALU.subtract), "sv", 1)

                        @block.gpsimd
                        def _(g, e=e, ci=ci, sname=sname):
                            st.w(g, "sv")
                            st.inc(g.dma_start(out=ap[sname][e, ci], in_=obuf[:]), "so", 16)

                    @block.sync
                    def _(s, e=e, bname=bname, cname=cname):
                        st.w(s, "sv")
                        st.inc(s.dma_start(out=brow[:], in_=ap[bname][e]), "sl", 16)
                        st.inc(s.dma_start(out=ccol[:], in_=ap[cname][e]), "sl", 16)

                    @block.vector
                    def _(v):
                        st.w(v, "sl")
                        v.reduce_sum(bmv[:], brow[:], axis=AXX)
                        v.tensor_scalar(bmv[:], bmv[:], 1.0 / D, None, ALU.mult)
                        st.inc(v.tensor_copy(bmb[:], bmv[:]), "sv", 1)

                    @block.tensor
                    def _(t):
                        st.w(t, "sv")
                        st.inc(t.matmul(pcol[:], st.sb["ones_row"][0:1, :], bmb[:], start=True, stop=True), "sp", 1)

                    @block.vector
                    def _(v, e=e, dest=dest):
                        st.w(v, "sp")
                        st.inc(v.tensor_tensor(st.sb[dest][:, e * NCH:(e + 1) * NCH], ccol[:],
                                               pcol[:, 0:1].broadcast_to([128, NCH]), ALU.subtract), "sv", 1)

        st.run_phase(_emit)
    full_barrier(st)


def phase_1a(st):
    nc, ap, block = st.nc, st.ap, st.blk
    from contextlib import ExitStack
    with ExitStack() as ctx:
        wq = ctx.enter_context(nc.sbuf_tensor("a_wq", [128, NCH * D], BF16))
        wqp = ctx.enter_context(nc.sbuf_tensor("a_wqp", [128, NCH * D], BF16))
        wk = ctx.enter_context(nc.sbuf_tensor("a_wk", [128, NCH * 512], BF16))
        wkp = ctx.enter_context(nc.sbuf_tensor("a_wkp", [128, NCH * 512], BF16))
        wv = ctx.enter_context(nc.sbuf_tensor("a_wv", [128, NCH * 512], BF16))
        cosb = ctx.enter_context(nc.sbuf_tensor("a_cos", [128, T], BF16))
        sinb = ctx.enter_context(nc.sbuf_tensor("a_sin", [128, T], BF16))
        xf = ctx.enter_context(nc.sbuf_tensor("a_xf", [128, NCH * W], F32))
        xb = ctx.enter_context(nc.sbuf_tensor("a_xb", [128, NCH * W], BF16))
        xh = ctx.enter_context(nc.sbuf_tensor("a_xh", [128, NCH * W], BF16))
        q1 = ctx.enter_context(nc.sbuf_tensor("a_q1", [128, W], F32))
        q2 = ctx.enter_context(nc.sbuf_tensor("a_q2", [128, W], F32))
        qo = ctx.enter_context(nc.sbuf_tensor("a_qo", [128, W], BF16))
        vo = ctx.enter_context(nc.sbuf_tensor("a_vo", [128, 512], BF16))
        lnb = mk_ln_buffers(st, nc, "a", ctx)
        ps1 = ctx.enter_context(nc.psum_tensor("a_p1", [128, W], F32))
        ps2 = ctx.enter_context(nc.psum_tensor("a_p2", [128, W], F32))
        psv = ctx.enter_context(nc.psum_tensor("a_pv", [128, 512], F32))

        def _emit():
            @block.gpsimd
            def _(g):
                st.inc(g.dma_start(out=wq[:].rearrange("p (c d) -> p c d", c=NCH), in_=ap["wq"].rearrange("(c p) d -> p c d", p=128)), "sg", 16)
                st.inc(g.dma_start(out=wqp[:].rearrange("p (c d) -> p c d", c=NCH), in_=ap["wqp"].rearrange("(c p) d -> p c d", p=128)), "sg", 16)
                st.inc(g.dma_start(out=wk[:].rearrange("p (c d) -> p c d", c=NCH), in_=ap["wk"].rearrange("(c p) d -> p c d", p=128)), "sg", 16)
                st.inc(g.dma_start(out=wkp[:].rearrange("p (c d) -> p c d", c=NCH), in_=ap["wkp"].rearrange("(c p) d -> p c d", p=128)), "sg", 16)
                st.inc(g.dma_start(out=wv[:].rearrange("p (c d) -> p c d", c=NCH), in_=ap["wv"].rearrange("(c p) d -> p c d", p=128)), "sg", 16)

            @block.sync
            def _(s):
                st.inc(s.dma_start(out=cosb[:], in_=ap["cosT"]), "sl", 16)
                st.inc(s.dma_start(out=sinb[:], in_=ap["sinTs"]), "sl", 16)

            for tb in range(NTB):
                @block.sync
                def _(s, tb=tb):
                    st.w(s, "sv", "so")
                    st.inc(s.dma_start(out=xf[:].rearrange("p (c t) -> p c t", c=NCH), in_=ap["x0"][:, :, tb * W:(tb + 1) * W]
                                       .rearrange("c p t -> p c t")), "sl", 16)

                @block.gpsimd
                def _(g, tb=tb):
                    st.w(g, "sv")
                    st.inc(g.dma_start(out=xb[:].rearrange("p (c t) -> p c t", c=NCH), in_=ap["x0"][:, :, tb * W:(tb + 1) * W]
                                       .rearrange("c p t -> p c t")), "sg", 16)

                ln_block(st, lnb, xf[:], xb[:], xh[:])

                for kind, nchunks, w_sb, wp_sb, outname, wstride in (
                    ("q", NCH, wq, wqp, "s_qT", D),
                    ("k", 4, wk, wkp, "s_kT", 512),
                ):
                    for dj in range(nchunks):
                        @block.tensor
                        def _(t, dj=dj, w_sb=w_sb, wp_sb=wp_sb, wstride=wstride):
                            st.w(t, "sv")
                            for c in range(NCH):
                                t.matmul(ps1[:], w_sb[:, c * wstride + dj * 128: c * wstride + dj * 128 + 128],
                                         xh[:, c * W:(c + 1) * W], start=(c == 0), stop=(c == NCH - 1))
                            for c in range(NCH):
                                mm = t.matmul(ps2[:], wp_sb[:, c * wstride + dj * 128: c * wstride + dj * 128 + 128],
                                              xh[:, c * W:(c + 1) * W], start=(c == 0), stop=(c == NCH - 1))
                            st.inc(mm, "sp", 1)

                        @block.vector
                        def _(v, dj=dj, tb=tb):
                            st.w(v, "sp", "so")
                            csl = cosb[:, tb * W:(tb + 1) * W]
                            ssl = sinb[:, tb * W:(tb + 1) * W]
                            v.tensor_tensor(q1[:], ps1[:], csl, ALU.mult)
                            v.tensor_tensor(q2[:], ps2[:], ssl, ALU.mult)
                            st.inc(v.tensor_tensor(qo[:], q1[:], q2[:], ALU.add), "sv", 1)

                        @block.gpsimd
                        def _(g, dj=dj, tb=tb, outname=outname):
                            st.w(g, "sv")
                            st.inc(g.dma_start(out=ap[outname][dj, :, tb * W:(tb + 1) * W], in_=qo[:]), "so", 16)

                for tt in range(4):
                    @block.tensor
                    def _(t, tt=tt):
                        st.w(t, "sv")
                        for c in range(NCH):
                            mm = t.matmul(psv[:], xh[:, c * W + tt * 128: c * W + tt * 128 + 128],
                                          wv[:, c * 512:(c + 1) * 512], start=(c == 0), stop=(c == NCH - 1))
                        st.inc(mm, "sp", 1)

                    @block.vector
                    def _(v, tt=tt):
                        st.w(v, "sp", "so")
                        st.inc(v.tensor_copy(vo[:], psv[:]), "sv", 1)

                    @block.gpsimd
                    def _(g, tt=tt, tb=tb):
                        st.w(g, "sv")
                        st.inc(g.dma_start(
                            out=ap["s_v"].rearrange("g t d -> t g d")[tb * W + tt * 128: tb * W + tt * 128 + 128, :, :],
                            in_=vo[:].rearrange("p (g d) -> p g d", g=NKV)), "so", 16)
        st.run_phase(_emit)
    full_barrier(st)


def phase_1b(st):
    nc, ap, block = st.nc, st.ap, st.blk
    from contextlib import ExitStack
    with ExitStack() as ctx:
        kT = ctx.enter_context(nc.sbuf_tensor("b_kT", [128, T], BF16))
        vg = ctx.enter_context(nc.sbuf_tensor("b_v", [128, T], BF16))
        qsb = ctx.enter_context(nc.sbuf_tensor("b_q", [128, 2 * W], BF16))
        pT = ctx.enter_context(nc.sbuf_tensor("b_p", [128, 2 * 1024], BF16))
        rs = ctx.enter_context(nc.sbuf_tensor("b_rs", [1, T], F32))
        rcp = ctx.enter_context(nc.sbuf_tensor("b_rcp", [1, T], F32))
        msk = ctx.enter_context(nc.sbuf_tensor("b_m", [128, 4 * W], BF16))
        oev = ctx.enter_context(nc.sbuf_tensor("b_oe", [128, 2 * W], BF16))
        s0 = ctx.enter_context(nc.psum_tensor("b_s0", [128, W], F32))
        s1 = ctx.enter_context(nc.psum_tensor("b_s1", [128, W], F32))
        s2 = ctx.enter_context(nc.psum_tensor("b_s2", [128, W], F32))
        s3 = ctx.enter_context(nc.psum_tensor("b_s3", [128, W], F32))
        ob0 = ctx.enter_context(nc.psum_tensor("b_o0", [128, W], F32))
        ob1 = ctx.enter_context(nc.psum_tensor("b_o1", [128, W], F32))
        rb0 = ctx.enter_context(nc.psum_tensor("b_r0", [1, W], F32))
        rb1 = ctx.enter_context(nc.psum_tensor("b_r1", [1, W], F32))
        pair_banks = [(s0, s1), (s2, s3)]

        def _emit():
            @block.sync
            def _(s):
                st.inc(s.dma_start(out=msk[:].rearrange("p (m q) -> p m q", m=4), in_=ap["masks"].rearrange("m p q -> p m q")), "sl", 16)

            rcp_tracker = {}
            exp_done = {}
            rsav_done = {}
            mask_done = {}
            scores_done = {}
            qload_done = {}
            evac_done = {}
            store_done = {}
            gpi = 0
            gqb = 0

            for g in range(NKV):
                @block.sync
                def _(s, g=g):
                    st.w(s, "sp", "sa", "sv")
                    st.inc(s.dma_start(out=kT[:], in_=ap["s_kT"][g]), "sl", 16)
                    st.inc(s.dma_start(out=vg[:].rearrange("p (j d) -> p j d", j=T // 128), in_=ap["s_v"][g].rearrange("(j p) d -> p j d", p=128)), "sl", 16)

                for hh in range(GR):
                    h = g * GR + hh
                    for qb in range(NTB):
                        nch_s = 4 * qb + 4
                        qpar = gqb % 2
                        qsl = qsb[:, qpar * W:(qpar + 1) * W]

                        @block.sync
                        def _(s, h=h, qb=qb, gqb=gqb, qsl=qsl):
                            if gqb - 2 in scores_done:
                                s.wait_ge(st.s["sp"], scores_done[gqb - 2])
                            st.inc(s.dma_start(out=qsl, in_=ap["s_qT"][h, :, qb * W:(qb + 1) * W]), "sl", 16)
                            qload_done[gqb] = st.c["sl"]

                        npair = (nch_s + 1) // 2
                        obk = ob0 if gqb % 2 == 0 else ob1
                        rbk = rb0 if gqb % 2 == 0 else rb1
                        for pi in range(npair):
                            c0 = 2 * pi
                            c1 = min(2 * pi + 1, nch_s - 1)
                            sA, sB = pair_banks[pi % 2]
                            pslice = pT[:, (pi % 2) * 1024:(pi % 2) * 1024 + 1024]

                            @block.tensor
                            def _(t, c0=c0, c1=c1, sA=sA, sB=sB, gpi=gpi, gqb=gqb, qsl=qsl, pi=pi):
                                t.wait_ge(st.s["sl"], qload_done[gqb])
                                if gpi - 2 in exp_done:
                                    t.wait_ge(st.s["sa"], exp_done[gpi - 2])
                                if pi == 0 and gqb - 1 in evac_done:
                                    t.wait_ge(st.s["sv"], evac_done[gqb - 1])
                                t.matmul(sA[:], kT[:, c0 * 128:(c0 + 1) * 128], qsl, start=True, stop=True)
                                mm = t.matmul(sB[:], kT[:, c1 * 128:(c1 + 1) * 128], qsl, start=True, stop=True)
                                st.inc(mm, "sp", 1)
                                if pi == npair - 1:
                                    scores_done[gqb] = st.c["sp"]

                            sp_after_scores = st.c["sp"]

                            @block.scalar
                            def _(a, sA=sA, sB=sB, pslice=pslice, c0=c0, c1=c1, gpi=gpi, n=sp_after_scores):
                                a.wait_ge(st.s["sp"], n)
                                if gpi - 2 in rsav_done:
                                    a.wait_ge(st.s["sp"], rsav_done[gpi - 2])
                                ssc = 1.0 / math.sqrt(128.0)
                                if c1 > c0:
                                    a.activation(pslice[:, 0:W], sA[:], AF.Exp, scale=ssc)
                                    st.inc(a.activation(pslice[:, W:2 * W], sB[:], AF.Exp, scale=ssc), "sa", 1)
                                else:
                                    st.inc(a.activation(pslice[:, 0:W], sA[:], AF.Exp, scale=ssc), "sa", 1)
                                exp_done[gpi] = st.c["sa"]

                            sa_after_exp = st.c["sa"]

                            masked = sorted(set(c for c in (c0, c1) if c >= 4 * qb))
                            if masked:
                                @block.vector
                                def _(v, c0=c0, qb=qb, pslice=pslice, masked=tuple(masked), n=sa_after_exp):
                                    v.wait_ge(st.s["sa"], n)
                                    last = None
                                    for c in masked:
                                        oi = c - 4 * qb
                                        psl = pslice[:, (c - c0) * W:(c - c0) * W + W]
                                        last = v.tensor_tensor(psl, psl, msk[:, oi * W:(oi + 1) * W], ALU.mult)
                                    st.inc(last, "sv", 1)
                                    mask_done[gpi] = st.c["sv"]

                            sv_after_mask = st.c["sv"]

                            @block.tensor
                            def _(t, c0=c0, c1=c1, pslice=pslice, nch_s=nch_s, gpi=gpi,
                                  na=sa_after_exp, nv=sv_after_mask, had_mask=bool(masked),
                                  obk=obk, rbk=rbk):
                                t.wait_ge(st.s["sa"], na)
                                if had_mask:
                                    t.wait_ge(st.s["sv"], nv)
                                cs = [c0] if c1 == c0 else [c0, c1]
                                for c in cs:
                                    psl = pslice[:, (c - c0) * W:(c - c0) * W + W]
                                    t.matmul(rbk[:], st.sb["ones_col"][:], psl,
                                             start=(c == 0), stop=(c == nch_s - 1))
                                    mm = t.matmul(obk[:], vg[:, c * 128:(c + 1) * 128], psl,
                                                  start=(c == 0), stop=(c == nch_s - 1))
                                st.inc(mm, "sp", 1)
                                rsav_done[gpi] = st.c["sp"]

                            gpi += 1

                        sp_after_last_rsav = st.c["sp"]

                        @block.vector
                        def _(v, h=h, qb=qb, gqb=gqb, n=sp_after_last_rsav, obk=obk, rbk=rbk, qpar=qpar):
                            v.wait_ge(st.s["sp"], n)
                            if qb == 0 and "sa" in rcp_tracker:
                                v.wait_ge(st.s["sa"], rcp_tracker["sa"])
                            if gqb - 2 in store_done:
                                v.wait_ge(st.s["so"], store_done[gqb - 2])
                            v.tensor_copy(rs[0:1, qb * W:(qb + 1) * W], rbk[:])
                            st.inc(v.tensor_copy(oev[:, qpar * W:(qpar + 1) * W], obk[:]), "sv", 1)
                            evac_done[gqb] = st.c["sv"]

                        sv_after_evac = st.c["sv"]

                        @block.gpsimd
                        def _(gq, h=h, qb=qb, gqb=gqb, n=sv_after_evac, qpar=qpar):
                            gq.wait_ge(st.s["sv"], n)
                            st.inc(gq.dma_start(out=ap["s_oT"][h, :, qb * W:(qb + 1) * W],
                                                in_=oev[:, qpar * W:(qpar + 1) * W]), "so", 16)
                            store_done[gqb] = st.c["so"]

                        gqb += 1

                    @block.scalar
                    def _(a, h=h):
                        st.w(a, "sv", "so")
                        a.activation(rcp[:], rs[:], AF.Ln)
                        st.inc(a.activation(rcp[:], rcp[:], AF.Exp, scale=-1.0), "sa", 1)
                    rcp_tracker["sa"] = st.c["sa"]

                    @block.gpsimd
                    def _(gq, h=h):
                        st.w(gq, "sa")
                        st.inc(gq.dma_start(out=ap[f"s_rcp{h}"], in_=rcp[:]), "so", 16)
        st.run_phase(_emit)
    full_barrier(st)


def phase_1c(st):
    nc, ap, block = st.nc, st.ap, st.blk
    from contextlib import ExitStack
    with ExitStack() as ctx:
        wo = ctx.enter_context(nc.sbuf_tensor("c_wo", [128, NCH * D], BF16))
        oT = ctx.enter_context(nc.sbuf_tensor("c_oT", [128, NCH * W], BF16))
        on = ctx.enter_context(nc.sbuf_tensor("c_on", [128, NCH * W], BF16))
        x0b = ctx.enter_context(nc.sbuf_tensor("c_x0", [128, NCH * W], F32))
        x1b = ctx.enter_context(nc.sbuf_tensor("c_x1", [128, NCH * W], F32))
        rcr = ctx.enter_context(nc.sbuf_tensor("c_rc", [1, NH * W], F32))
        rcb = ctx.enter_context(nc.sbuf_tensor("c_rcb", [1, NH * W], BF16))
        bc = ctx.enter_context(nc.psum_tensor("c_bc", [128, W], F32))
        pa = ctx.enter_context(nc.psum_tensor("c_pa", [128, W], F32))

        def _emit():
            @block.gpsimd
            def _(g):
                st.inc(g.dma_start(out=wo[:].rearrange("p (c d) -> p c d", c=NCH), in_=ap["woa"].rearrange("(c p) d -> p c d", p=128)), "sg", 16)

            for tb in range(NTB):
                @block.sync
                def _(s, tb=tb):
                    st.w(s, "sv", "so")
                    st.inc(s.dma_start(out=oT[:].rearrange("p (h t) -> p h t", h=NH),
                                       in_=ap["s_oT"][:, :, tb * W:(tb + 1) * W]
                                       .rearrange("h p t -> p h t")), "sl", 16)
                    st.inc(s.dma_start(out=x0b[:].rearrange("p (c t) -> p c t", c=NCH), in_=ap["x0"][:, :, tb * W:(tb + 1) * W]
                                       .rearrange("c p t -> p c t")), "sl", 16)
                    for h in range(NH):
                        st.inc(s.dma_start(out=rcr[0:1, h * W:(h + 1) * W],
                                           in_=ap[f"s_rcp{h}"][0:1, tb * W:(tb + 1) * W]), "sl", 16)

                @block.vector
                def _(v, tb=tb):
                    st.w(v, "sl", "sg", "sq")
                    st.inc(v.tensor_copy(rcb[:], rcr[:]), "sv", 1)

                for h in range(NH):
                    @block.tensor
                    def _(t, h=h):
                        st.w(t, "sv")
                        st.inc(t.matmul(bc[:], st.sb["ones_row"][0:1, :], rcb[0:1, h * W:(h + 1) * W],
                                        start=True, stop=True), "sp", 1)

                    @block.vector
                    def _(v, h=h):
                        st.w(v, "sp")
                        st.inc(v.tensor_tensor(on[:, h * W:(h + 1) * W], oT[:, h * W:(h + 1) * W],
                                               bc[:], ALU.mult), "sv", 1)

                for c in range(NCH):
                    @block.tensor
                    def _(t, c=c):
                        st.w(t, "sv")
                        for hc in range(NCH):
                            mm = t.matmul(pa[:], wo[:, hc * D + c * 128: hc * D + c * 128 + 128],
                                          on[:, hc * W:(hc + 1) * W], start=(hc == 0), stop=(hc == NCH - 1))
                        st.inc(mm, "sp", 1)

                    @block.vector
                    def _(v, c=c):
                        st.w(v, "sp")
                        st.inc(v.tensor_tensor(x1b[:, c * W:(c + 1) * W], pa[:],
                                               x0b[:, c * W:(c + 1) * W], ALU.add), "sv", 1)

                @block.gpsimd
                def _(g2, tb=tb):
                    st.w(g2, "sv")
                    st.inc(g2.dma_start(out=ap["s_x1"][:, :, tb * W:(tb + 1) * W].rearrange("c p t -> p c t"), in_=x1b[:].rearrange("p (c t) -> p c t", c=NCH)), "so", 16)
        st.run_phase(_emit)
    full_barrier(st)


def phase_ffn(st, src_name, dst_name):
    nc, ap, block = st.nc, st.ap, st.blk
    pfx = "f2_" if dst_name == "y" else "f_"
    if dst_name == "s_x2" and DBG.get("ffn") and "dbgr_t" not in st.sb:
        st.sb["dbgr_t"] = nc.alloc_sbuf_tensor("dbgr_t", [32, 9 * W], BF16)
    from contextlib import ExitStack
    with ExitStack() as ctx:
        gwsb = ctx.enter_context(nc.sbuf_tensor(pfx + "gw", [128, NCH * E], BF16))
        w1e = ctx.enter_context(nc.sbuf_tensor(pfx + "w1e", [128, NCH * D], BF16))
        w2e = ctx.enter_context(nc.sbuf_tensor(pfx + "w2e", [128, NCH * D], BF16))
        xf = ctx.enter_context(nc.sbuf_tensor(pfx + "xf", [128, NCH * W], F32))
        xb = ctx.enter_context(nc.sbuf_tensor(pfx + "xb", [128, NCH * W], BF16))
        xh = ctx.enter_context(nc.sbuf_tensor(pfx + "xh", [128, NCH * W], BF16))
        hsb = ctx.enter_context(nc.sbuf_tensor(pfx + "h", [128, NCH * W], BF16))
        hn = ctx.enter_context(nc.sbuf_tensor(pfx + "hn", [128, NCH * W], BF16))
        zsb = ctx.enter_context(nc.sbuf_tensor(pfx + "z", [128, NCH * W], BF16))
        gs = ctx.enter_context(nc.sbuf_tensor(pfx + "gs", [32, 8 * W], F32))
        wr = ctx.enter_context(nc.sbuf_tensor(pfx + "wr", [32, 4 * W], F32))
        wrb = ctx.enter_context(nc.sbuf_tensor(pfx + "wrb", [1, 4 * W], BF16))
        er = ctx.enter_context(nc.sbuf_tensor(pfx + "er", [1, 6 * W], F32))
        erb = ctx.enter_context(nc.sbuf_tensor(pfx + "erb", [1, 4 * W], BF16))
        acc = ctx.enter_context(nc.sbuf_tensor(pfx + "acc", [128, W], F32))
        xout = ctx.enter_context(nc.sbuf_tensor(pfx + "xo", [128, NCH * W], F32))
        lnb = mk_ln_buffers(st, nc, pfx, ctx)
        ps1 = ctx.enter_context(nc.psum_tensor(pfx + "p1", [128, W], F32))
        ps2 = ctx.enter_context(nc.psum_tensor(pfx + "p2", [128, W], F32))
        psg = ctx.enter_context(nc.psum_tensor(pfx + "pg", [4, W], F32))
        psbe = ctx.enter_context(nc.psum_tensor(pfx + "pbe", [128, W], F32))

        def _emit():
            @block.gpsimd
            def _(g):
                st.inc(g.dma_start(out=gwsb[:].rearrange("p (c e) -> p c e", c=NCH), in_=ap["gw"].rearrange("(c p) e -> p c e", p=128)), "sg", 16)

            @block.vector
            def _(v):
                v.memset(gs[:], 0.0)
                st.inc(v.memset(wr[:], 0.0), "sv", 1)

            for tb in range(NTB):
                @block.sync
                def _(s, tb=tb):
                    st.w(s, "sv", "so", "sa")
                    st.inc(s.dma_start(out=xf[:].rearrange("p (c t) -> p c t", c=NCH), in_=ap[src_name][:, :, tb * W:(tb + 1) * W]
                                       .rearrange("c p t -> p c t")), "sl", 16)

                @block.gpsimd
                def _(g, tb=tb):
                    st.w(g, "sv")
                    st.inc(g.dma_start(out=xb[:].rearrange("p (c t) -> p c t", c=NCH), in_=ap[src_name][:, :, tb * W:(tb + 1) * W]
                                       .rearrange("c p t -> p c t")), "sg", 16)

                ln_block(st, lnb, xf[:], xb[:], xh[:])
                @block.tensor
                def _(t):
                    st.w(t, "sv")
                    for c in range(NCH):
                        mm = t.matmul(psg[:], gwsb[:, c * E:(c + 1) * E], xh[:, c * W:(c + 1) * W],
                                      start=(c == 0), stop=(c == NCH - 1))
                    st.inc(mm, "sp", 1)

                sgr = gs[:, 0:W]
                t1 = gs[:, W:2 * W]
                t2 = gs[:, 2 * W:3 * W]
                t3 = gs[:, 3 * W:4 * W]
                m1 = gs[:, 4 * W:5 * W]
                m2 = gs[:, 5 * W:6 * W]
                ind = gs[:, 6 * W:7 * W]
                wun = gs[:, 7 * W:8 * W]

                @block.scalar
                def _(a):
                    st.w(a, "sp")
                    st.inc(a.activation(sgr[0:4, :] if hasattr(sgr, "shape") else sgr, psg[:], AF.Identity, bias=st.sb["gb"][:, 0:1]), "sa", 1)

                @block.vector
                def _(v):
                    st.w(v, "sa")
                    v.stream_shuffle(t1, sgr, ROT)
                    v.stream_shuffle(t2, t1, ROT)
                    v.stream_shuffle(t3, t2, ROT)
                    v.tensor_tensor(m1, sgr, t1, ALU.max)
                    v.tensor_tensor(m2, t2, t3, ALU.max)
                    v.tensor_tensor(wun, sgr, t1, ALU.min)
                    v.tensor_tensor(ind, t2, t3, ALU.min)
                    v.tensor_tensor(t1, m1, m2, ALU.min)
                    v.tensor_tensor(m1, m1, m2, ALU.max)
                    v.tensor_tensor(t2, wun, ind, ALU.max)
                    v.tensor_tensor(m2, t1, t2, ALU.max)
                    v.tensor_tensor(t1, sgr, m1, ALU.subtract)
                    v.tensor_tensor(t2, sgr, m2, ALU.subtract)
                    v.tensor_scalar(t2, t2, 1e12, 1.0, ALU.mult, ALU.add)
                    st.inc(v.tensor_scalar(ind, t2, 1.0, 0.0, ALU.min, ALU.max), "sv", 1)

                @block.scalar
                def _(a):
                    st.w(a, "sv")
                    st.inc(a.activation(t1[0:4, :] if True else t1, t1[0:4, :] if True else t1, AF.Exp), "sa", 1)

                @block.vector
                def _(v):
                    st.w(v, "sa")
                    v.tensor_tensor(wun, t1, ind, ALU.mult)
                    v.stream_shuffle(t2, wun, ROT)
                    v.tensor_tensor(t2, wun, t2, ALU.add)
                    v.stream_shuffle(t3, t2, ROT)
                    v.stream_shuffle(t3, t3, ROT)
                    st.inc(v.tensor_tensor(t2, t2, t3, ALU.add), "sv", 1)

                @block.scalar
                def _(a):
                    st.w(a, "sv")
                    a.activation(t3[0:4, :], t2[0:4, :], AF.Ln)
                    st.inc(a.activation(t3[0:4, :], t3[0:4, :], AF.Exp, scale=-1.0), "sa", 1)

                @block.vector
                def _(v):
                    st.w(v, "sa")
                    v.tensor_tensor(sgr, wun, t3, ALU.mult)
                    v.stream_shuffle(wr[:, W:2 * W], sgr, ROT)
                    v.stream_shuffle(wr[:, 2 * W:3 * W], wr[:, W:2 * W], ROT)
                    v.stream_shuffle(wr[:, 3 * W:4 * W], wr[:, 2 * W:3 * W], ROT)
                    v.tensor_copy(wrb[0:1, 0:W], sgr[0:1, :])
                    v.tensor_copy(wrb[0:1, W:2 * W], wr[0:1, W:2 * W])
                    v.tensor_copy(wrb[0:1, 2 * W:3 * W], wr[0:1, 2 * W:3 * W])
                    st.inc(v.tensor_copy(wrb[0:1, 3 * W:4 * W], wr[0:1, 3 * W:4 * W]), "sv", 1)

                for c in range(NCH):
                    @block.tensor
                    def _(t, c=c):
                        st.w(t, "sv")
                        for e in range(E):
                            mm = t.matmul(psbe[:], st.sb["be2r"][0:1, e * D + c * 128: e * D + c * 128 + 128],
                                          wrb[0:1, e * W:(e + 1) * W], start=(e == 0), stop=(e == E - 1))
                        st.inc(mm, "sp", 1)

                    @block.vector
                    def _(v, c=c):
                        st.w(v, "sp")
                        v.tensor_tensor(acc[:], psbe[:], xh[:, c * W:(c + 1) * W], ALU.add)
                        st.inc(v.tensor_tensor(xout[:, c * W:(c + 1) * W], acc[:],
                                               xf[:, c * W:(c + 1) * W], ALU.add), "sv", 1)

                if dst_name == "s_x2" and tb == 0 and DBG.get("ffn"):
                    @block.vector
                    def _(v):
                        st.w(v, "sv", "sa")
                        dbg = st.sb["dbgr_t"][:]
                        v.tensor_copy(dbg[:, 0 * W:1 * W], sgr)
                        v.tensor_copy(dbg[:, 1 * W:2 * W], m1)
                        v.tensor_copy(dbg[:, 2 * W:3 * W], m2)
                        v.tensor_copy(dbg[:, 3 * W:4 * W], ind)
                        v.tensor_copy(dbg[:, 4 * W:5 * W], wun)
                        st.inc(v.tensor_copy(dbg[0:1, 0:1], dbg[0:1, 0:1]), "sv", 1)

                for e in range(E):
                    @block.sync
                    def _(s, e=e):
                        st.w(s, "sp", "sa")
                        st.inc(s.dma_start(out=w1e[:].rearrange("p (c d) -> p c d", c=NCH), in_=ap["s_w1c"][e].rearrange("c p d -> p c d")), "sl", 16)
                        st.inc(s.dma_start(out=w2e[:].rearrange("p (c d) -> p c d", c=NCH), in_=ap["s_w2c"][e].rearrange("c p d -> p c d")), "sl", 16)

                    for dj in range(NCH):
                        pse = ps1 if dj % 2 == 0 else ps2

                        @block.tensor
                        def _(t, e=e, dj=dj, pse=pse):
                            st.w(t, "sl", "sa")
                            for c in range(NCH):
                                mm = t.matmul(pse[:], w1e[:, c * D + dj * 128: c * D + dj * 128 + 128],
                                              xh[:, c * W:(c + 1) * W], start=(c == 0), stop=(c == NCH - 1))
                            st.inc(mm, "sp", 1)

                        @block.scalar
                        def _(a, e=e, dj=dj, pse=pse):
                            st.w(a, "sp")
                            st.inc(a.activation(hsb[:, dj * W:(dj + 1) * W], pse[:], AF.Identity,
                                                bias=st.sb["b1c"][:, e * NCH + dj: e * NCH + dj + 1]), "sa", 1)

                    @block.vector
                    def _(v, e=e):
                        st.w(v, "sa")
                        st.inc(v.tensor_tensor(lnb["sq"], hsb[:], hsb[:], ALU.mult), "sv", 1)

                    @block.tensor
                    def _(t, e=e):
                        st.w(t, "sv")
                        for c in range(NCH):
                            mm = t.matmul(lnb["psA"], st.sb["ones_col"][:], lnb["sq"][:, c * W:(c + 1) * W],
                                          start=(c == 0), stop=(c == NCH - 1))
                        st.inc(mm, "sp", 1)

                    rst1 = er[0:1, 0:W]

                    @block.vector
                    def _(v):
                        st.w(v, "sp")
                        st.inc(v.tensor_scalar(rst1, lnb["psA"][0:1, :], 1.0 / D, None, ALU.mult), "sv", 1)

                    @block.scalar
                    def _(a):
                        st.w(a, "sv")
                        a.activation(rst1, rst1, AF.Ln, bias=st.sb["eps"][0:1, 0:1])
                        st.inc(a.activation(rst1, rst1, AF.Exp, scale=-0.5), "sa", 1)

                    @block.vector
                    def _(v):
                        st.w(v, "sa")
                        st.inc(v.tensor_copy(erb[0:1, 0:W], rst1), "sv", 1)

                    for dj in range(NCH):
                        @block.tensor
                        def _(t, e=e, dj=dj):
                            st.w(t, "sv")
                            st.inc(t.matmul(lnb["ps_bc"], st.sb["g1r"][0:1, e * D + dj * 128: e * D + dj * 128 + 128],
                                            erb[0:1, 0:W], start=True, stop=True), "sp", 1)

                        @block.vector
                        def _(v, e=e, dj=dj):
                            st.w(v, "sp")
                            v.tensor_tensor(hn[:, dj * W:(dj + 1) * W], hsb[:, dj * W:(dj + 1) * W],
                                            lnb["ps_bc"], ALU.mult)
                            st.inc(v.tensor_scalar(hn[:, dj * W:(dj + 1) * W], hn[:, dj * W:(dj + 1) * W],
                                                   st.sb["be1c"][:, e * NCH + dj: e * NCH + dj + 1], 0.0,
                                                   ALU.add, ALU.max), "sv", 1)

                    for dj in range(NCH):
                        pse = ps1 if dj % 2 == 0 else ps2

                        @block.tensor
                        def _(t, e=e, dj=dj, pse=pse):
                            st.w(t, "sv", "sa")
                            for c in range(NCH):
                                mm = t.matmul(pse[:], w2e[:, c * D + dj * 128: c * D + dj * 128 + 128],
                                              hn[:, c * W:(c + 1) * W], start=(c == 0), stop=(c == NCH - 1))
                            st.inc(mm, "sp", 1)

                        @block.scalar
                        def _(a, e=e, dj=dj, pse=pse):
                            st.w(a, "sp")
                            st.inc(a.activation(zsb[:, dj * W:(dj + 1) * W],
                                                pse[:], AF.Identity,
                                                bias=st.sb["b2c"][:, e * NCH + dj: e * NCH + dj + 1]), "sa", 1)

                    @block.vector
                    def _(v, e=e):
                        st.w(v, "sa")
                        st.inc(v.tensor_tensor(lnb["sq"], zsb[:],
                                               zsb[:], ALU.mult), "sv", 1)

                    @block.tensor
                    def _(t, e=e):
                        st.w(t, "sv")
                        for c in range(NCH):
                            mm = t.matmul(lnb["psB"], st.sb["ones_col"][:], lnb["sq"][:, c * W:(c + 1) * W],
                                          start=(c == 0), stop=(c == NCH - 1))
                        st.inc(mm, "sp", 1)

                    rst2 = er[0:1, (1 + e) * W:(2 + e) * W]

                    @block.vector
                    def _(v, rst2=rst2, e=e, tb=tb):
                        st.w(v, "sp")
                        if dst_name == "s_x2" and tb == 0 and e == 0 and DBG.get("ffn"):
                            v.tensor_copy(st.sb["dbgr_t"][:][0:1, 4 * W:5 * W].bitcast(F32)[0:1, 0:W // 2] if False else st.sb["dbgr_t"][:][0:1, 4 * W:5 * W], lnb["psB"][0:1, :])
                        st.inc(v.tensor_scalar(rst2, lnb["psB"][0:1, :], 1.0 / D, None, ALU.mult), "sv", 1)

                    @block.scalar
                    def _(a, rst2=rst2):
                        st.w(a, "sv")
                        a.activation(rst2, rst2, AF.Ln, bias=st.sb["eps"][0:1, 0:1])
                        st.inc(a.activation(rst2, rst2, AF.Exp, scale=-0.5), "sa", 1)

                    @block.vector
                    def _(v, e=e, rst2=rst2):
                        st.w(v, "sa")
                        wrow = sgr[0:1, :] if e == 0 else wr[0:1, e * W:(e + 1) * W]
                        v.tensor_tensor(er[0:1, 5 * W:6 * W], wrow, rst2, ALU.mult)
                        st.inc(v.tensor_copy(erb[0:1, 0:W], er[0:1, 5 * W:6 * W]), "sv", 1)

                    for c in range(NCH):
                        @block.tensor
                        def _(t, e=e, c=c):
                            st.w(t, "sv")
                            st.inc(t.matmul(lnb["ps_bc"], st.sb["g2r"][0:1, e * D + c * 128: e * D + c * 128 + 128],
                                            erb[0:1, 0:W], start=True, stop=True), "sp", 1)

                        @block.vector
                        def _(v, c=c):
                            st.w(v, "sp")
                            v.tensor_tensor(acc[:], zsb[:, c * W:(c + 1) * W], lnb["ps_bc"], ALU.mult)
                            st.inc(v.tensor_tensor(xout[:, c * W:(c + 1) * W], xout[:, c * W:(c + 1) * W],
                                                   acc[:], ALU.add), "sv", 1)


                @block.gpsimd
                def _(g2, tb=tb):
                    st.w(g2, "sv")
                    st.inc(g2.dma_start(out=ap[dst_name][:, :, tb * W:(tb + 1) * W].rearrange("c p t -> p c t"), in_=xout[:].rearrange("p (c t) -> p c t", c=NCH)), "so", 16)

                if dst_name == "s_x2" and tb == 0 and DBG.get("ffn"):
                    @block.vector
                    def _(v):
                        st.w(v, "sv")
                        dbg = st.sb["dbgr_t"][:]
                        v.tensor_copy(dbg[0:1, 5 * W:6 * W], er[0:1, 1 * W:2 * W])
                        v.tensor_copy(dbg[0:1, 6 * W:7 * W], er[0:1, 2 * W:3 * W])
                        v.tensor_copy(dbg[0:1, 7 * W:8 * W], er[0:1, 3 * W:4 * W])
                        st.inc(v.tensor_copy(dbg[0:1, 8 * W:9 * W], er[0:1, 4 * W:5 * W]), "sv", 1)

                    @block.gpsimd
                    def _(g3):
                        st.w(g3, "sv")
                        st.inc(g3.dma_start(out=ap["s_dbgR"][:, 0:9 * W], in_=st.sb["dbgr_t"][:]), "so", 16)
                        dw = DBG.get("dump_what", "xout")
                        buf = {"xout": xout, "z0": zsb, "h0": hsb, "hn0": hn, "xh0": xh, "sq0": lnb and None}.get(dw)
                        if dw == "z0":
                            st.inc(g3.dma_start(out=ap["s_dbgA"][:, 0:NCH * W], in_=zsb[:]), "so", 16)
                        elif dw == "h0":
                            st.inc(g3.dma_start(out=ap["s_dbgA"][:, 0:NCH * W], in_=hsb[:]), "so", 16)
                        elif dw == "hn0":
                            st.inc(g3.dma_start(out=ap["s_dbgA"][:, 0:NCH * W], in_=hn[:]), "so", 16)
                        elif dw == "xh0":
                            st.inc(g3.dma_start(out=ap["s_dbgA"][:, 0:NCH * W], in_=xh[:]), "so", 16)
                        else:
                            st.inc(g3.dma_start(out=ap["s_dbgA"][:, 0:NCH * W], in_=xout[:]), "so", 16)
        st.run_phase(_emit)
    full_barrier(st)


def phase_3(st):
    nc, ap, block = st.nc, st.ap, st.blk
    from contextlib import ExitStack

    with ExitStack() as ctx:
        xf = ctx.enter_context(nc.sbuf_tensor("d_xf", [128, NCH * W], F32))
        xb = ctx.enter_context(nc.sbuf_tensor("d_xb", [128, NCH * W], BF16))
        xh = ctx.enter_context(nc.sbuf_tensor("d_xh", [128, NCH * W], BF16))
        lnb = mk_ln_buffers(st, nc, "d", ctx)
        def _emit():
            for tb in range(NTB):
                @block.sync
                def _(s, tb=tb):
                    st.w(s, "sv", "so")
                    st.inc(s.dma_start(out=xf[:].rearrange("p (c t) -> p c t", c=NCH), in_=ap["s_x2"][:, :, tb * W:(tb + 1) * W]
                                       .rearrange("c p t -> p c t")), "sl", 16)

                @block.gpsimd
                def _(g, tb=tb):
                    st.w(g, "sv")
                    st.inc(g.dma_start(out=xb[:].rearrange("p (c t) -> p c t", c=NCH), in_=ap["s_x2"][:, :, tb * W:(tb + 1) * W]
                                       .rearrange("c p t -> p c t")), "sg", 16)

                ln_block(st, lnb, xf[:], xb[:], xh[:])

                @block.gpsimd
                def _(g, tb=tb):
                    st.w(g, "sv")
                    st.inc(g.dma_start(out=ap["s_xh3"][:, :, tb * W:(tb + 1) * W].rearrange("c p t -> p c t"), in_=xh[:].rearrange("p (c t) -> p c t", c=NCH)), "so", 16)
        st.run_phase(_emit)
    full_barrier(st)

    with ExitStack() as ctx:
        wsb = ctx.enter_context(nc.sbuf_tensor("e_w", [128, NCH * D], BF16))
        xhs = ctx.enter_context(nc.sbuf_tensor("e_xh", [128, NCH * (W + 2)], BF16))
        mix = ctx.enter_context(nc.sbuf_tensor("e_mx", [128, 4 * W], BF16))
        evac = ctx.enter_context(nc.sbuf_tensor("e_ev", [128, W], BF16))
        ps1 = ctx.enter_context(nc.psum_tensor("e_p1", [128, W], F32))
        ps2 = ctx.enter_context(nc.psum_tensor("e_p2", [128, W], F32))

        def _emit():
            def mix_pass(w_name, mixc_name, out_name, act):
                @block.gpsimd
                def _(g):
                    st.w(g, "sp", "sa", "so")
                    st.inc(g.dma_start(out=wsb[:].rearrange("p (c d) -> p c d", c=NCH), in_=ap[w_name].rearrange("(c p) d -> p c d", p=128)), "sg", 16)

                for tb in range(NTB):
                    lo = max(0, tb * W - 1)
                    hi = min(T, (tb + 1) * W + 1)
                    npad_l = 1 if tb == 0 else 0
                    xv = xhs[:].rearrange("p (c w) -> p c w", c=NCH)

                    @block.sync
                    def _(s, tb=tb, lo=lo, hi=hi, npad_l=npad_l, xv=xv):
                        st.w(s, "sp", "sa")
                        st.inc(s.dma_start(
                            out=xv[:, :, npad_l:npad_l + (hi - lo)],
                            in_=ap["s_xh3"][:, :, lo:hi].rearrange("c p t -> p c t")), "sl", 16)

                    if tb == 0 or tb == NTB - 1:
                        @block.vector
                        def _(v, tb=tb, xv=xv):
                            st.w(v, "sl", "sp")
                            col = 0 if tb == 0 else (W + 1)
                            st.inc(v.memset(xv[:, :, col:col + 1], 0.0), "sv", 1)

                    @block.vector
                    def _(v, tb=tb, xv=xv, mixc_name=mixc_name):
                        st.w(v, "sl", "sp")
                        for c in range(4):
                            cur = xv[:, c, 1:1 + W]
                            sh = xv[:, c, 0:W] if c < 2 else xv[:, c, 2:2 + W]
                            dst = mix[:, c * W:(c + 1) * W]
                            v.tensor_tensor(dst, cur, sh, ALU.subtract)
                            v.tensor_scalar(dst, dst, st.sb[mixc_name][:, c:c + 1], None, ALU.mult)
                            v.tensor_tensor(dst, dst, sh, ALU.add)
                        st.inc(v.tensor_copy(mix[0:1, 0:1], mix[0:1, 0:1]), "sv", 1)

                    for dj in range(NCH):
                        pse = ps1 if dj % 2 == 0 else ps2

                        @block.tensor
                        def _(t, dj=dj, pse=pse, xv=xv):
                            st.w(t, "sv", "sa", "sg")
                            for c in range(NCH):
                                rhs = mix[:, c * W:(c + 1) * W] if c < 4 else xv[:, c, 1:1 + W]
                                mm = t.matmul(pse[:], wsb[:, c * D + dj * 128: c * D + dj * 128 + 128],
                                              rhs, start=(c == 0), stop=(c == NCH - 1))
                            st.inc(mm, "sp", 1)

                        @block.scalar
                        def _(a, dj=dj, pse=pse, act=act):
                            st.w(a, "sp", "so")
                            st.inc(a.activation(evac[:], pse[:], act), "sa", 1)

                        @block.gpsimd
                        def _(g, tb=tb, dj=dj, out_name=out_name):
                            st.w(g, "sa")
                            st.inc(g.dma_start(out=ap[out_name][dj, :, tb * W:(tb + 1) * W], in_=evac[:]), "so", 16)

            mix_pass("wbk", "mixk", "s_ek", AF.Exp)
            mix_pass("wbv", "mixv", "s_vb", AF.Identity)
            mix_pass("wbr", "mixr", "s_sr", AF.Sigmoid)
        st.run_phase(_emit)
    full_barrier(st)

    with ExitStack() as ctx:
        ek = ctx.enter_context(nc.sbuf_tensor("s3_ek", [128, T], BF16))
        vb = ctx.enter_context(nc.sbuf_tensor("s3_vb", [128, T], BF16))
        sr = ctx.enter_context(nc.sbuf_tensor("s3_sr", [128, T], BF16))
        eku = ctx.enter_context(nc.sbuf_tensor("s3_eku", [128, T], BF16))
        ekv = ctx.enter_context(nc.sbuf_tensor("s3_ekv", [128, T], BF16))
        asc = ctx.enter_context(nc.sbuf_tensor("s3_a", [128, T + 1], F32))
        bsc = ctx.enter_context(nc.sbuf_tensor("s3_b", [128, T + 1], F32))
        wkv = ctx.enter_context(nc.sbuf_tensor("s3_w", [128, T], BF16))
        def _emit():
            for cj in range(NCH):
                @block.sync
                def _(s, cj=cj):
                    st.w(s, "so", "sv")
                    st.inc(s.dma_start(out=ek[:], in_=ap["s_ek"][cj]), "sl", 16)
                    st.inc(s.dma_start(out=vb[:], in_=ap["s_vb"][cj]), "sl", 16)
                    st.inc(s.dma_start(out=sr[:], in_=ap["s_sr"][cj]), "sl", 16)

                @block.vector
                def _(v, cj=cj):
                    st.w(v, "sl")
                    v.tensor_scalar(eku[:], ek[:], st.sb["eu"][:, cj:cj + 1], None, ALU.mult)
                    v.tensor_tensor(ekv[:], ek[:], vb[:], ALU.mult)
                    v.memset(asc[:, 0:1], 0.0)
                    v.memset(bsc[:, 0:1], 0.0)
                    v.tensor_tensor_scan(asc[:, 1:T + 1], st.sb["ew"][:, cj:cj + 1].broadcast_to([128, T]),
                                         ekv[:], 0.0, ALU.mult, ALU.add)
                    v.tensor_tensor_scan(bsc[:, 1:T + 1], st.sb["ew"][:, cj:cj + 1].broadcast_to([128, T]),
                                         ek[:], 0.0, ALU.mult, ALU.add)
                    v.tensor_tensor(ekv[:], eku[:], vb[:], ALU.mult)
                    v.tensor_tensor(asc[:, 0:T], asc[:, 0:T], ekv[:], ALU.add)
                    st.inc(v.tensor_tensor(bsc[:, 0:T], bsc[:, 0:T], eku[:], ALU.add), "sv", 1)

                @block.scalar
                def _(a, cj=cj):
                    st.w(a, "sa", "sv")
                    a.activation(bsc[:, 0:T], bsc[:, 0:T], AF.Ln)
                    st.inc(a.activation(bsc[:, 0:T], bsc[:, 0:T], AF.Exp, scale=-1.0), "sa", 1)

                @block.vector
                def _(v, cj=cj):
                    st.w(v, "sa")
                    v.tensor_tensor(asc[:, 0:T], asc[:, 0:T], bsc[:, 0:T], ALU.mult)
                    st.inc(v.tensor_tensor(wkv[:], asc[:, 0:T], sr[:], ALU.mult), "sv", 1)

                @block.gpsimd
                def _(g, cj=cj):
                    st.w(g, "sv")
                    st.inc(g.dma_start(out=ap["s_srw"][cj], in_=wkv[:]), "so", 16)
        st.run_phase(_emit)
    full_barrier(st)

    with ExitStack() as ctx:
        wo = ctx.enter_context(nc.sbuf_tensor("g3_wo", [128, NCH * D], BF16))
        sw = ctx.enter_context(nc.sbuf_tensor("g3_sw", [128, NCH * W], BF16))
        x2b = ctx.enter_context(nc.sbuf_tensor("g3_x2", [128, NCH * W], F32))
        x3b = ctx.enter_context(nc.sbuf_tensor("g3_x3", [128, NCH * W], F32))
        pa = ctx.enter_context(nc.psum_tensor("g3_pa", [128, W], F32))

        def _emit():
            @block.gpsimd
            def _(g):
                st.w(g, "so")
                st.inc(g.dma_start(out=wo[:].rearrange("p (c d) -> p c d", c=NCH), in_=ap["wbo"].rearrange("(c p) d -> p c d", p=128)), "sg", 16)

            for tb in range(NTB):
                @block.sync
                def _(s, tb=tb):
                    st.w(s, "sv", "so")
                    st.inc(s.dma_start(out=sw[:].rearrange("p (c t) -> p c t", c=NCH), in_=ap["s_srw"][:, :, tb * W:(tb + 1) * W]
                                       .rearrange("c p t -> p c t")), "sl", 16)
                    st.inc(s.dma_start(out=x2b[:].rearrange("p (c t) -> p c t", c=NCH), in_=ap["s_x2"][:, :, tb * W:(tb + 1) * W]
                                       .rearrange("c p t -> p c t")), "sl", 16)

                for c in range(NCH):
                    @block.tensor
                    def _(t, c=c):
                        st.w(t, "sl", "sv", "sg")
                        for ic in range(NCH):
                            mm = t.matmul(pa[:], wo[:, ic * D + c * 128: ic * D + c * 128 + 128],
                                          sw[:, ic * W:(ic + 1) * W], start=(ic == 0), stop=(ic == NCH - 1))
                        st.inc(mm, "sp", 1)

                    @block.vector
                    def _(v, c=c):
                        st.w(v, "sp")
                        st.inc(v.tensor_tensor(x3b[:, c * W:(c + 1) * W], pa[:],
                                               x2b[:, c * W:(c + 1) * W], ALU.add), "sv", 1)

                @block.gpsimd
                def _(g2, tb=tb):
                    st.w(g2, "sv")
                    st.inc(g2.dma_start(out=ap["s_x3"][:, :, tb * W:(tb + 1) * W].rearrange("c p t -> p c t"), in_=x3b[:].rearrange("p (c t) -> p c t", c=NCH)), "so", 16)
        st.run_phase(_emit)
    full_barrier(st)


def phase_final(st, debug_taps):
    block = st.block

    @block.sync
    def _(s):
        st.wall(s)
        for tap in debug_taps:
            st.inc(s.dma_start(out=st.ap["dbg_" + tap], in_=st.ap[tap]), "sl", 16)
        s.wait_ge(st.s["sl"], st.c["sl"])

    @block.gpsimd
    def _(g):
        st.wall(g)



TAP_SPECS = {
    "s_x1": ([NCH, 128, T], F32), "s_x2": ([NCH, 128, T], F32), "s_x3": ([NCH, 128, T], F32),
    "s_qT": ([NH, 128, T], BF16), "s_kT": ([NKV, 128, T], BF16), "s_v": ([NKV, T, 128], BF16),
    "s_oT": ([NH, 128, T], BF16), "s_xh3": ([NCH, 128, T], BF16),
    "s_ek": ([NCH, 128, T], BF16), "s_vb": ([NCH, 128, T], BF16),
    "s_sr": ([NCH, 128, T], BF16), "s_srw": ([NCH, 128, T], BF16),
    "s_w1c": ([E, NCH, 128, D], BF16), "s_w2c": ([E, NCH, 128, D], BF16),
    "s_dbgR": ([32, 12 * W], BF16), "s_dbgA": ([128, 8 * W], F32),
}


def build_nc(debug_taps=(), phases=("pre", "1a", "1b", "1c", "2", "3", "4")):
    nc = bass.Bass("TRN2", target_bir_lowering=False, debug=False)

    with (
        nc.semaphore("sl") as sl, nc.semaphore("sg") as sg, nc.semaphore("sp") as sp,
        nc.semaphore("sa") as sa, nc.semaphore("sv") as sv, nc.semaphore("so") as so,
        nc.semaphore("sq") as sq, nc.Block() as block,
    ):
        st = St(nc, block, dict(sl=sl, sg=sg, sp=sp, sa=sa, sv=sv, so=so, sq=sq))

        def inp(name, shape, dt=F32):
            st.ap[name] = nc.dram_tensor(name, shape, dt, kind="ExternalInput").ap()

        def scr(name, shape, dt):
            st.ap[name] = nc.dram_tensor(name, shape, dt, kind="Internal").ap()

        inp("x0", [NCH, 128, T])
        inp("wq", [D, D]); inp("wqp", [D, D])
        inp("wk", [D, 512]); inp("wkp", [D, 512]); inp("wv", [D, 512])
        inp("woa", [D, D])
        inp("wbk", [D, D]); inp("wbv", [D, D]); inp("wbr", [D, D]); inp("wbo", [D, D])
        inp("mixk_c", [128, NCH]); inp("mixv_c", [128, NCH]); inp("mixr_c", [128, NCH])
        inp("sd_c", [128, NCH]); inp("sf_c", [128, NCH])
        inp("w1", [E, D, D]); inp("w2", [E, D, D])
        inp("b1r", [E, 1, D]); inp("b2r", [E, 1, D])
        inp("b1_cc", [E, 128, NCH]); inp("b2_cc", [E, 128, NCH])
        inp("be1_cc", [128, NCH * E])
        inp("g1_r", [1, E * D]); inp("g2_r", [1, E * D]); inp("be2_r", [1, E * D])
        inp("gw", [D, E]); inp("gb_col", [E, 1]); inp("eps11", [1, 1])
        inp("cosT", [128, T], BF16); inp("sinTs", [128, T], BF16)
        inp("masks", [4, 128, W], BF16)
        inp("ones_col", [128, 1], BF16); inp("ones_row", [1, 128], BF16)

        st.ap["y"] = nc.dram_tensor("y", [NCH, 128, T], F32, kind="ExternalOutput").ap()

        scr("s_qT", [NH, 128, T], BF16); scr("s_kT", [NKV, 128, T], BF16)
        scr("s_v", [NKV, T, 128], BF16); scr("s_oT", [NH, 128, T], BF16)
        for h in range(NH):
            scr(f"s_rcp{h}", [1, T], F32)
        scr("s_x1", [NCH, 128, T], F32); scr("s_x2", [NCH, 128, T], F32)
        scr("s_x3", [NCH, 128, T], F32); scr("s_xh3", [NCH, 128, T], BF16)
        scr("s_w1c", [E, NCH, 128, D], BF16); scr("s_w2c", [E, NCH, 128, D], BF16)
        scr("s_dbgR", [32, 12 * W], BF16); scr("s_dbgA", [128, 8 * W], F32)
        scr("s_ek", [NCH, 128, T], BF16); scr("s_vb", [NCH, 128, T], BF16)
        scr("s_sr", [NCH, 128, T], BF16); scr("s_srw", [NCH, 128, T], BF16)

        for tap in debug_taps:
            shp, dt = TAP_SPECS[tap]
            st.ap["dbg_" + tap] = nc.dram_tensor("dbg_" + tap, shp, dt, kind="ExternalOutput").ap()

        if "pre" in phases:
            phase_preamble(st)
        if "1a" in phases:
            phase_1a(st)
        if "1b" in phases:
            phase_1b(st)
        if "1c" in phases:
            phase_1c(st)
        if "2" in phases:
            phase_ffn(st, "s_x1", "s_x2")
        if "3" in phases:
            phase_3(st)
        if "4" in phases:
            phase_ffn(st, "s_x3", "y")
        phase_final(st, debug_taps)
    return nc


def stage_inputs(inputs):
    x = np.asarray(inputs["L0_attn_a_hidden_states"], np.float32)
    wq = np.asarray(inputs["L0_attn_a_Wq"], np.float32)
    wk = np.asarray(inputs["L0_attn_a_Wk"], np.float32)
    wv = np.asarray(inputs["L0_attn_a_Wv"], np.float32)
    woa = np.asarray(inputs["L0_attn_a_Wo"], np.float32)
    bk = np.asarray(inputs["L0_attn_b_Wk"], np.float32)
    bv = np.asarray(inputs["L0_attn_b_Wv"], np.float32)
    br = np.asarray(inputs["L0_attn_b_Wr"], np.float32)
    bo = np.asarray(inputs["L0_attn_b_Wo"], np.float32)
    sd = np.asarray(inputs["L0_attn_b_spatial_decay"], np.float32)
    sf = np.asarray(inputs["L0_attn_b_spatial_first"], np.float32)
    mk = np.asarray(inputs["L0_attn_b_mix_k"], np.float32)
    mv = np.asarray(inputs["L0_attn_b_mix_v"], np.float32)
    mr = np.asarray(inputs["L0_attn_b_mix_r"], np.float32)
    gw = np.asarray(inputs["L0_ffn_gate_W"], np.float32)
    gb = np.asarray(inputs["L0_ffn_gate_b"], np.float32)
    w1 = np.asarray(inputs["L0_ffn_W1"], np.float32)
    b1 = np.asarray(inputs["L0_ffn_b1"], np.float32)
    g1 = np.asarray(inputs["L0_ffn_g1"], np.float32)
    be1 = np.asarray(inputs["L0_ffn_be1"], np.float32)
    w2 = np.asarray(inputs["L0_ffn_W2"], np.float32)
    b2 = np.asarray(inputs["L0_ffn_b2"], np.float32)
    g2 = np.asarray(inputs["L0_ffn_g2"], np.float32)
    be2 = np.asarray(inputs["L0_ffn_be2"], np.float32)

    cosT, sinTs = _rope_tables()
    common = {
        "wq": np.ascontiguousarray(wq), "wqp": _perm_cols(wq),
        "wk": np.ascontiguousarray(wk), "wkp": _perm_cols(wk),
        "wv": np.ascontiguousarray(wv), "woa": np.ascontiguousarray(woa),
        "wbk": np.ascontiguousarray(bk.T), "wbv": np.ascontiguousarray(bv.T),
        "wbr": np.ascontiguousarray(br.T), "wbo": np.ascontiguousarray(bo.T),
        "mixk_c": _cols(mk), "mixv_c": _cols(mv), "mixr_c": _cols(mr),
        "sd_c": _cols(sd), "sf_c": _cols(sf),
        "w1": np.ascontiguousarray(w1), "w2": np.ascontiguousarray(w2),
        "b1r": np.ascontiguousarray(b1.reshape(E, 1, D)),
        "b2r": np.ascontiguousarray(b2.reshape(E, 1, D)),
        "b1_cc": np.stack([_cols(b1[e]) for e in range(E)]),
        "b2_cc": np.stack([_cols(b2[e]) for e in range(E)]),
        "be1_cc": np.concatenate([_cols(be1[e]) for e in range(E)], axis=1),
        "g1_r": np.ascontiguousarray(g1.reshape(1, E * D)),
        "g2_r": np.ascontiguousarray(g2.reshape(1, E * D)),
        "be2_r": np.ascontiguousarray(be2.reshape(1, E * D)),
        "gw": np.ascontiguousarray(gw), "gb_col": np.ascontiguousarray(gb.reshape(E, 1)),
        "eps11": np.full((1, 1), LN_EPS, np.float32),
        "cosT": cosT, "sinTs": sinTs, "masks": _masks(),
        "ones_col": np.ones((128, 1), BF), "ones_row": np.ones((1, 128), BF),
    }
    in_maps = []
    for c in range(8):
        m = dict(common)
        m["x0"] = np.ascontiguousarray(x[c // 2].T.reshape(NCH, 128, T))
        in_maps.append(m)
    return in_maps


def kernel(**inputs):
    in_maps = stage_inputs(inputs)
    nc = build_nc()
    res = run_bass_kernel_spmd(nc, in_maps, core_ids=list(range(8)))
    out = np.empty((B, T, D), np.float32)
    for b in range(B):
        out[b] = res.results[2 * b]["y"].reshape(D, T).T
    return out
